# version 01
# baseline (speedup 1.0000x reference)
import numpy as np
import ml_dtypes
import concourse.bass as bass
import concourse.mybir as mybir
from concourse.bass_utils import run_bass_kernel_spmd
from concourse.bass_core_rust import IndirectOffsetOnAxis

F32 = mybir.dt.float32
BF16 = mybir.dt.bfloat16
I32 = mybir.dt.int32
U32 = mybir.dt.uint32
AF = mybir.ActivationFunctionType
ALU = mybir.AluOpType
AX = mybir.AxisListType

B, T, C = 4, 4096, 1024
H, M, CHUNK, D = 16, 64, 128, 64
NCH = T // CHUNK
KSEL = 511
FF = 4096
DELTA = 1e-6
PASS_T = 512
NPASS = T // PASS_T
PCH = PASS_T // CHUNK
NITER = 6
TWO_PI = float(2.0 * np.pi)


class Prog:

    CLASSES = ("t", "v", "s", "g", "gd", "yd", "sd")
    ENG = {"t": "tensor", "v": "vector", "s": "scalar", "g": "gpsimd",
           "gd": "gpsimd", "yd": "sync", "sd": "scalar"}
    DMA = ("gd", "yd", "sd")
    R = 16

    def __init__(self):
        self.items = []
        self.counts = {c: 0 for c in self.CLASSES}
        self.trunc = None
        self.dummyS = None
        self.dummyD = None

    def _add(self, cls, fn, deps):
        deps = tuple(d for d in deps if d is not None)
        self.counts[cls] += 1
        h = (cls, self.counts[cls])
        self.items.append((cls, fn, deps, h, len(self.items)))
        return h

    def t(self, fn, deps=()): return self._add("t", fn, deps)
    def v(self, fn, deps=()): return self._add("v", fn, deps)
    def s(self, fn, deps=()): return self._add("s", fn, deps)
    def g(self, fn, deps=()): return self._add("g", fn, deps)
    def gd(self, fn, deps=()): return self._add("gd", fn, deps)
    def yd(self, fn, deps=()): return self._add("yd", fn, deps)
    def sd(self, fn, deps=()): return self._add("sd", fn, deps)

    def _semkey(self, h):
        cls, idx = h
        if cls in self.DMA:
            return (cls, (idx - 1) % self.R), 16 * ((idx - 1) // self.R + 1)
        return (cls, 0), idx

    def emit(self, nc, block, stack):
        sems = {}
        for c in self.CLASSES:
            if c in self.DMA:
                for r in range(self.R):
                    sems[(c, r)] = stack.enter_context(nc.semaphore(f"pr_{c}{r}"))
            else:
                sems[(c, 0)] = stack.enter_context(nc.semaphore(f"pr_{c}"))
        per_eng = {}
        for it in self.items:
            per_eng.setdefault(self.ENG[it[0]], []).append(it)

        def run(eng_name, eng):
            waited = {}
            for cls, fn, deps, h, seq in per_eng.get(eng_name, []):
                if cls in self.DMA:
                    key, val = self._semkey(h)
                    prev = val - 16
                    if prev > 0 and waited.get(key, -1) < prev:
                        eng.wait_ge(sems[key], prev)
                        waited[key] = prev
                for d in deps:
                    key, val = self._semkey(d)
                    if waited.get(key, -1) >= val:
                        continue
                    eng.wait_ge(sems[key], val)
                    waited[key] = val
                key, val = self._semkey(h)
                if self.trunc is not None and seq >= self.trunc:
                    if cls in self.DMA:
                        fn = lambda e: e.dma_start(out=self.dummyS[0:1, 0:4],
                                                   in_=self.dummyD[0:1, 0:4])
                    else:
                        fn = lambda e: e.nop()
                fn(eng).then_inc(sems[key], 16 if cls in self.DMA else 1)

        @block.gpsimd
        def _(g): run("gpsimd", g)

        @block.sync
        def _(y): run("sync", y)

        @block.vector
        def _(v): run("vector", v)

        @block.tensor
        def _(t): run("tensor", t)

        @block.scalar
        def _(s): run("scalar", s)

        @block.scalar
        def _(s):
            for c in self.CLASSES:
                n = self.counts[c]
                if not n:
                    continue
                if c in self.DMA:
                    for r in range(self.R):
                        cnt = (n - 1 - r) // self.R + 1 if r < n else 0
                        if r < min(n, self.R) and cnt > 0:
                            s.wait_ge(sems[(c, r)], 16 * cnt)
                else:
                    s.wait_ge(sems[(c, 0)], n)


def build_nc(debug_stage=0, dbg=False, trunc=None):
    nc = bass.Bass("TRN2", target_bir_lowering=False, debug=dbg)
    from contextlib import ExitStack
    ctx = ExitStack()

    din = lambda n, shape, dt=F32: nc.dram_tensor(n, shape, dt, kind="ExternalInput").ap()
    dscr = lambda n, shape, dt: nc.dram_tensor(n, shape, dt).ap()
    sb = lambda n, shape, dt=F32: ctx.enter_context(nc.sbuf_tensor(n, shape, dt))
    psum = lambda n, shape: ctx.enter_context(nc.psum_tensor(n, shape, F32))

    xD = din("x", [T, C])
    wqkvD = din("wqkv", [3 * C, C])
    bqkvD = din("bqkv", [3 * C, 1])
    projD = din("proj", [H, D, M])
    woutD = din("wout", [C, C])
    maskD = din("maskb", [T, T])
    posD = din("posb", [128, 32], I32)
    rbD = din("rb", [1, 1])
    fwqD = din("fwq", [C, C])
    fwkD = din("fwk", [C, C])
    fwvD = din("fwv", [C, C])
    fwoD = din("fwo", [C, C])
    w1D = din("w1", [C, FF])
    w2D = din("w2", [FF, C])
    bout_repD = din("bout_rep", [128, C])
    bv_repD = din("bv_rep", [128, C])
    ln1_repD = din("ln1_rep", [128, C])
    ln2_repD = din("ln2_rep", [128, C])
    rw_repD = din("rw_rep", [128, C])
    icolD = din("icol", [128, 1])
    iota512D = din("iota512", [128, 512])
    thiD = din("thi", [128, 32])
    tloD = din("tlo", [128, 32])
    tcolD = din("tcol", [128, 32])
    triD = din("tri", [128, 128], BF16)
    idf128D = din("idf128", [128, 128])
    onesrowD = din("onesrow", [1, 128])
    awmaskD = din("awmask", [128, 512], BF16)
    invfD = din("invf", [1, C])
    onesmD = din("onesm", [128, 32])

    outD = nc.dram_tensor("out", [T + 1, C], F32, kind="ExternalOutput").ap()

    xbfD = dscr("xbf", [T, C], BF16)
    wvbfD = dscr("wvbf", [C, C], BF16)
    woutbfD = dscr("woutbf", [C, C], BF16)
    kpxD = dscr("kpx_d", [2 * C, T], BF16)
    hD = dscr("h_d", [T, C], BF16)
    x1D = dscr("x1_d", [T, C], BF16)
    wwD = dscr("ww_d", [1, 4096], F32)
    sel8D = dscr("sel8_d", [8, 512], F32)
    rowsD = dscr("rows_d", [4, 512], F32)
    m1D = dscr("m1_d", [512, T], BF16)
    hnD = dscr("hn_d", [512, C], BF16)
    qtokD = dscr("qtok_d", [512, C], BF16)
    ktokD = dscr("ktok_d", [512, C], BF16)
    hn2D = dscr("hn2_d", [512, C], BF16)
    aDr = dscr("a_d", [H * 512, 512], BF16)

    P = Prog()
    P.trunc = trunc
    P.dummyD = dscr("dummy_d", [1, 4], F32)

    BIGW = sb("BIGW", [128, 32768], BF16)
    P.dummyS = sb("dummy_s", [1, 4], F32)
    Wqp = BIGW[:, 0:8192]
    Wkp = BIGW[:, 8192:16384]
    WvT = BIGW[:, 16384:24576]
    WoutT = BIGW[:, 24576:32768]
    qpT = sb("qpT", [128, 8 * PASS_T], BF16)
    kpT = sb("kpT", [128, 8 * PASS_T], BF16)
    vst = sb("vst", [128, PCH * 1040], BF16)
    xT = sb("xT", [128, 8 * 512], BF16)
    kptok = sb("kptok", [128, 2048], BF16)
    state_z = sb("state_z", [128, 16 * 65], BF16)
    awb = sb("awb", [128, 2048], BF16)
    hh = sb("hh", [128, 1024], BF16)
    bqp = sb("bqp", [128, 8])
    bkp = sb("bkp", [128, 8])
    kspart = sb("kspart", [128, 8 * PCH])
    ksrun = sb("ksrun", [128, 8])
    recn = sb("recn", [128, 16])
    rr = sb("rr", [128, 1024])
    stat = sb("stat", [128, 8])
    x1b = sb("x1b", [128, 1024], BF16)
    wacc = sb("wacc", [128, 32])
    trashb = sb("trashb", [128, 1024], BF16)
    trashv = sb("trashv", [128, 1024], BF16)
    zbf = sb("zbf", [128, 1024], BF16)
    bout_rep = sb("bout_rep_s", [128, C])
    bv_rep = sb("bv_rep_s", [128, C])
    ln1_rep = sb("ln1_rep_s", [128, C])
    ln2_rep = sb("ln2_rep_s", [128, C])
    rw_rep = sb("rw_rep_s", [128, C])
    icol = sb("icol_s", [128, 1])
    iota512 = sb("iota512_s", [128, 512])
    thi = sb("thi_s", [128, 32])
    tlo = sb("tlo_s", [128, 32])
    tcol = sb("tcol_s", [128, 32])
    tri = sb("tri_s", [128, 128], BF16)
    idf128 = sb("idf128_s", [128, 128])
    onesrow = sb("onesrow_s", [1, 128])
    awmask = sb("awmask_s", [128, 512], BF16)
    invf = sb("invf_s", [1, C])
    neglarge = sb("neglarge", [1, 128])
    poslarge = sb("poslarge", [1, 128])
    czero = sb("czero", [128, 1])
    bstg = sb("bstg", [128, 1], BF16)
    brow = sb("brow", [1, 128])
    lodc = sb("lodc", [128, 2])
    pos_s = sb("pos_s", [128, 32], I32)
    wrepx = None
    cmpt = sb("cmpt", [128, 4096], BF16)
    cnt2 = sb("cnt2", [128, 2])
    tpc = sb("tpc", [1, 128])
    tpp = sb("tpp", [1, 128])
    st8 = sb("st8", [1, 8])
    probes = sb("probes", [128, 1])
    r1 = sb("r1", [1, 128])
    r1m = sb("r1m", [1, 128], U32)
    m01 = sb("m01", [128, 32])
    scan_t = sb("scan_t", [128, 32])
    offs = sb("offs", [128, 1])
    ordp = sb("ordp", [128, 32])
    meta = sb("meta", [128, 192], BF16)
    po = sb("po", [128, 512], BF16)
    sel8 = sb("sel8_s", [8, 512])
    selrow = sb("selrow", [1, 512])
    selrow2 = sb("selrow2", [1, 512])
    selrow3 = sb("selrow3", [1, 512])
    posrow = sb("posrow", [1, 512])
    wrowt = sb("wrowt", [1, 512])
    scatrow = sb("scatrow", [1, 512])
    selcol = sb("selcol", [128, 4])
    selcoli = sb("selcoli", [128, 4], I32)
    scatcoli = sb("scatcoli", [128, 4], I32)
    wselc = sb("wselc", [128, 4])
    selrep = sb("selrep", [128, 512])
    selrep_bfv = selrep[:, :].bitcast(BF16)
    posi2 = sb("posi2", [128, 32], I32)
    posf = sb("posf", [128, 32])
    tmp32 = sb("tmp32", [128, 32])
    onesm = sb("onesm_s", [128, 32])
    SCRATCH4 = sb("SCRATCH4", [128, 4096])
    qTb = SCRATCH4[:, 0:2048].bitcast(BF16)
    vselb = SCRATCH4[:, 2048:4096].bitcast(BF16)
    wrepx = SCRATCH4
    m1f = SCRATCH4
    x2a = SCRATCH4
    scs = sb("scs", [128, 512])
    scs_bfv = scs[:, :].bitcast(BF16)
    
    arow = sb("arow", [128, 512], BF16)
    aT = sb("aT", [128, 1024], BF16)
    kTb = cmpt
    rcop = hh
    hnt = hh
    qtok = x1b
    cosb = sb("cosb", [128, 1024], BF16)
    sinb = sb("sinb", [128, 1024], BF16)
    rr2 = bout_rep
    r2f = bv_rep
    msel = vst[:, 0:2048].bitcast(BF16) if False else None
    xsel = None

    p_qkv = [psum(f"p_qkv{i}", [128, 512]) for i in range(2)]
    p_awA = psum("p_awA", [128, 512])
    p_awB = psum("p_awB", [128, 512])
    p_ctxA = psum("p_ctxA", [128, 512])
    p_ctxB = psum("p_ctxB", [128, 512])
    p_kv = [psum(f"p_kv{i}", [128, 512]) for i in range(2)]

    def dma_g(dst, src, deps=(), allow_slow_non_contiguous=False):
        if allow_slow_non_contiguous:
            return P.gd(lambda g: g.dma_start(out=dst, in_=src,
                                              allow_slow_non_contiguous=True), deps)
        return P.gd(lambda g: g.dma_start(out=dst, in_=src), deps)

    def dma_y(dst, src, deps=(), tr=False):
        if tr:
            return P.yd(lambda y: y.dma_start(out=dst, in_=src, transpose=True), deps)
        return P.yd(lambda y: y.dma_start(out=dst, in_=src), deps)

    ld = {}
    for name, dst, src in [
        ("icol", icol, icolD), ("iota512", iota512, iota512D), ("thi", thi, thiD),
        ("tlo", tlo, tloD), ("tcol", tcol, tcolD), ("tri", tri, triD),
        ("idf128", idf128, idf128D), ("onesrow", onesrow, onesrowD),
        ("awmask", awmask, awmaskD), ("invf", invf, invfD),
        ("bout", bout_rep, bout_repD), ("bv", bv_rep, bv_repD),
        ("ln1", ln1_rep, ln1_repD), ("ln2", ln2_rep, ln2_repD),
        ("rw", rw_rep, rw_repD), ("pos", pos_s, posD), ("onesm", onesm, onesmD),
    ]:
        ld[name] = dma_g(dst[:, :], src[:, :])
    h_cz = P.v(lambda v: v.memset(czero[:, :], 0.0))
    h_zb = P.v(lambda v: v.memset(zbf[:, :], 0.0))
    h_sz = P.v(lambda v: v.memset(state_z[:, :], 0.0))
    h_ks0 = P.v(lambda v: v.memset(ksrun[:, :], 0.0))
    P.v(lambda v: v.memset(neglarge[0:1, :], -1e30))
    P.v(lambda v: v.memset(poslarge[0:1, :], 1e30))

    h_xbf = dma_g(xbfD[:, :], xD[:, :])
    h_wvb = dma_g(wvbfD[:, :], wqkvD[2 * C:3 * C, :])
    h_wob = dma_g(woutbfD[:, :], woutD[:, :])

    h_kpz = []
    for mt in range(16):
        for j in range(4):
            h_kpz.append(dma_g(
                kpxD[mt * 128 + 64:mt * 128 + 128, j * 1024:(j + 1) * 1024],
                zbf[0:64, :], deps=[h_zb]))

    stg = BIGW[:, 16384:32768]
    h_wqk = [dma_g(stg[:, r * 1024:(r + 1) * 1024],
                   wqkvD[r * 128:(r + 1) * 128, :]) for r in range(16)]
    projb = rr
    projb = awb
    h_projb = dma_g(projb[0:64, 0:1024].rearrange("d (h m) -> d h m", h=16),
                    projD.rearrange("h d m -> d h m"))
    bd = kptok
    h_bd0 = P.v(lambda v: v.memset(bd[:, 0:1024], 0.0))
    h_bdc = []
    for p in range(8):
        h_bdc.append(P.v(lambda v, p=p: v.tensor_copy(
            bd[0:64, p * 128:p * 128 + 64], projb[0:64, (2 * p) * 64:(2 * p + 1) * 64]),
            [h_bd0, h_projb]))
        h_bdc.append(P.v(lambda v, p=p: v.tensor_copy(
            bd[64:128, p * 128 + 64:p * 128 + 128],
            projb[0:64, (2 * p + 1) * 64:(2 * p + 2) * 64]), [h_bd0, h_projb]))
    h_fold = []
    prev_cp = None
    for wi in range(2):
        Wdst = Wqp if wi == 0 else Wkp
        for pr in range(8):
            for kc in range(8):
                hm = P.t(lambda t, wi=wi, pr=pr, kc=kc: t.matmul(
                    p_qkv[0][:, 0:128],
                    stg[:, (wi * 8 + pr) * 1024 + kc * 128:(wi * 8 + pr) * 1024 + (kc + 1) * 128],
                    bd[:, pr * 128:(pr + 1) * 128], start=True, stop=True),
                    [h_wqk[wi * 8 + pr], prev_cp] + (h_bdc if (wi == 0 and pr == 0 and kc == 0) else []))
                prev_cp = P.v(lambda v, Wdst=Wdst, pr=pr, kc=kc: v.tensor_copy(
                    Wdst[:, kc * 1024 + pr * 128:kc * 1024 + (pr + 1) * 128],
                    p_qkv[0][:, 0:128]), [hm])
                h_fold.append(prev_cp)
    h_bias = []
    prev_hm2 = None
    for bi in range(2):
        bdst = bqp if bi == 0 else bkp
        row0 = bi * 1024
        for pr in range(8):
            hb1 = dma_g(bstg[:, 0:1], bqkvD[row0 + pr * 128:row0 + (pr + 1) * 128, 0:1],
                        deps=([h_bias[-1], prev_hm2] if h_bias else []))
            hm = P.t(lambda t, pr=pr: t.matmul(p_qkv[1][0:1, 0:128], bstg[:, 0:1],
                                               bd[:, pr * 128:(pr + 1) * 128],
                                               start=True, stop=True),
                     [hb1, prev_hm2] + h_bdc)
            hv = P.v(lambda v: v.tensor_copy(brow[0:1, :], p_qkv[1][0:1, 0:128]),
                     [hm] + ([h_bias[-1]] if h_bias else []))
            hm2 = P.t(lambda t: t.matmul(p_qkv[1][:, 128:129], brow[0:1, :],
                                         idf128[0:1, 0:1], start=True, stop=True),
                      [hv, ld["idf128"]])
            prev_hm2 = hm2
            h_bias.append(P.v(lambda v, bdst=bdst, pr=pr: v.tensor_copy(
                bdst[:, pr:pr + 1], p_qkv[1][:, 128:129]), [hm2]))
    fold_done = h_fold + h_bias
    last_bias_mm = prev_hm2

    h_WvT, h_WoutT = [], []
    for kc in range(8):
        h_WvT.append(dma_y(WvT[:, kc * 1024:(kc + 1) * 1024],
                           wvbfD[:, kc * 128:(kc + 1) * 128],
                           deps=[h_wvb] + fold_done, tr=True))
        h_WoutT.append(dma_y(WoutT[:, kc * 1024:(kc + 1) * 1024],
                             woutbfD[:, kc * 128:(kc + 1) * 128],
                             deps=[h_wob] + fold_done, tr=True))

    h_rb = dma_g(brow[0:1, 0:1], rbD[0:1, 0:1], deps=[last_bias_mm, h_bias[-1]])
    h_rbm = P.t(lambda t: t.matmul(p_qkv[1][:, 129:130], onesrow[0:1, :],
                                   brow[0:1, 0:1], start=True, stop=True),
                [h_rb, ld["onesrow"]] + h_bias)
    h_rbc = P.v(lambda v: v.tensor_copy(lodc[:, 1:2], p_qkv[1][:, 129:130]), [h_rbm])

    vcol = lambda cc, hd: cc * 1040 + hd * 65
    h_x1_done = []
    h_router = []
    bank_last = [h_fold[-1], h_rbc]
    aw_last = [None, None]
    awb_readers = [[h_bdc[-1], h_projb], [h_bdc[-1], h_projb]]
    prev_div = []
    prev_hst = None
    prev_sq = None
    prev_m7 = None
    prev_hr = [None, None]
    first_in_kv = [True, True]
    statez_done = [h_sz]
    prev_pass = []
    prev_ap = None

    for pz in range(NPASS):
        t0 = pz * PASS_T
        h_vones = P.v(lambda v: v.memset(
            vst[:, :].rearrange("p (cc h x) -> p cc h x", cc=PCH, h=H)[:, :, :, 64:65],
            1.0), prev_pass)
        hxt = [dma_y(xT[:, kc * 512:(kc + 1) * 512],
                     xbfD[t0:t0 + 512, kc * 128:(kc + 1) * 128],
                     deps=[h_xbf] + prev_pass, tr=True) for kc in range(8)]
        qkv_done = []
        kp_evacs = []
        kp_accs = []
        pq = 0
        for wi in range(2):
            W, bias, dst = (Wqp, bqp, qpT) if wi == 0 else (Wkp, bkp, kpT)
            for mi in range(8):
                hmm = None
                for kc in range(8):
                    hmm = P.t(lambda t, W=W, mi=mi, kc=kc, pq=pq: t.matmul(
                        p_qkv[pq % 2][:, :],
                        W[:, kc * 1024 + mi * 128:kc * 1024 + (mi + 1) * 128],
                        xT[:, kc * 512:(kc + 1) * 512],
                        start=(kc == 0), stop=(kc == 7)),
                        (hxt + fold_done + prev_pass + [bank_last[pq % 2]])
                        if kc == 0 else [])
                hev = P.v(lambda v, dst=dst, bias=bias, mi=mi, pq=pq: v.tensor_scalar(
                    dst[:, mi * PASS_T:(mi + 1) * PASS_T], p_qkv[pq % 2][:, :],
                    bias[:, mi:mi + 1], 0.0, op0=ALU.add, op1=ALU.max),
                    [hmm] + h_bias + prev_pass)
                qkv_done.append(hev)
                bank_last[pq % 2] = hev
                if wi == 1:
                    kp_evacs.append(hev)
                    for j in range(PCH):
                        hacc = P.s(lambda s, dst=dst, mi=mi, j=j: s.activation(
                            trashb[:, 0:128],
                            dst[:, mi * PASS_T + j * 128:mi * PASS_T + (j + 1) * 128],
                            AF.Copy, bias=0.0, scale=1.0,
                            accum_out=kspart[:, mi * PCH + j:mi * PCH + j + 1]),
                            [hev] + prev_pass + ([prev_sq] if prev_sq else []))
                        prev_sq = hacc
                        kp_accs.append(hacc)
                        qkv_done.append(hacc)
                pq += 1
        for tt_ in range(PCH):
            for halfn in range(2):
                hmm = None
                for kc in range(8):
                    hmm = P.t(lambda t, tt_=tt_, halfn=halfn, kc=kc, pq=pq: t.matmul(
                        p_qkv[pq % 2][:, :],
                        xT[:, kc * 512 + tt_ * 128:kc * 512 + (tt_ + 1) * 128],
                        WvT[:, kc * 1024 + halfn * 512:kc * 1024 + (halfn + 1) * 512],
                        start=(kc == 0), stop=(kc == 7)),
                        (hxt + h_WvT + [bank_last[pq % 2]]) if kc == 0 else [])
                hev = P.v(lambda v, tt_=tt_, halfn=halfn, pq=pq: v.scalar_tensor_tensor(
                    vst[:, :].rearrange("p (cc h x) -> p cc h x", cc=PCH, h=H)[
                        :, tt_, halfn * 8:(halfn + 1) * 8, 0:64],
                    p_qkv[pq % 2][:, :].rearrange("p (h x) -> p h x", h=8),
                    1.0,
                    bv_rep[:, halfn * 512:(halfn + 1) * 512].rearrange("p (h x) -> p h x", h=8),
                    op0=ALU.mult, op1=ALU.add), [hmm, h_vones, ld["bv"]])
                qkv_done.append(hev)
                bank_last[pq % 2] = hev
                pq += 1
        h_kpst = []
        for mt in range(8):
            dst = kpxD[mt * 256:(mt + 1) * 256, t0:t0 + PASS_T].rearrange(
                "(two sub) t -> two sub t", sub=128)[:, 0:64, :]
            src = kpT[:, mt * PASS_T:(mt + 1) * PASS_T].rearrange(
                "p (two t) -> p two t", two=1)[:, 0, :].rearrange(
                "p t -> p t")
            h_kpst.append(dma_g(dst, kpT[:, mt * PASS_T:(mt + 1) * PASS_T].rearrange(
                "p t -> p t"), deps=kp_evacs))

        pass_consumers = []
        for cc in range(PCH):
            ch = pz * PCH + cc
            ct0 = cc * 128
            hkptok = dma_y(kptok[:, :], kpxD[:, t0 + ct0:t0 + ct0 + 128],
                           deps=h_kpst + h_kpz + [statez_done[-1]] +
                           ([h_rbc, h_rbm] if ch == 0 else []), tr=True)
            h_ksu = P.v(lambda v, cc=cc: v.tensor_tensor(
                ksrun[:, :], ksrun[:, :], kspart[:, cc::PCH], op=ALU.add),
                kp_accs + [statez_done[-1]] + ([h_ks0] if ch == 0 else []))
            h_div = []
            statez_prev = list(statez_done)
            h_stm = []
            for rnd in range(2):
                h0 = rnd * 8
                h_awm = []
                for k in range(8):
                    hd = h0 + k
                    mt, ro = hd // 2, (hd % 2) * 64
                    bankp = p_awA if (k % 2 == 0) else p_awB
                    slot = (k // 2) * 128
                    h_awm.append(P.t(lambda t, mt=mt, ro=ro, bankp=bankp, slot=slot,
                                     ct0=ct0: t.matmul(
                        bankp[:, slot:slot + 128],
                        kpT[ro:ro + 64, mt * PASS_T + ct0:mt * PASS_T + ct0 + 128],
                        qpT[ro:ro + 64, mt * PASS_T + ct0:mt * PASS_T + ct0 + 128],
                        start=True, stop=True),
                        (qkv_done if (k == 0 and rnd == 0) else []) +
                        ([aw_last[k % 2]] if k < 2 else [])))
                h_awe = []
                for bi, bankp in enumerate((p_awA, p_awB)):
                    h_awe.append(P.v(lambda v, bankp=bankp, h0=h0, bi=bi: v.tensor_tensor(
                        awb[:, :].rearrange("p (b x) -> p b x", x=128)[
                            :, h0 + bi:h0 + 8:2, :],
                        bankp[:, :].rearrange("p (b x) -> p b x", x=128),
                        awmask[:, :].rearrange("p (b x) -> p b x", x=128),
                        op=ALU.mult), h_awm + [ld["awmask"]] + awb_readers[rnd]))
                for bi in range(2):
                    aw_last[bi] = h_awe[bi]
                h_ctxm = []
                for k in range(8):
                    hd = h0 + k
                    pcx = p_ctxA if k < 4 else p_ctxB
                    slot = (k % 4) * 65
                    mt = hd // 2
                    h_in = P.t(lambda t, hd=hd, cc=cc, pcx=pcx, slot=slot: t.matmul(
                        pcx[:, slot:slot + 65],
                        awb[:, hd * 128:(hd + 1) * 128],
                        vst[:, vcol(cc, hd):vcol(cc, hd) + 65],
                        start=True, stop=False),
                        h_awe + (h_div[-2:] if h_div else prev_div))
                    h_hi = P.t(lambda t, mt=mt, ct0=ct0, hd=hd, pcx=pcx, slot=slot:
                               t.matmul(
                        pcx[:, slot:slot + 65],
                        qpT[:, mt * PASS_T + ct0:mt * PASS_T + ct0 + 128],
                        state_z[:, hd * 65:(hd + 1) * 65],
                        start=False, stop=True), [h_in] + statez_prev)
                    h_ctxm.append(h_hi)
                awb_readers[rnd] = [h_ctxm[-1]]
                h_rec1 = P.v(lambda v, rnd=rnd: v.tensor_scalar(
                    recn[:, rnd * 8:rnd * 8 + 4], p_ctxA[:, 64:260:65],
                    float(DELTA), None, op0=ALU.add), h_ctxm + prev_div)
                h_rec2 = P.v(lambda v, rnd=rnd: v.tensor_scalar(
                    recn[:, rnd * 8 + 4:rnd * 8 + 8], p_ctxB[:, 64:260:65],
                    float(DELTA), None, op0=ALU.add), h_ctxm + prev_div)
                h_rec3 = P.v(lambda v, rnd=rnd: v.reciprocal(
                    recn[:, rnd * 8:rnd * 8 + 8], recn[:, rnd * 8:rnd * 8 + 8]),
                    [h_rec1, h_rec2])
                for k in range(8):
                    hd = h0 + k
                    pcx = p_ctxA if k < 4 else p_ctxB
                    slot = (k % 4) * 65
                    hdv = P.v(lambda v, hd=hd, pcx=pcx, slot=slot: v.tensor_scalar(
                        hh[:, hd * 64:(hd + 1) * 64], pcx[:, slot:slot + 64],
                        recn[:, hd:hd + 1], None, op0=ALU.mult),
                        [h_rec3, prev_hst])
                    h_div.append(hdv)
                for k in range(8):
                    hd = h0 + k
                    bank = hd // 8
                    hst = P.t(lambda t, hd=hd, cc=cc, bank=bank,
                              stf=first_in_kv[bank]: t.matmul(
                        p_kv[bank][:, (hd % 8) * 64:(hd % 8 + 1) * 64],
                        kptok[:, hd * 128 - 64 * (hd % 2):hd * 128 - 64 * (hd % 2) + 128],
                        vst[:, vcol(cc, hd):vcol(cc, hd) + 64],
                        start=stf, stop=(ch == NCH - 1 and hd % 8 == 7),
                        skip_group_check=True),
                        [hkptok] + (h_ctxm if k == 0 and rnd == 1 else []) +
                        (statez_prev if ch else []))
                    first_in_kv[bank] = False
                    h_stm.append(hst)
            h_szc = []
            for bank in range(2):
                h_szc.append(P.v(lambda v, bank=bank: v.tensor_copy(
                    state_z[:, :].rearrange("p (hd x) -> p hd x", x=65)[
                        :, bank * 8:(bank + 1) * 8, 0:64],
                    p_kv[bank][:, :].rearrange("p (hd x) -> p hd x", x=64)),
                    h_stm + statez_prev))
            h_szc.append(P.v(lambda v: v.tensor_copy(
                state_z[0:64, :].rearrange("p (hd x) -> p hd x", x=65)[:, 0::2, 64:65],
                ksrun[0:64, :].unsqueeze(2)), h_stm + [h_ksu]))
            h_szc.append(P.v(lambda v: v.tensor_copy(
                state_z[64:128, :].rearrange("p (hd x) -> p hd x", x=65)[:, 1::2, 64:65],
                ksrun[64:128, :].unsqueeze(2)), h_stm + [h_ksu]))
            statez_done = h_szc
            h_hst = dma_g(hD[t0 + ct0:t0 + ct0 + 128, :], hh[:, :], deps=h_div)
            prev_div = h_div[-2:]
            prev_hst = h_hst
            pass_consumers.append(h_hst)
            pass_consumers.extend(h_stm)
            pass_consumers.append(h_ksu)

        hT = xT
        h_hT = [dma_y(hT[:, kc * 512:(kc + 1) * 512],
                      hD[t0:t0 + 512, kc * 128:(kc + 1) * 128],
                      deps=pass_consumers + qkv_done, tr=True) for kc in range(8)]
        for tt_ in range(PCH):
            ct0 = tt_ * 128
            h_x = dma_g(rr[:, :], xD[t0 + ct0:t0 + ct0 + 128, :],
                        deps=[d for d in (prev_ap, prev_sq) if d])
            h_wo = []
            for halfn in range(2):
                hmm = None
                for kc in range(8):
                    hmm = P.t(lambda t, tt_=tt_, halfn=halfn, kc=kc: t.matmul(
                        p_qkv[halfn][:, :],
                        hT[:, kc * 512 + tt_ * 128:kc * 512 + (tt_ + 1) * 128],
                        WoutT[:, kc * 1024 + halfn * 512:kc * 1024 + (halfn + 1) * 512],
                        start=(kc == 0), stop=(kc == 7)),
                        (h_hT + h_WoutT + h_stm +
                         [d for d in (prev_hr[halfn], bank_last[halfn]) if d])
                        if kc == 0 else [])
                h_wo.append(hmm)
            h_xb = P.v(lambda v: v.tensor_tensor(rr[:, :], rr[:, :], bout_rep[:, :],
                                                 op=ALU.add), [h_x, ld["bout"]])
            h_r = []
            for halfn in range(2):
                h_r.append(P.v(lambda v, halfn=halfn: v.scalar_tensor_tensor(
                    rr[:, halfn * 512:(halfn + 1) * 512], p_qkv[halfn][:, :], 1.0,
                    rr[:, halfn * 512:(halfn + 1) * 512], op0=ALU.mult, op1=ALU.add,
                    accum_out=stat[:, halfn:halfn + 1]), [h_xb, h_wo[halfn]]))
            prev_hr = list(h_r)
            for halfn in range(2):
                bank_last[halfn] = h_r[halfn]
            h_sq = P.s(lambda s: s.activation(trashb[:, :], rr[:, :], AF.Square,
                                              accum_out=stat[:, 2:3]),
                       h_r + [d for d in (prev_sq,) if d])
            prev_sq = h_sq
            h_m2 = P.v(lambda v: v.tensor_tensor(stat[:, 0:1], stat[:, 0:1],
                                                 stat[:, 1:2], op=ALU.add), h_r)
            h_m2b = P.v(lambda v: v.tensor_scalar(stat[:, 0:1], stat[:, 0:1],
                                                  1.0 / 1024.0, None, op0=ALU.mult),
                        [h_m2])
            h_m3 = P.v(lambda v: v.tensor_tensor(stat[:, 3:4], stat[:, 0:1],
                                                 stat[:, 0:1], op=ALU.mult), [h_m2b])
            h_m4 = P.v(lambda v: v.tensor_scalar(stat[:, 2:3], stat[:, 2:3],
                                                 1.0 / 1024.0, None, op0=ALU.mult),
                       [h_sq])
            h_m5 = P.v(lambda v: v.tensor_tensor(stat[:, 3:4], stat[:, 2:3],
                                                 stat[:, 3:4], op=ALU.subtract),
                       [h_m3, h_m4])
            h_m6 = P.v(lambda v: v.tensor_scalar(stat[:, 3:4], stat[:, 3:4], 1e-5,
                                                 None, op0=ALU.add), [h_m5])
            h_m7 = P.s(lambda s: s.activation(stat[:, 4:5], stat[:, 3:4], AF.Sqrt),
                       [h_m6] + [d for d in (prev_m7,) if d])
            prev_m7 = h_m7
            h_m8 = P.v(lambda v: v.reciprocal(stat[:, 4:5], stat[:, 4:5]), [h_m7])
            h_ap = P.v(lambda v: v.tensor_scalar(x1b[:, :], rr[:, :], stat[:, 0:1],
                                                 stat[:, 4:5], op0=ALU.subtract,
                                                 op1=ALU.mult),
                       [h_m8] + (h_x1_done[-1:] if h_x1_done else []) +
                       (h_router[-1:] if h_router else []))
            prev_ap = h_ap
            a_glob = pz * PCH + tt_
            h_rt = P.v(lambda v, a_glob=a_glob: v.scalar_tensor_tensor(
                trashv[:, :], x1b[:, :], 1.0, rw_rep[:, :], op0=ALU.mult,
                op1=ALU.mult, accum_out=wacc[:, a_glob:a_glob + 1]),
                [h_ap, ld["rw"]])
            h_router.append(h_rt)
            h_x1 = dma_g(x1D[t0 + ct0:t0 + ct0 + 128, :], x1b[:, :], deps=[h_ap])
            h_x1_done.append(h_x1)
            pass_consumers = pass_consumers + [h_x1, h_rt, h_ap]
        prev_pass = pass_consumers

    h_wb = P.v(lambda v: v.tensor_scalar(wacc[:, :], wacc[:, :], lodc[:, 1:2], None,
                                         op0=ALU.add), [h_rbc] + h_router)
    h_ww = dma_g(wwD[:, :], wacc[:, :], deps=[h_wb])

    if debug_stage == 1:
        hdbg = [P.gd(lambda g, a=a: g.dma_start(
            out=outD[a * 128:(a + 1) * 128, :], in_=x1D[a * 128:(a + 1) * 128, :]),
            h_x1_done) for a in range(32)]
        P.gd(lambda g: g.dma_start(out=outD[T:T + 1, 0:32], in_=wacc[0:1, :]), [h_wb])
        with nc.Block() as block, ExitStack() as sstack:
            P.emit(nc, block, sstack)
        ctx.close()
        return nc

    msel = vst[:, 0:2048]
    m1T = vst[:, 2048:3072]
    oT = xT
    hn2T = qpT
    scs_bf = None
    _h1small = None

    def h1slot(ft):
        small = [(sinb, 0), (sinb, 512), (cosb, 0), (cosb, 512),
                 (aT, 0), (aT, 512), (x1b, 0), (x1b, 512),
                 (arow, 0), (trashv, 0), (trashv, 512),
                 (scs_bfv, 0), (scs_bfv, 512),
                 (selrep_bfv, 0), (selrep_bfv, 512), (po, 0)]
        if ft < 16:
            tns, base = small[ft]
            return tns[:, base:base + 512]
        tns = (kTb, kpT)[(ft - 16) // 8]
        return tns[:, ((ft - 16) % 8) * 512:((ft - 16) % 8 + 1) * 512]

    def xsel_t(c):
        return (awb, kptok)[c // 2][:, (c % 2) * 1024:(c % 2) * 1024 + 1024]
    h_wrep = dma_g(wrepx[:, :], wwD[0:1, :].partition_broadcast(128), deps=[h_ww])
    h_l1 = P.v(lambda v: v.tensor_reduce(cnt2[:, 0:1], wacc[:, :], axis=AX.X, op=ALU.min), [h_wb])
    h_l2 = P.v(lambda v: v.tensor_reduce(cnt2[:, 1:2], wacc[:, :], axis=AX.X, op=ALU.max), [h_wb, h_l1])
    h_t1 = P.t(lambda t: t.transpose(p_qkv[0][0:1, 0:128], cnt2[:, 0:1], idf128[:, :]),
               [h_l1, bank_last[0], ld["idf128"]])
    h_t2 = P.t(lambda t: t.transpose(p_qkv[1][0:1, 0:128], cnt2[:, 1:2], idf128[:, :]),
               [h_l2, bank_last[1]])
    bank_last[0], bank_last[1] = h_t1, h_t2
    h_c1 = P.v(lambda v: v.tensor_copy(tpc[0:1, :], p_qkv[0][0:1, 0:128]), [h_t1])
    h_c2 = P.v(lambda v: v.tensor_copy(tpp[0:1, :], p_qkv[1][0:1, 0:128]), [h_t2])
    h_s1 = P.v(lambda v: v.tensor_reduce(st8[0:1, 0:1], tpc[0:1, :], axis=AX.X, op=ALU.min), [h_c1])
    h_s2 = P.v(lambda v: v.tensor_reduce(st8[0:1, 1:2], tpp[0:1, :], axis=AX.X, op=ALU.max), [h_c2])
    h_s3 = P.v(lambda v: v.tensor_scalar(st8[0:1, 0:1], st8[0:1, 0:1], 0.001, None,
                                         op0=ALU.subtract), [h_s1])
    h_s4 = P.v(lambda v: v.memset(st8[0:1, 3:4], -1e30), [h_s2])
    bl = [h_s3, h_s4]
    h_cm_prev = None
    for it in range(NITER):
        h_d1 = P.v(lambda v: v.tensor_tensor(st8[0:1, 2:3], st8[0:1, 1:2], st8[0:1, 0:1],
                                             op=ALU.subtract), bl)
        h_d2 = P.v(lambda v: v.tensor_scalar(st8[0:1, 2:3], st8[0:1, 2:3], 1.0 / 129.0,
                                             None, op0=ALU.mult), [h_d1])
        h_bc = P.t(lambda t: t.matmul(p_qkv[0][:, 0:2], onesrow[0:1, :], st8[0:1, 0:3:2],
                                      start=True, stop=True),
                   [h_d2, bank_last[0], ld["onesrow"]])
        bank_last[0] = h_bc
        h_lc = P.v(lambda v: v.tensor_copy(lodc[:, 0:2], p_qkv[0][:, 0:2]),
                   [h_bc] + ([h_cm_prev] if h_cm_prev else []))
        h_pr = P.v(lambda v: v.scalar_tensor_tensor(probes[:, 0:1], icol[:, 0:1],
                                                    lodc[:, 1:2], lodc[:, 0:1],
                                                    op0=ALU.mult, op1=ALU.add),
                   [h_lc, ld["icol"]])
        h_cm = P.v(lambda v: v.tensor_scalar(cmpt[:, :], wrepx[:, :], probes[:, 0:1],
                                             None, op0=ALU.is_gt, op1=ALU.add,
                                             accum_out=cnt2[:, 0:1]),
                   [h_pr, h_wrep, h_l2])
        h_cm_prev = h_cm
        h_t3 = P.t(lambda t: t.transpose(p_qkv[0][0:1, 128:256], cnt2[:, 0:1],
                                         idf128[:, :]), [h_cm, bank_last[0]])
        h_t4 = P.t(lambda t: t.transpose(p_qkv[1][0:1, 0:128], probes[:, 0:1],
                                         idf128[:, :]), [h_pr, bank_last[1]])
        bank_last[0], bank_last[1] = h_t3, h_t4
        h_c3 = P.v(lambda v: v.tensor_copy(tpc[0:1, :], p_qkv[0][0:1, 128:256]),
                   [h_t3, h_s1])
        h_c4 = P.v(lambda v: v.tensor_copy(tpp[0:1, :], p_qkv[1][0:1, 0:128]),
                   [h_t4, h_s2])
        r1mf = r1m[0:1, :].bitcast(F32)

        def masked_red(cmp_op, thr_v, big_sign, red_op, dst_sl, deps):
            h1 = P.v(lambda v: v.tensor_scalar(brow[0:1, :], tpc[0:1, :], thr_v,
                                               None, op0=cmp_op), deps)
            h2 = P.v(lambda v: v.tensor_tensor(r1[0:1, :], brow[0:1, :], tpp[0:1, :],
                                               op=ALU.mult), [h1])
            h3 = P.v(lambda v: v.tensor_scalar(r1mf, brow[0:1, :],
                                               big_sign * 1e30, -big_sign * 1e30,
                                               op0=ALU.mult, op1=ALU.add), [h2])
            h4 = P.v(lambda v: v.tensor_tensor(r1[0:1, :], r1[0:1, :], r1mf,
                                               op=ALU.add), [h3])
            h5 = P.v(lambda v, dst_sl=dst_sl: v.tensor_reduce(
                st8[0:1, dst_sl:dst_sl + 1], r1[0:1, :], axis=AX.X, op=red_op), [h4])
            return h5

        h_e3 = masked_red(ALU.is_equal, float(KSEL), 1.0, ALU.max, 4,
                          [h_c3, h_c4, h_rb, h_rbm])
        h_e4 = P.v(lambda v: v.tensor_tensor(st8[0:1, 3:4], st8[0:1, 3:4],
                                             st8[0:1, 4:5], op=ALU.max), [h_e3])
        h_f3 = masked_red(ALU.is_ge, float(KSEL + 1), 1.0, ALU.max, 5, [h_e3])
        h_f4 = P.v(lambda v: v.tensor_tensor(st8[0:1, 0:1], st8[0:1, 0:1],
                                             st8[0:1, 5:6], op=ALU.max), [h_f3])
        h_g3 = masked_red(ALU.is_le, float(KSEL), -1.0, ALU.min, 6, [h_f3, h_f4])
        h_g4 = P.v(lambda v: v.tensor_tensor(st8[0:1, 1:2], st8[0:1, 1:2],
                                             st8[0:1, 6:7], op=ALU.min), [h_g3])
        bl = [h_e4, h_f4, h_g4]
    h_tb = P.t(lambda t: t.matmul(p_qkv[0][:, 2:3], onesrow[0:1, :], st8[0:1, 3:4],
                                  start=True, stop=True), bl + [bank_last[0]])
    bank_last[0] = h_tb
    h_tc = P.v(lambda v: v.tensor_copy(lodc[:, 0:1], p_qkv[0][:, 2:3]), [h_tb, h_g3])
    h_mm1 = P.v(lambda v: v.tensor_scalar(m01[:, :], wacc[:, :], lodc[:, 0:1], None,
                                          op0=ALU.is_gt), [h_tc, h_wb])
    h_sc1 = P.v(lambda v: v.tensor_tensor_scan(scan_t[:, :], m01[:, :], onesm[:, :],
                                               0.0, op0=ALU.add, op1=ALU.mult),
                [h_mm1, ld["onesm"]])
    h_rs1 = P.v(lambda v: v.tensor_copy(po[:, 0:1], cnt2[:, 0:1]), [h_cm_prev, h_c3])
    h_rs2 = P.v(lambda v: v.tensor_reduce(cnt2[:, 0:1], m01[:, :], axis=AX.X,
                                          op=ALU.add), [h_mm1, h_rs1])
    h_rs3 = P.v(lambda v: v.tensor_copy(po[:, 0:1], cnt2[:, 0:1]), [h_rs2])
    h_tri = P.t(lambda t: t.matmul(p_qkv[0][:, 3:4], tri[:, :], po[:, 0:1],
                                   start=True, stop=True),
                [h_rs3, ld["tri"], bank_last[0]])
    bank_last[0] = h_tri
    h_of1 = P.v(lambda v: v.tensor_copy(offs[:, 0:1], p_qkv[0][:, 3:4]), [h_tri])
    h_or1 = P.v(lambda v: v.scalar_tensor_tensor(ordp[:, :], scan_t[:, :], offs[:, 0:1],
                                                 m01[:, :], op0=ALU.add,
                                                 op1=ALU.subtract), [h_sc1, h_of1])
    h_or2 = P.v(lambda v: v.tensor_tensor(ordp[:, :], ordp[:, :], m01[:, :],
                                          op=ALU.mult), [h_or1])
    h_or3 = P.v(lambda v: v.tensor_scalar(tmp32[:, :], m01[:, :], -511.0, 511.0,
                                          op0=ALU.mult, op1=ALU.add), [h_or1])
    h_or4 = P.v(lambda v: v.tensor_tensor(ordp[:, :], ordp[:, :], tmp32[:, :],
                                          op=ALU.add), [h_or2, h_or3])
    h_p1 = P.v(lambda v: v.tensor_scalar(posi2[:, :], pos_s[:, :], 63, None,
                                         op0=ALU.bitwise_and), [ld["pos"]])
    h_p2 = P.v(lambda v: v.tensor_copy(tmp32[:, :], posi2[:, :]), [h_p1, h_or4])
    h_p3 = P.v(lambda v: v.tensor_scalar(posi2[:, :], pos_s[:, :], 6, None,
                                         op0=ALU.arith_shift_right), [h_p2])
    h_p4 = P.v(lambda v: v.tensor_copy(posf[:, :], posi2[:, :]), [h_p3])
    h_me = []
    h_me.append(P.v(lambda v: v.tensor_copy(meta[:, 0:192:6], thi[:, :]), [ld["thi"]]))
    h_me.append(P.v(lambda v: v.tensor_copy(meta[:, 1:192:6], tlo[:, :]), [ld["tlo"]]))
    h_me.append(P.v(lambda v: v.tensor_copy(meta[:, 2:192:6], posf[:, :]), [h_p4]))
    h_me.append(P.v(lambda v: v.tensor_copy(meta[:, 3:192:6], tmp32[:, :]), [h_p2, h_or4]))
    h_me.append(P.v(lambda v: v.tensor_copy(meta[:, 4:192:6], wacc[:, :]), [h_wb]))
    h_po_prev = None
    h_mm_prev = None
    for a in range(32):
        h_poa = P.v(lambda v, a=a: v.tensor_scalar(po[:, :], iota512[:, :],
                                                   ordp[:, a:a + 1], None,
                                                   op0=ALU.is_equal),
                    [h_or4, ld["iota512"]] + ([h_mm_prev] if h_mm_prev else [h_rs3, h_tri]))
        h_mm_prev = P.t(lambda t, a=a: t.matmul(p_qkv[1][0:5, 0:512],
                                                meta[:, 6 * a:6 * a + 5], po[:, :],
                                                start=(a == 0), stop=(a == 31),
                                                skip_group_check=True),
                        [h_poa, bank_last[1]] + h_me)
        h_po_prev = h_poa
    bank_last[1] = h_mm_prev
    h_s8 = P.v(lambda v: v.tensor_copy(sel8[0:5, :], p_qkv[1][0:5, 0:512]), [h_mm_prev])
    h_s8d = dma_g(sel8D[0:5, :], sel8[0:5, :], deps=[h_s8])
    h_r1d = dma_g(selrow[0:1, :], sel8D[0:1, :], deps=[h_s8d])
    h_r2d = dma_g(selrow2[0:1, :], sel8D[1:2, :], deps=[h_s8d])
    h_r3d = dma_g(posrow[0:1, :], sel8D[2:3, :], deps=[h_s8d])
    h_r4d = dma_g(selrow3[0:1, :], sel8D[3:4, :], deps=[h_s8d])
    h_r5d = dma_g(wrowt[0:1, :], sel8D[4:5, :], deps=[h_s8d])
    h_ri = P.v(lambda v: v.tensor_scalar(selrow[0:1, :], selrow[0:1, :], 64.0, None,
                                         op0=ALU.mult), [h_r1d])
    h_ri2 = P.v(lambda v: v.tensor_tensor(selrow[0:1, :], selrow[0:1, :],
                                          selrow2[0:1, :], op=ALU.add), [h_ri, h_r2d])
    h_sca = P.v(lambda v: v.tensor_scalar(scatrow[0:1, :], selrow[0:1, :],
                                          float(T), None, op0=ALU.min), [h_ri2])
    h_gat = P.v(lambda v: v.tensor_scalar(selrow[0:1, :], selrow[0:1, :],
                                          float(T - 1), None, op0=ALU.min), [h_sca])
    h_pi = P.v(lambda v: v.tensor_scalar(posrow[0:1, :], posrow[0:1, :], 64.0, None,
                                         op0=ALU.mult), [h_r3d])
    h_pi2 = P.v(lambda v: v.tensor_tensor(posrow[0:1, :], posrow[0:1, :],
                                          selrow3[0:1, :], op=ALU.add), [h_pi, h_r4d])
    h_rwd = dma_g(rowsD[0:1, :], selrow[0:1, :], deps=[h_gat])
    h_rwd2 = dma_g(rowsD[1:2, :], scatrow[0:1, :], deps=[h_sca])
    h_rwd3 = dma_g(rowsD[2:3, :], wrowt[0:1, :], deps=[h_r5d])
    h_cl1 = dma_g(selcol[:, 0:4], rowsD[0:1, :].rearrange("o (c p) -> (o p) c", p=128), allow_slow_non_contiguous=True,
                  deps=[h_rwd])
    h_cl2 = dma_g(wselc[:, 0:4], rowsD[2:3, :].rearrange("o (c p) -> (o p) c", p=128), allow_slow_non_contiguous=True,
                  deps=[h_rwd3])
    h_cl3 = dma_g(lodc[:, 0:2], rowsD[1:2, 0:256].rearrange("o (c p) -> (o p) c", p=128), allow_slow_non_contiguous=True,
                  deps=[h_rwd2, h_lc, h_tc, h_mm1])
    h_cl4 = dma_g(cnt2[:, 0:2], rowsD[1:2, 256:512].rearrange("o (c p) -> (o p) c", p=128), allow_slow_non_contiguous=True,
                  deps=[h_rwd2, h_rs2, h_cm_prev])
    h_ci = P.v(lambda v: v.tensor_copy(selcoli[:, :], selcol[:, :]), [h_cl1])
    h_si1 = P.v(lambda v: v.tensor_copy(scatcoli[:, 0:2], lodc[:, 0:2]), [h_cl3])
    h_si2 = P.v(lambda v: v.tensor_copy(scatcoli[:, 2:4], cnt2[:, 0:2]), [h_cl4])

    h_uns = []
    prev_u = [h_ap, prev_sq, prev_m7]
    for a in range(32):
        h_ux = dma_g(x1b[:, :], x1D[a * 128:(a + 1) * 128, :],
                     deps=prev_u + h_x1_done[-1:])
        h_u1 = P.s(lambda s: s.activation(rr[:, :], x1b[:, :], AF.Copy, bias=0.0,
                                          scale=2.0, accum_out=stat[:, 0:1]),
                   [h_ux] + prev_u)
        h_u2 = P.s(lambda s: s.activation(trashb[:, :], rr[:, :], AF.Square,
                                          accum_out=stat[:, 2:3]), [h_u1])
        h_u3 = P.v(lambda v: v.tensor_scalar(stat[:, 0:1], stat[:, 0:1], 1.0 / 1024.0,
                                             None, op0=ALU.mult), [h_u1] + prev_u)
        h_u4 = P.v(lambda v: v.tensor_tensor(stat[:, 3:4], stat[:, 0:1], stat[:, 0:1],
                                             op=ALU.mult), [h_u3])
        h_u5 = P.v(lambda v: v.tensor_scalar(stat[:, 2:3], stat[:, 2:3], 1.0 / 1024.0,
                                             None, op0=ALU.mult), [h_u2])
        h_u6 = P.v(lambda v: v.tensor_tensor(stat[:, 3:4], stat[:, 2:3], stat[:, 3:4],
                                             op=ALU.subtract), [h_u4, h_u5])
        h_u7 = P.v(lambda v: v.tensor_scalar(stat[:, 3:4], stat[:, 3:4], 1e-5, None,
                                             op0=ALU.add), [h_u6])
        h_u8 = P.s(lambda s: s.activation(stat[:, 4:5], stat[:, 3:4], AF.Sqrt),
                   [h_u7, h_u2])
        h_u9 = P.v(lambda v: v.reciprocal(stat[:, 4:5], stat[:, 4:5]), [h_u8])
        h_u10 = P.v(lambda v: v.tensor_scalar(rr2[:, :], rr[:, :], stat[:, 0:1],
                                              stat[:, 4:5], op0=ALU.subtract,
                                              op1=ALU.mult), [h_u9] + h_uns[-1:])
        h_ud = dma_g(outD[a * 128:(a + 1) * 128, :], rr2[:, :], deps=[h_u10])
        h_uns.append(h_ud)
        prev_u = [h_u10, h_u2, h_u8]

    h_xg = []
    for c in range(4):
        h_xg.append(P.gd(lambda g, c=c: g.indirect_dma_start(
            xsel_t(c), None, x1D,
            IndirectOffsetOnAxis(selcoli[:, c:c + 1], 0)),
            [h_ci, aw_last[0], aw_last[1], statez_done[-1]] + h_x1_done + prev_div))
    h_m1 = []
    prev_mg = None
    for c in range(4):
        h_mg = P.gd(lambda g, c=c: g.indirect_dma_start(
            m1f[:, 0:4096], None, maskD,
            IndirectOffsetOnAxis(selcoli[:, c:c + 1], 0)),
            [h_ci, h_cm_prev] + ([prev_mg] if prev_mg else []))
        h_mc = P.v(lambda v, c=c: v.tensor_copy(cmpt[:, :], m1f[:, 0:4096]),
                   [h_mg] + ([h_cm_prev] if c == 0 else []) + h_m1[-1:])
        h_md = dma_g(m1D[c * 128:(c + 1) * 128, :], cmpt[:, :], deps=[h_mc])
        prev_mg = h_mc
        h_m1.append(h_md)

    wslots = {"fwq": BIGW[:, 0:8192], "fwk": BIGW[:, 8192:16384],
              "fwv": BIGW[:, 16384:24576], "fwo": BIGW[:, 24576:32768]}
    wsrc = {"fwq": fwqD, "fwk": fwkD, "fwv": fwvD, "fwo": fwoD}
    h_wffl = {}
    wdeps = list(prev_pass) + [h_mm_prev] + list(aw_last) + list(prev_div)
    for k in wslots:
        h_wffl[k] = [dma_g(wslots[k][:, kc * 1024:(kc + 1) * 1024],
                           wsrc[k][kc * 128:(kc + 1) * 128, :], deps=wdeps)
                     for kc in range(8)]

    h_msel_ev = []
    mselbanks = [p_qkv[0], p_qkv[1], p_awA, p_awB]
    mb_last = {0: bank_last[0], 1: bank_last[1], 2: aw_last[0], 3: aw_last[1]}
    h_selrep = P.t(lambda t: t.matmul(p_ctxA[:, 0:512], onesrow[0:1, :],
                                      selrow[0:1, :], start=True, stop=True),
                   [h_gat, prev_div[0] if prev_div else None, ld["onesrow"]])
    h_srv = P.v(lambda v: v.tensor_copy(selrep[:, :], p_ctxA[:, 0:512]),
                [h_selrep] + prev_div)
    h_m1T = [None, None]
    h_poc = None
    prev_ms = {0: None, 1: None, 2: None, 3: None}
    for a in range(32):
        buf = a % 2
        h_mt = dma_y(m1T[:, buf * 512:(buf + 1) * 512],
                     m1D[0:512, a * 128:(a + 1) * 128],
                     deps=h_m1 + ([h_m1T[buf]] if h_m1T[buf] else
                                  [h_mm_prev, statez_done[-1]]), tr=True)
        h_poc = P.v(lambda v, a=a: v.tensor_scalar(po[:, :], selrep[:, :],
                                                   tcol[:, a:a + 1], None,
                                                   op0=ALU.is_equal),
                    [h_srv, ld["tcol"], h_po_prev] +
                    ([h_poc] if h_poc else []) +
                    ([prev_ms[3]] if prev_ms[3] else []))
        for it_ in range(4):
            hmv = P.t(lambda t, a=a, it_=it_, buf=buf: t.matmul(
                mselbanks[it_][:, 0:512],
                m1T[:, buf * 512 + it_ * 128:buf * 512 + (it_ + 1) * 128],
                po[:, :], start=(a == 0), stop=(a == 31), skip_group_check=True),
                [h_mt, h_poc] + ([mb_last[it_]] if mb_last[it_] else []))
            mb_last[it_] = hmv
            h_m1T[buf] = hmv
        prev_ms[3] = mb_last[3]
    for it_ in range(4):
        hme = P.v(lambda v, it_=it_: v.tensor_copy(
            msel[:, it_ * 512:(it_ + 1) * 512], mselbanks[it_][:, 0:512]),
            [mb_last[it_], h_srv])
        hmz = P.v(lambda v, it_=it_: v.memset(msel[:, it_ * 512 + 511:it_ * 512 + 512],
                                              -1e30), [hme])
        h_msel_ev.append(hmz)
        mb_last[it_] = hmz

    h_hn = []
    prev_sq2 = None
    for c in range(4):
        h_q1 = P.s(lambda s, c=c: s.activation(trashb[:, :], xsel_t(c),
                                               AF.Square, accum_out=stat[:, 2:3]),
                   [h_xg[c]] + ([prev_sq2] if prev_sq2 else [prev_u[1]]) + h_uns[-1:])
        prev_sq2 = h_q1
        h_q2 = P.v(lambda v: v.tensor_scalar(stat[:, 3:4], stat[:, 2:3], 1.0 / 1024.0,
                                             None, op0=ALU.add if False else ALU.mult),
                   [h_q1] + h_uns[-1:])
        h_q3 = P.v(lambda v: v.tensor_scalar(stat[:, 3:4], stat[:, 3:4], 1e-6, None,
                                             op0=ALU.add), [h_q2])
        h_q4 = P.s(lambda s: s.activation(stat[:, 4:5], stat[:, 3:4], AF.Sqrt), [h_q3])
        h_q5 = P.v(lambda v: v.reciprocal(stat[:, 4:5], stat[:, 4:5]), [h_q4])
        h_q6 = P.v(lambda v, c=c: v.scalar_tensor_tensor(
            hnt[:, :], xsel_t(c), stat[:, 4:5],
            ln1_rep[:, :], op0=ALU.mult, op1=ALU.mult),
            [h_q5, ld["ln1"], h_xg[c]] + ([h_hn[-1]] if h_hn else prev_div))
        h_hnd = dma_g(hnD[c * 128:(c + 1) * 128, :], hnt[:, :], deps=[h_q6])
        h_hn.append(h_hnd)
    hnT = xT
    h_hnT = [dma_y(hnT[:, kc * 512:(kc + 1) * 512], hnD[0:512, kc * 128:(kc + 1) * 128],
                   deps=h_hn + h_hT, tr=True) for kc in range(8)]

    angs = m1f[:, 0:1024]
    angc = m1f[:, 1024:2048]
    angt = m1f[:, 2048:3072]
    angt2v = m1f[:, 3072:4096]
    angt3v = rr[:, 0:1024]
    posi128 = r2f[:, 0:1024].bitcast(I32)
    qkbanks = [p_ctxA, p_ctxB]
    qk_last = {0: h_srv, 1: prev_div[1] if len(prev_div) > 1 else None}
    h_qtok = {}
    prev_rope = []
    for wq_name, dstD in (("fwq", qtokD), ("fwk", ktokD)):
        for c in range(4):
            h_qm = []
            for halfn in range(2):
                hmm = None
                for kc in range(8):
                    hmm = P.t(lambda t, c=c, halfn=halfn, kc=kc, wq_name=wq_name:
                              t.matmul(
                        qkbanks[halfn][:, 0:512],
                        hnT[:, kc * 512 + c * 128:kc * 512 + (c + 1) * 128],
                        wslots[wq_name][:, kc * 1024 + halfn * 512:kc * 1024 + (halfn + 1) * 512],
                        start=(kc == 0), stop=(kc == 7)),
                        (h_hnT + h_wffl[wq_name] +
                         ([qk_last[halfn]] if qk_last[halfn] else []))
                        if kc == 0 else [])
                h_qm.append(hmm)
            h_a1 = P.t(lambda t, c=c: t.matmul(p_kv[0][:, 0:512],
                                               posrow[0:1, c * 128:(c + 1) * 128],
                                               invf[0:1, 0:512], start=True, stop=True),
                       [h_pi2, ld["invf"], statez_done[-1] if wq_name == "fwq" and c == 0 else None] +
                       ([prev_rope[-1]] if prev_rope else []))
            h_a2 = P.t(lambda t, c=c: t.matmul(p_kv[1][:, 0:512],
                                               posrow[0:1, c * 128:(c + 1) * 128],
                                               invf[0:1, 512:1024], start=True,
                                               stop=True),
                       [h_pi2, statez_done[-1] if wq_name == "fwq" and c == 0 else None])
            h_b1 = P.v(lambda v: v.tensor_copy(angs[:, 0:512], p_kv[0][:, 0:512]),
                       [h_a1] + prev_rope[-1:])
            h_b2 = P.v(lambda v: v.tensor_copy(angs[:, 512:1024], p_kv[1][:, 0:512]),
                       [h_a2])
            h_b3 = P.v(lambda v: v.tensor_scalar(angc[:, :], angs[:, :],
                                                 float(0.5 * np.pi), None,
                                                 op0=ALU.add), [h_b1, h_b2])
            hsin = None
            for dst_ap, hb in ((sinb, [h_b1, h_b2]), (cosb, [h_b3])):
                src_ang = angs if dst_ap is sinb else angc
                h_t1_ = P.v(lambda v, src_ang=src_ang: v.tensor_scalar(
                    angt[:, :], src_ang[:, :], float(1.0 / TWO_PI), None,
                    op0=ALU.mult), hb + ([hsin] if hsin else []))
                h_t2_ = P.v(lambda v: v.tensor_copy(posi128, angt[:, :]),
                            [h_t1_, h_srv] + prev_u)
                h_t3_ = P.v(lambda v: v.tensor_copy(angt2v[:, :], posi128), [h_t2_])
                h_t4_ = P.v(lambda v: v.tensor_tensor(angt3v[:, :], angt2v[:, :],
                                                      angt[:, :], op=ALU.is_gt),
                            [h_t3_])
                h_t5_ = P.v(lambda v: v.tensor_tensor(angt2v[:, :], angt2v[:, :],
                                                      angt3v[:, :], op=ALU.subtract),
                            [h_t4_])
                h_t6_ = P.v(lambda v, src_ang=src_ang: v.scalar_tensor_tensor(
                    angt[:, :], angt2v[:, :], float(-TWO_PI), src_ang[:, :],
                    op0=ALU.mult, op1=ALU.add), [h_t5_])
                h_t7_ = P.v(lambda v: v.tensor_scalar(angt3v[:, :], angt[:, :],
                                                      float(np.pi), None,
                                                      op0=ALU.is_gt), [h_t6_])
                h_t8_ = P.v(lambda v: v.scalar_tensor_tensor(
                    angt[:, :], angt3v[:, :], float(-TWO_PI), angt[:, :],
                    op0=ALU.mult, op1=ALU.add), [h_t7_])
                hsin = P.s(lambda s, src_ang=src_ang: s.activation(
                    src_ang[:, :], angt[:, :], AF.Sin), [h_t8_, prev_sq2])
                prev_rope.append(hsin)
            h_e1_ = P.v(lambda v: v.tensor_copy(qtok[:, 0:512], qkbanks[0][:, 0:512]),
                        [h_qm[0], hsin] + prev_u +
                        (h_qtok.get(wq_name, [])[-1:] if h_qtok.get(wq_name) else
                         (h_qtok.get("fwq", [])[-1:] if wq_name == "fwk" else [])))
            h_e2_ = P.v(lambda v: v.tensor_copy(qtok[:, 512:1024],
                                                qkbanks[1][:, 0:512]), [h_qm[1]])
            qk_last[0], qk_last[1] = h_e1_, h_e2_
            h_rc1 = P.v(lambda v: v.tensor_copy(
                rcop[:, :].rearrange("p (h a b) -> p h a b", a=2, b=32)[:, :, 0, :],
                qtok[:, :].rearrange("p (h a b) -> p h a b", a=2, b=32)[:, :, 1, :]),
                [h_e1_, h_e2_] + prev_rope[-2:])
            h_rc2 = P.v(lambda v: v.tensor_copy(
                rcop[:, :].rearrange("p (h a b) -> p h a b", a=2, b=32)[:, :, 1, :],
                qtok[:, :].rearrange("p (h a b) -> p h a b", a=2, b=32)[:, :, 0, :]),
                [h_rc1])
            h_rm1 = P.v(lambda v: v.tensor_tensor(rcop[:, :], rcop[:, :], angs[:, :],
                                                  op=ALU.mult), [h_rc2])
            h_rm2 = P.v(lambda v: v.tensor_tensor(qtok[:, :], qtok[:, :], angc[:, :],
                                                  op=ALU.mult), [h_rm1])
            h_rm3 = P.v(lambda v: v.tensor_tensor(qtok[:, :], qtok[:, :], rcop[:, :],
                                                  op=ALU.add), [h_rm2])
            prev_rope.append(h_rm3)
            h_qd = dma_g(dstD[c * 128:(c + 1) * 128, :], qtok[:, :], deps=[h_rm3])
            h_qtok.setdefault(wq_name, []).append(h_qd)
    h_qT = [dma_y(qTb[:, kc * 512:(kc + 1) * 512],
                  qtokD[0:512, kc * 128:(kc + 1) * 128],
                  deps=h_qtok["fwq"] + h_qtok["fwk"] + [prev_rope[-1]], tr=True)
            for kc in range(8)]
    h_kT = [dma_y(kTb[:, kc * 512:(kc + 1) * 512],
                  ktokD[0:512, kc * 128:(kc + 1) * 128],
                  deps=h_qtok["fwk"] + h_m1 + [h_cm_prev, prev_mg], tr=True)
            for kc in range(8)]
    h_vm = []
    for c in range(4):
        for halfn in range(2):
            hmm = None
            for kc in range(8):
                hmm = P.t(lambda t, c=c, halfn=halfn, kc=kc: t.matmul(
                    qkbanks[halfn][:, 0:512],
                    hnT[:, kc * 512 + c * 128:kc * 512 + (c + 1) * 128],
                    wslots["fwv"][:, kc * 1024 + halfn * 512:kc * 1024 + (halfn + 1) * 512],
                    start=(kc == 0), stop=(kc == 7)),
                    (h_hnT + h_wffl["fwv"] + [qk_last[halfn]]) if kc == 0 else [])
            hev = P.v(lambda v, c=c, halfn=halfn: v.tensor_copy(
                vselb[:, c * 1024 + halfn * 512:c * 1024 + (halfn + 1) * 512],
                qkbanks[halfn][:, 0:512]), [hmm])
            qk_last[halfn] = hev
            h_vm.append(hev)

    scbanks = [p_qkv[0], p_qkv[1], p_awA, p_awB]
    sc_last = {i: mb_last[i] for i in range(4)}
    h_adma = {}
    h_arow_prev = None
    h_ad_prev = None
    h_sume = {}
    bank_i = 0
    h_scev_all = []
    for c in range(4):
        for hd in range(16):
            kc2, ro = hd // 2, (hd % 2) * 64
            bk = bank_i % 4
            bank_i += 1
            h_scm = P.t(lambda t, c=c, hd=hd, kc2=kc2, ro=ro, bk=bk: t.matmul(
                scbanks[bk][:, 0:512],
                qTb[ro:ro + 64, kc2 * 512 + c * 128:kc2 * 512 + (c + 1) * 128],
                kTb[ro:ro + 64, kc2 * 512:kc2 * 512 + 512],
                start=True, stop=True),
                h_qT + h_kT + [sc_last[bk]])
            dep_ms = h_msel_ev[c]
            h_s1_ = P.v(lambda v, c=c, bk=bk: v.scalar_tensor_tensor(
                scs[:, 0:512], scbanks[bk][:, 0:512], 0.125,
                msel[:, c * 512:(c + 1) * 512], op0=ALU.mult, op1=ALU.add),
                [h_scm, dep_ms] + ([h_arow_prev] if h_arow_prev else []))
            h_s2_ = P.v(lambda v: v.tensor_reduce(stat[:, 6:7], scs[:, 0:512],
                                                  axis=AX.X, op=ALU.max), [h_s1_])
            h_s3_ = P.v(lambda v: v.tensor_scalar(stat[:, 6:7], stat[:, 6:7], -1.0,
                                                  None, op0=ALU.mult), [h_s2_])
            h_s4_ = P.s(lambda s, hd=hd: s.activation(
                rr[:, 0:512], scs[:, 0:512], AF.Exp, bias=stat[:, 6:7], scale=1.0,
                accum_out=stat[:, 7:8]),
                [h_s3_, prev_sq2] + ([h_ad_prev] if h_ad_prev else []) +
                (prev_u if (c == 0 and hd == 0) else []))
            h_s5_ = P.v(lambda v: v.reciprocal(stat[:, 7:8], stat[:, 7:8]),
                        [h_s4_])
            h_s6_ = P.v(lambda v: v.tensor_scalar(arow[:, :], rr[:, 0:512],
                                                  stat[:, 7:8], None, op0=ALU.mult),
                        [h_s5_])
            sc_last[bk] = h_s1_
            h_arow_prev = h_s6_
            h_ad = P.gd(lambda g, c=c, hd=hd: g.dma_start(
                out=aDr[hd * 512 + c * 128:hd * 512 + (c + 1) * 128, :],
                in_=arow[:, :]), [h_s6_])
            h_ad_prev = h_ad
            h_adma.setdefault(hd, []).append(h_ad)
            h_scev_all.append(h_s6_)
    obanks = [p_ctxA, p_ctxB, p_kv[0], p_kv[1]]
    ob_last = {0: qk_last[0], 1: qk_last[1], 2: None, 3: None}
    h_oev = []
    aT_prev = [None, None]
    for hd in range(16):
        bk = hd % 4
        win = hd * 64 - 64 * (hd % 2)
        h_om = None
        for jc in range(4):
            buf = jc % 2
            h_at = dma_y(aT[:, buf * 512:(buf + 1) * 512],
                         aDr[hd * 512:hd * 512 + 512, jc * 128:(jc + 1) * 128],
                         deps=h_adma[hd] + ([aT_prev[buf]] if aT_prev[buf] else []),
                         tr=True)
            h_om = P.t(lambda t, hd=hd, jc=jc, bk=bk, win=win, buf=buf: t.matmul(
                obanks[bk][:, 0:512],
                vselb[:, jc * 1024 + win:jc * 1024 + win + 128],
                aT[:, buf * 512:(buf + 1) * 512],
                start=(jc == 0), stop=(jc == 3), skip_group_check=True),
                [h_at] + h_vm + ([ob_last[bk]] if ob_last[bk] else []) +
                ([statez_done[-1]] if bk >= 2 else []))
            aT_prev[buf] = h_om
        ro = (hd % 2) * 64
        hoe = P.v(lambda v, hd=hd, bk=bk, ro=ro: v.tensor_copy(
            oT[ro:ro + 64, (hd // 2) * 512:(hd // 2 + 1) * 512],
            obanks[bk][ro:ro + 64, 0:512]), [h_om])
        ob_last[bk] = hoe
        h_oev.append(hoe)
    h_x2a = []
    wo_last = {0: ob_last[0], 1: ob_last[1]}
    for c in range(4):
        for halfn in range(2):
            hmm = None
            for kc in range(8):
                hmm = P.t(lambda t, c=c, halfn=halfn, kc=kc: t.matmul(
                    qkbanks[halfn][:, 0:512],
                    oT[:, kc * 512 + c * 128:kc * 512 + (c + 1) * 128],
                    wslots["fwo"][:, kc * 1024 + halfn * 512:kc * 1024 + (halfn + 1) * 512],
                    start=(kc == 0), stop=(kc == 7)),
                    (h_oev + h_wffl["fwo"] + [wo_last[halfn]]) if kc == 0 else [])
            hx2 = P.v(lambda v, c=c, halfn=halfn: v.scalar_tensor_tensor(
                x2a[:, c * 1024 + halfn * 512:c * 1024 + (halfn + 1) * 512],
                qkbanks[halfn][:, 0:512], 1.0,
                xsel_t(c)[:, halfn * 512:(halfn + 1) * 512],
                op0=ALU.mult, op1=ALU.add),
                [hmm, h_xg[c]] + ([prev_rope[-1]] if c == 0 and halfn == 0 else []))
            wo_last[halfn] = hx2
            h_x2a.append(hx2)

    h_hn2 = []
    prev_sq3 = prev_sq2
    for c in range(4):
        h_q1 = P.s(lambda s, c=c: s.activation(trashb[:, :],
                                               x2a[:, c * 1024:(c + 1) * 1024],
                                               AF.Square, accum_out=stat[:, 2:3]),
                   [h_x2a[2 * c], h_x2a[2 * c + 1], prev_sq3])
        prev_sq3 = h_q1
        h_q2 = P.v(lambda v: v.tensor_scalar(stat[:, 3:4], stat[:, 2:3], 1.0 / 1024.0,
                                             None, op0=ALU.mult), [h_q1])
        h_q3 = P.v(lambda v: v.tensor_scalar(stat[:, 3:4], stat[:, 3:4], 1e-6, None,
                                             op0=ALU.add), [h_q2])
        h_q4 = P.s(lambda s: s.activation(stat[:, 4:5], stat[:, 3:4], AF.Sqrt),
                   [h_q3, prev_sq3])
        prev_sq3 = h_q4
        h_q5 = P.v(lambda v: v.reciprocal(stat[:, 4:5], stat[:, 4:5]), [h_q4])
        h_q6 = P.v(lambda v, c=c: v.scalar_tensor_tensor(
            hnt[:, :], x2a[:, c * 1024:(c + 1) * 1024], stat[:, 4:5],
            ln2_rep[:, :], op0=ALU.mult, op1=ALU.mult),
            [h_q5, ld["ln2"]] + ([h_hn2[-1]] if h_hn2 else [h_hn[-1]]))
        h_h2d = dma_g(hn2D[c * 128:(c + 1) * 128, :], hnt[:, :], deps=[h_q6])
        h_hn2.append(h_h2d)
    h_hn2T = [dma_y(hn2T[:, kc * 512:(kc + 1) * 512],
                    hn2D[0:512, kc * 128:(kc + 1) * 128],
                    deps=h_hn2 + h_scev_all[-1:] + [h_mm_prev, prev_sq2],
                    tr=True) for kc in range(8)]
    w1deps = h_x2a + h_scev_all + h_oev
    h_w1l = [dma_g(BIGW[:, kc * 4096:(kc + 1) * 4096], w1D[kc * 128:(kc + 1) * 128, :],
                   deps=w1deps) for kc in range(8)]
    h_gel = []
    h_gcp = []
    g_last = {0: wo_last[0], 1: wo_last[1], 2: aw_last[0], 3: aw_last[1]}
    g_last = {0: wo_last[0], 1: wo_last[1], 2: sc_last[2], 3: sc_last[3]}
    h1_written = {}
    for ft in range(32):
        bk = ft % 4
        hmm = None
        for kc in range(8):
            hmm = P.t(lambda t, ft=ft, kc=kc, bk=bk: t.matmul(
                scbanks[bk][:, 0:512],
                BIGW[:, kc * 4096 + ft * 128:kc * 4096 + (ft + 1) * 128],
                hn2T[:, kc * 512:(kc + 1) * 512],
                start=(kc == 0), stop=(kc == 7)),
                (h_w1l + h_hn2T + [g_last[bk]] +
                 (h_kT if ft == 2 else []) + (h_vm if ft == 3 else []))
                if kc == 0 else [])
        gs = (ft % 2) * 512
        hge = P.s(lambda s, ft=ft, bk=bk, gs=gs: s.activation(
            rr[:, gs:gs + 512], scbanks[bk][:, 0:512],
            AF.Gelu_apprx_tanh), [hmm, prev_sq3, prev_rope[-1], h_oev[-1],
                                  h_adma[15][-1], h_srv, h_scev_all[-1]] +
            ([h_gel[-1]] if h_gel else []) +
            ([h_gcp[ft - 2]] if ft >= 2 else []))
        hcp = P.v(lambda v, ft=ft, gs=gs: v.tensor_copy(
            h1slot(ft), rr[:, gs:gs + 512]), [hge] + h_uns[-1:])
        g_last[bk] = hge
        h_gcp.append(hcp)
        h_gel.append(hge)
        h_gel.append(hcp)
        h1_written[ft] = hcp
    h_w2l = [dma_g(BIGW[:, fc * 1024:(fc + 1) * 1024], w2D[fc * 128:(fc + 1) * 128, :],
                   deps=h_gel) for fc in range(32)]
    h_sct = []
    w2_last = {0: g_last[0], 1: g_last[1]}
    prev_tail = list(prev_u) + [prev_rope[-1]]
    for c in range(4):
        h_ps = []
        for halfn in range(2):
            hmm = None
            for fc in range(32):
                hmm = P.t(lambda t, c=c, halfn=halfn, fc=fc: t.matmul(
                    qkbanks[halfn][:, 0:512],
                    h1slot(fc)[:, c * 128:(c + 1) * 128],
                    BIGW[:, fc * 1024 + halfn * 512:fc * 1024 + (halfn + 1) * 512],
                    start=(fc == 0), stop=(fc == 31)),
                    (h_w2l + h_gel + [w2_last[halfn]]) if fc == 0 else [])
            h_ps.append(hmm)
        h_v1 = []
        for halfn in range(2):
            h_v1.append(P.v(lambda v, c=c, halfn=halfn: v.scalar_tensor_tensor(
                rr[:, halfn * 512:(halfn + 1) * 512], qkbanks[halfn][:, 0:512], 1.0,
                x2a[:, c * 1024 + halfn * 512:c * 1024 + (halfn + 1) * 512],
                op0=ALU.mult, op1=ALU.add),
                [h_ps[halfn]] + prev_tail + ([h_x2a[-1]] if c == 0 and halfn == 0 else [])))
            w2_last[halfn] = h_v1[halfn]
        h_v2 = []
        for halfn in range(2):
            h_v2.append(P.v(lambda v, c=c, halfn=halfn: v.scalar_tensor_tensor(
                r2f[:, halfn * 512:(halfn + 1) * 512], rr[:, halfn * 512:(halfn + 1) * 512],
                wselc[:, c:c + 1],
                xsel_t(c)[:, halfn * 512:(halfn + 1) * 512],
                op0=ALU.mult, op1=ALU.add, accum_out=stat[:, halfn:halfn + 1]),
                [h_v1[halfn], h_cl2, h_xg[c]] + prev_tail))
        h_w1_ = P.s(lambda s: s.activation(trashb[:, :], r2f[:, :], AF.Square,
                                           accum_out=stat[:, 2:3]),
                    h_v2 + [prev_sq3] + ([h_gel[-1]] if h_gel else []))
        prev_sq3 = h_w1_
        h_w2_ = P.v(lambda v: v.tensor_tensor(stat[:, 0:1], stat[:, 0:1], stat[:, 1:2],
                                              op=ALU.add), h_v2)
        h_w3_ = P.v(lambda v: v.tensor_scalar(stat[:, 0:1], stat[:, 0:1], 1.0 / 1024.0,
                                              None, op0=ALU.mult), [h_w2_])
        h_w4_ = P.v(lambda v: v.tensor_tensor(stat[:, 3:4], stat[:, 0:1], stat[:, 0:1],
                                              op=ALU.mult), [h_w3_])
        h_w5_ = P.v(lambda v: v.tensor_scalar(stat[:, 2:3], stat[:, 2:3], 1.0 / 1024.0,
                                              None, op0=ALU.mult), [h_w1_])
        h_w6_ = P.v(lambda v: v.tensor_tensor(stat[:, 3:4], stat[:, 2:3], stat[:, 3:4],
                                              op=ALU.subtract), [h_w4_, h_w5_])
        h_w7_ = P.v(lambda v: v.tensor_scalar(stat[:, 3:4], stat[:, 3:4], 1e-5, None,
                                              op0=ALU.add), [h_w6_])
        h_w8_ = P.s(lambda s: s.activation(stat[:, 4:5], stat[:, 3:4], AF.Sqrt),
                    [h_w7_, prev_sq3])
        h_w9_ = P.v(lambda v: v.reciprocal(stat[:, 4:5], stat[:, 4:5]), [h_w8_])
        h_wA_ = P.v(lambda v: v.tensor_scalar(rr2[:, :], r2f[:, :], stat[:, 0:1],
                                              stat[:, 4:5], op0=ALU.subtract,
                                              op1=ALU.mult),
                    [h_w9_] + prev_tail + h_uns[-1:] + h_sct[-1:])
        h_sc_ = P.gd(lambda g, c=c: g.indirect_dma_start(
            outD, IndirectOffsetOnAxis(scatcoli[:, c:c + 1], 0), rr2[:, :], None),
            [h_wA_, h_si1, h_si2] + h_uns)
        h_sct.append(h_sc_)
        prev_tail = [h_wA_, h_w8_]
    P.g(lambda g: g.engine_nop(), h_sct)

    with nc.Block() as block, ExitStack() as sstack:
        P.emit(nc, block, sstack)
    ctx.close()
    nc._n_items = len(P.items)
    return nc


def _consts():
    t_grid = (np.arange(32)[None, :] * 128 + np.arange(128)[:, None]).astype(np.float32)
    inv = (1.0 / 10000.0 ** (np.arange(0, 64, 2, dtype=np.float64) / 64.0))
    invs = np.concatenate([-inv, inv])
    invf = np.tile(invs, 16).astype(np.float32)[None, :]
    return {
        "icol": np.arange(1, 129, dtype=np.float32).reshape(128, 1),
        "iota512": np.broadcast_to(np.arange(512, dtype=np.float32), (128, 512)).copy(),
        "thi": np.floor(t_grid / 64).astype(np.float32),
        "tlo": (t_grid % 64).astype(np.float32),
        "tcol": t_grid.copy(),
        "tri": np.tril(np.ones((128, 128)), -1).T.astype(ml_dtypes.bfloat16),
        "idf128": np.eye(128, dtype=np.float32),
        "onesrow": np.ones((1, 128), np.float32),
        "awmask": np.tile(np.triu(np.ones((128, 128), np.float32)), (1, 4)).astype(ml_dtypes.bfloat16),
        "invf": invf,
        "onesm": np.ones((128, 32), np.float32),
    }


def _stage_core(inputs, b):
    rep = lambda v: np.ascontiguousarray(np.broadcast_to(v.reshape(1, -1), (128, v.size)).astype(np.float32))
    pos = np.asarray(inputs["L0_ffn_position_ids"][b]).astype(np.int32)
    posb = np.ascontiguousarray(pos.reshape(32, 128).T)
    m = {
        "x": np.ascontiguousarray(inputs["L0_attn_x"][b], np.float32),
        "wqkv": np.ascontiguousarray(inputs["L0_attn_Wqkv"], np.float32),
        "bqkv": np.ascontiguousarray(inputs["L0_attn_bqkv"], np.float32).reshape(-1, 1),
        "proj": np.ascontiguousarray(inputs["L0_attn_proj"], np.float32),
        "wout": np.ascontiguousarray(inputs["L0_attn_Wout"], np.float32),
        "maskb": np.ascontiguousarray(inputs["L0_ffn_attention_mask"][b, 0], np.float32),
        "posb": posb,
        "rb": np.asarray(inputs["L0_ffn_router_b"], np.float32).reshape(1, 1),
        "fwq": np.ascontiguousarray(inputs["L0_ffn_wq"], np.float32),
        "fwk": np.ascontiguousarray(inputs["L0_ffn_wk"], np.float32),
        "fwv": np.ascontiguousarray(inputs["L0_ffn_wv"], np.float32),
        "fwo": np.ascontiguousarray(inputs["L0_ffn_wo"], np.float32),
        "w1": np.ascontiguousarray(inputs["L0_ffn_w1"], np.float32),
        "w2": np.ascontiguousarray(inputs["L0_ffn_w2"], np.float32),
        "bout_rep": rep(np.asarray(inputs["L0_attn_bout"])),
        "bv_rep": rep(np.asarray(inputs["L0_attn_bqkv"])[2048:]),
        "ln1_rep": rep(np.asarray(inputs["L0_ffn_ln1"])),
        "ln2_rep": rep(np.asarray(inputs["L0_ffn_ln2"])),
        "rw_rep": rep(np.asarray(inputs["L0_ffn_router_w"])[:, 0]),
    }
    m.update(_consts())
    return m


def kernel(**inputs):
    nc = build_nc(debug_stage=0)
    maps = [_stage_core(inputs, c >> 1) for c in range(8)]
    res = run_bass_kernel_spmd(nc, maps, core_ids=list(range(8)))
    out = np.stack([np.asarray(res.results[2 * b]["out"])[:T] for b in range(B)])
    return out


# version 02
# speedup vs baseline: 1.0776x; 1.0776x over previous
import numpy as np
import ml_dtypes
import concourse.bass as bass
import concourse.mybir as mybir
from concourse.bass_utils import run_bass_kernel_spmd
from concourse.bass_core_rust import IndirectOffsetOnAxis

F32 = mybir.dt.float32
BF16 = mybir.dt.bfloat16
I32 = mybir.dt.int32
U32 = mybir.dt.uint32
AF = mybir.ActivationFunctionType
ALU = mybir.AluOpType
AX = mybir.AxisListType

B, T, C = 4, 4096, 1024
H, M, CHUNK, D = 16, 64, 128, 64
NCH = T // CHUNK
KSEL = 511
FF = 4096
DELTA = 1e-6
PASS_T = 512
NPASS = T // PASS_T
PCH = PASS_T // CHUNK
NITER = 6
TWO_PI = float(2.0 * np.pi)


class Prog:

    CLASSES = ("t", "v", "s", "g", "gd", "yd", "sd")
    ENG = {"t": "tensor", "v": "vector", "s": "scalar", "g": "gpsimd",
           "gd": "gpsimd", "yd": "sync", "sd": "scalar"}
    DMA = ("gd", "yd", "sd")
    R = 16

    def __init__(self):
        self.items = []
        self.counts = {c: 0 for c in self.CLASSES}
        self.trunc = None
        self.dummyS = None
        self.dummyD = None

    def _add(self, cls, fn, deps):
        deps = tuple(d for d in deps if d is not None)
        self.counts[cls] += 1
        h = (cls, self.counts[cls])
        self.items.append((cls, fn, deps, h, len(self.items)))
        return h

    def t(self, fn, deps=()): return self._add("t", fn, deps)
    def v(self, fn, deps=()): return self._add("v", fn, deps)
    def s(self, fn, deps=()): return self._add("s", fn, deps)
    def g(self, fn, deps=()): return self._add("g", fn, deps)
    def gd(self, fn, deps=()): return self._add("gd", fn, deps)
    def yd(self, fn, deps=()): return self._add("yd", fn, deps)
    def sd(self, fn, deps=()): return self._add("sd", fn, deps)

    def _semkey(self, h):
        cls, idx = h
        if cls in self.DMA:
            return (cls, (idx - 1) % self.R), 16 * ((idx - 1) // self.R + 1)
        return (cls, 0), idx

    def emit(self, nc, block, stack):
        sems = {}
        for c in self.CLASSES:
            if c in self.DMA:
                for r in range(self.R):
                    sems[(c, r)] = stack.enter_context(nc.semaphore(f"pr_{c}{r}"))
            else:
                sems[(c, 0)] = stack.enter_context(nc.semaphore(f"pr_{c}"))
        per_eng = {}
        for it in self.items:
            per_eng.setdefault(self.ENG[it[0]], []).append(it)

        def run(eng_name, eng):
            waited = {}
            for cls, fn, deps, h, seq in per_eng.get(eng_name, []):
                if cls in self.DMA:
                    key, val = self._semkey(h)
                    prev = val - 16
                    if prev > 0 and waited.get(key, -1) < prev:
                        eng.wait_ge(sems[key], prev)
                        waited[key] = prev
                for d in deps:
                    key, val = self._semkey(d)
                    if waited.get(key, -1) >= val:
                        continue
                    eng.wait_ge(sems[key], val)
                    waited[key] = val
                key, val = self._semkey(h)
                if self.trunc is not None and seq >= self.trunc:
                    if cls in self.DMA:
                        fn = lambda e: e.dma_start(out=self.dummyS[0:1, 0:4],
                                                   in_=self.dummyD[0:1, 0:4])
                    else:
                        fn = lambda e: e.nop()
                fn(eng).then_inc(sems[key], 16 if cls in self.DMA else 1)

        @block.gpsimd
        def _(g): run("gpsimd", g)

        @block.sync
        def _(y): run("sync", y)

        @block.vector
        def _(v): run("vector", v)

        @block.tensor
        def _(t): run("tensor", t)

        @block.scalar
        def _(s): run("scalar", s)

        @block.scalar
        def _(s):
            for c in self.CLASSES:
                n = self.counts[c]
                if not n:
                    continue
                if c in self.DMA:
                    for r in range(self.R):
                        cnt = (n - 1 - r) // self.R + 1 if r < n else 0
                        if r < min(n, self.R) and cnt > 0:
                            s.wait_ge(sems[(c, r)], 16 * cnt)
                else:
                    s.wait_ge(sems[(c, 0)], n)


def build_nc(debug_stage=0, dbg=False, trunc=None):
    nc = bass.Bass("TRN2", target_bir_lowering=False, debug=dbg)
    from contextlib import ExitStack
    ctx = ExitStack()

    din = lambda n, shape, dt=F32: nc.dram_tensor(n, shape, dt, kind="ExternalInput").ap()
    dscr = lambda n, shape, dt: nc.dram_tensor(n, shape, dt).ap()
    sb = lambda n, shape, dt=F32: ctx.enter_context(nc.sbuf_tensor(n, shape, dt))
    psum = lambda n, shape: ctx.enter_context(nc.psum_tensor(n, shape, F32))

    xD = din("x", [T, C])
    wqkvD = din("wqkv", [3 * C, C])
    bqkvD = din("bqkv", [3 * C, 1])
    projD = din("proj", [H, D, M])
    woutD = din("wout", [C, C])
    maskD = din("maskb", [T, T])
    posD = din("posb", [128, 32], I32)
    rbD = din("rb", [1, 1])
    fwqD = din("fwq", [C, C])
    fwkD = din("fwk", [C, C])
    fwvD = din("fwv", [C, C])
    fwoD = din("fwo", [C, C])
    w1D = din("w1", [C, FF])
    w2D = din("w2", [FF, C])
    bout_repD = din("bout_rep", [128, C])
    bv_repD = din("bv_rep", [128, C])
    ln1_repD = din("ln1_rep", [128, C])
    ln2_repD = din("ln2_rep", [128, C])
    rw_repD = din("rw_rep", [128, C])
    icolD = din("icol", [128, 1])
    iota512D = din("iota512", [128, 512])
    thiD = din("thi", [128, 32])
    tloD = din("tlo", [128, 32])
    tcolD = din("tcol", [128, 32])
    triD = din("tri", [128, 128], BF16)
    idf128D = din("idf128", [128, 128])
    onesrowD = din("onesrow", [1, 128])
    awmaskD = din("awmask", [128, 512], BF16)
    invfD = din("invf", [1, C])
    onesmD = din("onesm", [128, 32])

    outD = nc.dram_tensor("out", [T + 1, C], F32, kind="ExternalOutput").ap()

    xbfD = dscr("xbf", [T, C], BF16)
    wvbfD = dscr("wvbf", [C, C], BF16)
    woutbfD = dscr("woutbf", [C, C], BF16)
    kpxD = dscr("kpx_d", [2 * C, T], BF16)
    hD = dscr("h_d", [T, C], BF16)
    x1D = dscr("x1_d", [T, C], BF16)
    wwD = dscr("ww_d", [1, 4096], F32)
    sel8D = dscr("sel8_d", [8, 512], F32)
    rowsD = dscr("rows_d", [4, 512], F32)
    m1D = dscr("m1_d", [512, T], BF16)
    hnD = dscr("hn_d", [512, C], BF16)
    qtokD = dscr("qtok_d", [512, C], BF16)
    ktokD = dscr("ktok_d", [512, C], BF16)
    hn2D = dscr("hn2_d", [512, C], BF16)
    aDr = dscr("a_d", [H * 512, 512], BF16)

    P = Prog()
    P.trunc = trunc
    P.dummyD = dscr("dummy_d", [1, 4], F32)

    BIGW = sb("BIGW", [128, 32768], BF16)
    P.dummyS = sb("dummy_s", [1, 4], F32)
    Wqp = BIGW[:, 0:8192]
    Wkp = BIGW[:, 8192:16384]
    WvT = BIGW[:, 16384:24576]
    WoutT = BIGW[:, 24576:32768]
    qpT = sb("qpT", [128, 8 * PASS_T], BF16)
    kpT = sb("kpT", [128, 8 * PASS_T], BF16)
    vst = sb("vst", [128, PCH * 1040], BF16)
    xT = sb("xT", [128, 8 * 512], BF16)
    kptok = sb("kptok", [128, 2048], BF16)
    state_z = sb("state_z", [128, 16 * 65], BF16)
    awb = sb("awb", [128, 2048], BF16)
    hh = sb("hh", [128, 1024], BF16)
    bqp = sb("bqp", [128, 8])
    bkp = sb("bkp", [128, 8])
    kspart = sb("kspart", [128, 8 * PCH])
    ksrun = sb("ksrun", [128, 8])
    recn = sb("recn", [128, 16])
    rr = sb("rr", [128, 1024])
    stat = sb("stat", [128, 8])
    ustat = sb("ustat", [128, 8])
    x1b = sb("x1b", [128, 1024], BF16)
    wacc = sb("wacc", [128, 32])
    trashb = sb("trashb", [128, 1024], BF16)
    trashv = sb("trashv", [128, 1024], BF16)
    zbf = sb("zbf", [128, 1024], BF16)
    bout_rep = sb("bout_rep_s", [128, C])
    bv_rep = sb("bv_rep_s", [128, C])
    ln1_rep = sb("ln1_rep_s", [128, C])
    ln2_rep = sb("ln2_rep_s", [128, C])
    rw_rep = sb("rw_rep_s", [128, C])
    icol = sb("icol_s", [128, 1])
    iota512 = sb("iota512_s", [128, 512])
    thi = sb("thi_s", [128, 32])
    tlo = sb("tlo_s", [128, 32])
    tcol = sb("tcol_s", [128, 32])
    tri = sb("tri_s", [128, 128], BF16)
    idf128 = sb("idf128_s", [128, 128])
    onesrow = sb("onesrow_s", [1, 128])
    awmask = sb("awmask_s", [128, 512], BF16)
    invf = sb("invf_s", [1, C])
    neglarge = sb("neglarge", [1, 128])
    poslarge = sb("poslarge", [1, 128])
    czero = sb("czero", [128, 1])
    bstg = sb("bstg", [128, 1], BF16)
    brow = sb("brow", [1, 128])
    lodc = sb("lodc", [128, 2])
    pos_s = sb("pos_s", [128, 32], I32)
    wrepx = None
    cmpt = sb("cmpt", [128, 4096], BF16)
    cnt2 = sb("cnt2", [128, 2])
    tpc = sb("tpc", [1, 128])
    tpp = sb("tpp", [1, 128])
    st8 = sb("st8", [1, 8])
    probes = sb("probes", [128, 1])
    r1 = sb("r1", [1, 128])
    r1m = sb("r1m", [1, 128], U32)
    m01 = sb("m01", [128, 32])
    scan_t = sb("scan_t", [128, 32])
    offs = sb("offs", [128, 1])
    ordp = sb("ordp", [128, 32])
    meta = sb("meta", [128, 192], BF16)
    po = sb("po", [128, 512], BF16)
    sel8 = sb("sel8_s", [8, 512])
    selrow = sb("selrow", [1, 512])
    selrow2 = sb("selrow2", [1, 512])
    selrow3 = sb("selrow3", [1, 512])
    posrow = sb("posrow", [1, 512])
    wrowt = sb("wrowt", [1, 512])
    scatrow = sb("scatrow", [1, 512])
    selcol = sb("selcol", [128, 4])
    selcoli = sb("selcoli", [128, 4], I32)
    scatcoli = sb("scatcoli", [128, 4], I32)
    wselc = sb("wselc", [128, 4])
    selrep = sb("selrep", [128, 512])
    selrep_bfv = selrep[:, :].bitcast(BF16)
    posi2 = sb("posi2", [128, 32], I32)
    posf = sb("posf", [128, 32])
    tmp32 = sb("tmp32", [128, 32])
    onesm = sb("onesm_s", [128, 32])
    SCRATCH4 = sb("SCRATCH4", [128, 4096])
    qTb = SCRATCH4[:, 0:2048].bitcast(BF16)
    vselb = SCRATCH4[:, 2048:4096].bitcast(BF16)
    wrepx = SCRATCH4
    m1f = SCRATCH4
    x2a = SCRATCH4
    scs = sb("scs", [128, 512])
    scs_bfv = scs[:, :].bitcast(BF16)
    
    arow = sb("arow", [128, 512], BF16)
    aT = sb("aT", [128, 1024], BF16)
    kTb = cmpt
    rcop = hh
    hnt = hh
    qtok = x1b
    cosb = sb("cosb", [128, 1024], BF16)
    sinb = sb("sinb", [128, 1024], BF16)
    rr2 = bout_rep
    r2f = bv_rep
    msel = vst[:, 0:2048].bitcast(BF16) if False else None
    xsel = None

    p_qkv = [psum(f"p_qkv{i}", [128, 512]) for i in range(2)]
    p_awA = psum("p_awA", [128, 512])
    p_awB = psum("p_awB", [128, 512])
    p_ctxA = psum("p_ctxA", [128, 512])
    p_ctxB = psum("p_ctxB", [128, 512])
    p_kv = [psum(f"p_kv{i}", [128, 512]) for i in range(2)]

    def dma_g(dst, src, deps=(), allow_slow_non_contiguous=False):
        if allow_slow_non_contiguous:
            return P.gd(lambda g: g.dma_start(out=dst, in_=src,
                                              allow_slow_non_contiguous=True), deps)
        return P.gd(lambda g: g.dma_start(out=dst, in_=src), deps)

    def dma_y(dst, src, deps=(), tr=False):
        if tr:
            return P.yd(lambda y: y.dma_start(out=dst, in_=src, transpose=True), deps)
        return P.yd(lambda y: y.dma_start(out=dst, in_=src), deps)

    def dma_s(dst, src, deps=(), tr=False):
        if tr:
            return P.sd(lambda s: s.dma_start(out=dst, in_=src, transpose=True), deps)
        return P.sd(lambda s: s.dma_start(out=dst, in_=src), deps)

    ld = {}
    for name, dst, src in [
        ("icol", icol, icolD), ("iota512", iota512, iota512D), ("thi", thi, thiD),
        ("tlo", tlo, tloD), ("tcol", tcol, tcolD), ("tri", tri, triD),
        ("idf128", idf128, idf128D), ("onesrow", onesrow, onesrowD),
        ("awmask", awmask, awmaskD), ("invf", invf, invfD),
        ("bout", bout_rep, bout_repD), ("bv", bv_rep, bv_repD),
        ("ln1", ln1_rep, ln1_repD), ("ln2", ln2_rep, ln2_repD),
        ("rw", rw_rep, rw_repD), ("pos", pos_s, posD), ("onesm", onesm, onesmD),
    ]:
        ld[name] = dma_g(dst[:, :], src[:, :])
    h_cz = P.v(lambda v: v.memset(czero[:, :], 0.0))
    h_zb = P.v(lambda v: v.memset(zbf[:, :], 0.0))
    h_sz = P.v(lambda v: v.memset(state_z[:, :], 0.0))
    h_ks0 = P.v(lambda v: v.memset(ksrun[:, :], 0.0))
    P.v(lambda v: v.memset(neglarge[0:1, :], -1e30))
    P.v(lambda v: v.memset(poslarge[0:1, :], 1e30))

    h_xbf = dma_g(xbfD[:, :], xD[:, :])
    h_wvb = dma_g(wvbfD[:, :], wqkvD[2 * C:3 * C, :])
    h_wob = dma_g(woutbfD[:, :], woutD[:, :])

    h_kpz = []
    for mt in range(16):
        for j in range(4):
            h_kpz.append(dma_g(
                kpxD[mt * 128 + 64:mt * 128 + 128, j * 1024:(j + 1) * 1024],
                zbf[0:64, :], deps=[h_zb]))

    stg = BIGW[:, 16384:32768]
    h_wqk = [dma_g(stg[:, r * 1024:(r + 1) * 1024],
                   wqkvD[r * 128:(r + 1) * 128, :]) for r in range(16)]
    projb = rr
    projb = awb
    h_projb = dma_g(projb[0:64, 0:1024].rearrange("d (h m) -> d h m", h=16),
                    projD.rearrange("h d m -> d h m"))
    bd = kptok
    h_bd0 = P.v(lambda v: v.memset(bd[:, 0:1024], 0.0))
    h_bdc = []
    for p in range(8):
        h_bdc.append(P.v(lambda v, p=p: v.tensor_copy(
            bd[0:64, p * 128:p * 128 + 64], projb[0:64, (2 * p) * 64:(2 * p + 1) * 64]),
            [h_bd0, h_projb]))
        h_bdc.append(P.v(lambda v, p=p: v.tensor_copy(
            bd[64:128, p * 128 + 64:p * 128 + 128],
            projb[0:64, (2 * p + 1) * 64:(2 * p + 2) * 64]), [h_bd0, h_projb]))
    h_fold = []
    prev_cp = None
    for wi in range(2):
        Wdst = Wqp if wi == 0 else Wkp
        for pr in range(8):
            for kc in range(8):
                hm = P.t(lambda t, wi=wi, pr=pr, kc=kc: t.matmul(
                    p_qkv[0][:, 0:128],
                    stg[:, (wi * 8 + pr) * 1024 + kc * 128:(wi * 8 + pr) * 1024 + (kc + 1) * 128],
                    bd[:, pr * 128:(pr + 1) * 128], start=True, stop=True),
                    [h_wqk[wi * 8 + pr], prev_cp] + (h_bdc if (wi == 0 and pr == 0 and kc == 0) else []))
                prev_cp = P.v(lambda v, Wdst=Wdst, pr=pr, kc=kc: v.tensor_copy(
                    Wdst[:, kc * 1024 + pr * 128:kc * 1024 + (pr + 1) * 128],
                    p_qkv[0][:, 0:128]), [hm])
                h_fold.append(prev_cp)
    h_bias = []
    prev_hm2 = None
    for bi in range(2):
        bdst = bqp if bi == 0 else bkp
        row0 = bi * 1024
        for pr in range(8):
            hb1 = dma_g(bstg[:, 0:1], bqkvD[row0 + pr * 128:row0 + (pr + 1) * 128, 0:1],
                        deps=([h_bias[-1], prev_hm2] if h_bias else []))
            hm = P.t(lambda t, pr=pr: t.matmul(p_qkv[1][0:1, 0:128], bstg[:, 0:1],
                                               bd[:, pr * 128:(pr + 1) * 128],
                                               start=True, stop=True),
                     [hb1, prev_hm2] + h_bdc)
            hv = P.v(lambda v: v.tensor_copy(brow[0:1, :], p_qkv[1][0:1, 0:128]),
                     [hm] + ([h_bias[-1]] if h_bias else []))
            hm2 = P.t(lambda t: t.matmul(p_qkv[1][:, 128:129], brow[0:1, :],
                                         idf128[0:1, 0:1], start=True, stop=True),
                      [hv, ld["idf128"]])
            prev_hm2 = hm2
            h_bias.append(P.v(lambda v, bdst=bdst, pr=pr: v.tensor_copy(
                bdst[:, pr:pr + 1], p_qkv[1][:, 128:129]), [hm2]))
    fold_done = h_fold + h_bias
    last_bias_mm = prev_hm2

    h_WvT, h_WoutT = [], []
    for kc in range(8):
        h_WvT.append(dma_y(WvT[:, kc * 1024:(kc + 1) * 1024],
                           wvbfD[:, kc * 128:(kc + 1) * 128],
                           deps=[h_wvb] + fold_done, tr=True))
        h_WoutT.append(dma_y(WoutT[:, kc * 1024:(kc + 1) * 1024],
                             woutbfD[:, kc * 128:(kc + 1) * 128],
                             deps=[h_wob] + fold_done, tr=True))

    h_rb = dma_g(brow[0:1, 0:1], rbD[0:1, 0:1], deps=[last_bias_mm, h_bias[-1]])
    h_rbm = P.t(lambda t: t.matmul(p_qkv[1][:, 129:130], onesrow[0:1, :],
                                   brow[0:1, 0:1], start=True, stop=True),
                [h_rb, ld["onesrow"]] + h_bias)
    h_rbc = P.v(lambda v: v.tensor_copy(lodc[:, 1:2], p_qkv[1][:, 129:130]), [h_rbm])

    vcol = lambda cc, hd: cc * 1040 + hd * 65
    urr = SCRATCH4[:, 0:1024]
    h_uns = []
    prev_uh = []
    h_x1_done = []
    h_router = []
    bank_last = [h_fold[-1], h_rbc]
    aw_last = [None, None]
    awb_readers = [[h_bdc[-1], h_projb], [h_bdc[-1], h_projb]]
    prev_div = []
    prev_hst = None
    prev_sq = None
    prev_m7 = None
    prev_hr = [None, None]
    first_in_kv = [True, True]
    statez_done = [h_sz]
    prev_pass = []
    prev_ap = None

    for pz in range(NPASS):
        t0 = pz * PASS_T
        h_vones = P.v(lambda v: v.memset(
            vst[:, :].rearrange("p (cc h x) -> p cc h x", cc=PCH, h=H)[:, :, :, 64:65],
            1.0), prev_pass)
        hxt = [dma_y(xT[:, kc * 512:(kc + 1) * 512],
                     xbfD[t0:t0 + 512, kc * 128:(kc + 1) * 128],
                     deps=[h_xbf] + prev_pass, tr=True) for kc in range(8)]
        qkv_done = []
        kp_evacs = []
        kp_accs = []
        pq = 0
        for wi in range(2):
            W, bias, dst = (Wqp, bqp, qpT) if wi == 0 else (Wkp, bkp, kpT)
            for mi in range(8):
                hmm = None
                for kc in range(8):
                    hmm = P.t(lambda t, W=W, mi=mi, kc=kc, pq=pq: t.matmul(
                        p_qkv[pq % 2][:, :],
                        W[:, kc * 1024 + mi * 128:kc * 1024 + (mi + 1) * 128],
                        xT[:, kc * 512:(kc + 1) * 512],
                        start=(kc == 0), stop=(kc == 7)),
                        (hxt + fold_done + prev_pass + [bank_last[pq % 2]])
                        if kc == 0 else [])
                hev = P.v(lambda v, dst=dst, bias=bias, mi=mi, pq=pq: v.tensor_scalar(
                    dst[:, mi * PASS_T:(mi + 1) * PASS_T], p_qkv[pq % 2][:, :],
                    bias[:, mi:mi + 1], 0.0, op0=ALU.add, op1=ALU.max),
                    [hmm] + h_bias + prev_pass)
                qkv_done.append(hev)
                bank_last[pq % 2] = hev
                if wi == 1:
                    kp_evacs.append(hev)
                    for j in range(PCH):
                        hacc = P.s(lambda s, dst=dst, mi=mi, j=j: s.activation(
                            trashb[:, 0:128],
                            dst[:, mi * PASS_T + j * 128:mi * PASS_T + (j + 1) * 128],
                            AF.Copy, bias=0.0, scale=1.0,
                            accum_out=kspart[:, mi * PCH + j:mi * PCH + j + 1]),
                            [hev] + prev_pass + ([prev_sq] if prev_sq else []))
                        prev_sq = hacc
                        kp_accs.append(hacc)
                        qkv_done.append(hacc)
                pq += 1
        for tt_ in range(PCH):
            for halfn in range(2):
                hmm = None
                for kc in range(8):
                    hmm = P.t(lambda t, tt_=tt_, halfn=halfn, kc=kc, pq=pq: t.matmul(
                        p_qkv[pq % 2][:, :],
                        xT[:, kc * 512 + tt_ * 128:kc * 512 + (tt_ + 1) * 128],
                        WvT[:, kc * 1024 + halfn * 512:kc * 1024 + (halfn + 1) * 512],
                        start=(kc == 0), stop=(kc == 7)),
                        (hxt + h_WvT + [bank_last[pq % 2]]) if kc == 0 else [])
                hev = P.v(lambda v, tt_=tt_, halfn=halfn, pq=pq: v.scalar_tensor_tensor(
                    vst[:, :].rearrange("p (cc h x) -> p cc h x", cc=PCH, h=H)[
                        :, tt_, halfn * 8:(halfn + 1) * 8, 0:64],
                    p_qkv[pq % 2][:, :].rearrange("p (h x) -> p h x", h=8),
                    1.0,
                    bv_rep[:, halfn * 512:(halfn + 1) * 512].rearrange("p (h x) -> p h x", h=8),
                    op0=ALU.mult, op1=ALU.add), [hmm, h_vones, ld["bv"]])
                qkv_done.append(hev)
                bank_last[pq % 2] = hev
                pq += 1
        h_kpst = []
        for mt in range(8):
            dst = kpxD[mt * 256:(mt + 1) * 256, t0:t0 + PASS_T].rearrange(
                "(two sub) t -> two sub t", sub=128)[:, 0:64, :]
            src = kpT[:, mt * PASS_T:(mt + 1) * PASS_T].rearrange(
                "p (two t) -> p two t", two=1)[:, 0, :].rearrange(
                "p t -> p t")
            h_kpst.append(dma_g(dst, kpT[:, mt * PASS_T:(mt + 1) * PASS_T].rearrange(
                "p t -> p t"), deps=kp_evacs))

        pass_consumers = []
        for cc in range(PCH):
            ch = pz * PCH + cc
            ct0 = cc * 128
            hkptok = dma_y(kptok[:, :], kpxD[:, t0 + ct0:t0 + ct0 + 128],
                           deps=h_kpst + h_kpz + [statez_done[-1]] +
                           ([h_rbc, h_rbm] if ch == 0 else []), tr=True)
            h_ksu = P.v(lambda v, cc=cc: v.tensor_tensor(
                ksrun[:, :], ksrun[:, :], kspart[:, cc::PCH], op=ALU.add),
                kp_accs + [statez_done[-1]] + ([h_ks0] if ch == 0 else []))
            h_div = []
            statez_prev = list(statez_done)
            h_stm = []
            for rnd in range(2):
                h0 = rnd * 8
                h_awm = []
                for k in range(8):
                    hd = h0 + k
                    mt, ro = hd // 2, (hd % 2) * 64
                    bankp = p_awA if (k % 2 == 0) else p_awB
                    slot = (k // 2) * 128
                    h_awm.append(P.t(lambda t, mt=mt, ro=ro, bankp=bankp, slot=slot,
                                     ct0=ct0: t.matmul(
                        bankp[:, slot:slot + 128],
                        kpT[ro:ro + 64, mt * PASS_T + ct0:mt * PASS_T + ct0 + 128],
                        qpT[ro:ro + 64, mt * PASS_T + ct0:mt * PASS_T + ct0 + 128],
                        start=True, stop=True),
                        (qkv_done if (k == 0 and rnd == 0) else []) +
                        ([aw_last[k % 2]] if k < 2 else [])))
                h_awe = []
                for bi, bankp in enumerate((p_awA, p_awB)):
                    h_awe.append(P.v(lambda v, bankp=bankp, h0=h0, bi=bi: v.tensor_tensor(
                        awb[:, :].rearrange("p (b x) -> p b x", x=128)[
                            :, h0 + bi:h0 + 8:2, :],
                        bankp[:, :].rearrange("p (b x) -> p b x", x=128),
                        awmask[:, :].rearrange("p (b x) -> p b x", x=128),
                        op=ALU.mult), h_awm + [ld["awmask"]] + awb_readers[rnd]))
                for bi in range(2):
                    aw_last[bi] = h_awe[bi]
                h_ctxm = []
                for k in range(8):
                    hd = h0 + k
                    pcx = p_ctxA if k < 4 else p_ctxB
                    slot = (k % 4) * 65
                    mt = hd // 2
                    h_in = P.t(lambda t, hd=hd, cc=cc, pcx=pcx, slot=slot: t.matmul(
                        pcx[:, slot:slot + 65],
                        awb[:, hd * 128:(hd + 1) * 128],
                        vst[:, vcol(cc, hd):vcol(cc, hd) + 65],
                        start=True, stop=False),
                        h_awe + (h_div[-2:] if h_div else prev_div))
                    h_hi = P.t(lambda t, mt=mt, ct0=ct0, hd=hd, pcx=pcx, slot=slot:
                               t.matmul(
                        pcx[:, slot:slot + 65],
                        qpT[:, mt * PASS_T + ct0:mt * PASS_T + ct0 + 128],
                        state_z[:, hd * 65:(hd + 1) * 65],
                        start=False, stop=True), [h_in] + statez_prev)
                    h_ctxm.append(h_hi)
                awb_readers[rnd] = [h_ctxm[-1]]
                h_rec1 = P.v(lambda v, rnd=rnd: v.tensor_scalar(
                    recn[:, rnd * 8:rnd * 8 + 4], p_ctxA[:, 64:260:65],
                    float(DELTA), None, op0=ALU.add), h_ctxm + prev_div)
                h_rec2 = P.v(lambda v, rnd=rnd: v.tensor_scalar(
                    recn[:, rnd * 8 + 4:rnd * 8 + 8], p_ctxB[:, 64:260:65],
                    float(DELTA), None, op0=ALU.add), h_ctxm + prev_div)
                h_rec3 = P.v(lambda v, rnd=rnd: v.reciprocal(
                    recn[:, rnd * 8:rnd * 8 + 8], recn[:, rnd * 8:rnd * 8 + 8]),
                    [h_rec1, h_rec2])
                for k in range(8):
                    hd = h0 + k
                    pcx = p_ctxA if k < 4 else p_ctxB
                    slot = (k % 4) * 65
                    hdv = P.v(lambda v, hd=hd, pcx=pcx, slot=slot: v.tensor_scalar(
                        hh[:, hd * 64:(hd + 1) * 64], pcx[:, slot:slot + 64],
                        recn[:, hd:hd + 1], None, op0=ALU.mult),
                        [h_rec3, prev_hst])
                    h_div.append(hdv)
                for k in range(8):
                    hd = h0 + k
                    bank = hd // 8
                    hst = P.t(lambda t, hd=hd, cc=cc, bank=bank,
                              stf=first_in_kv[bank]: t.matmul(
                        p_kv[bank][:, (hd % 8) * 64:(hd % 8 + 1) * 64],
                        kptok[:, hd * 128 - 64 * (hd % 2):hd * 128 - 64 * (hd % 2) + 128],
                        vst[:, vcol(cc, hd):vcol(cc, hd) + 64],
                        start=stf, stop=(ch == NCH - 1 and hd % 8 == 7),
                        skip_group_check=True),
                        [hkptok] + (h_ctxm if k == 0 and rnd == 1 else []) +
                        (statez_prev if ch else []))
                    first_in_kv[bank] = False
                    h_stm.append(hst)
            h_szc = []
            for bank in range(2):
                h_szc.append(P.v(lambda v, bank=bank: v.tensor_copy(
                    state_z[:, :].rearrange("p (hd x) -> p hd x", x=65)[
                        :, bank * 8:(bank + 1) * 8, 0:64],
                    p_kv[bank][:, :].rearrange("p (hd x) -> p hd x", x=64)),
                    h_stm + statez_prev))
            h_szc.append(P.v(lambda v: v.tensor_copy(
                state_z[0:64, :].rearrange("p (hd x) -> p hd x", x=65)[:, 0::2, 64:65],
                ksrun[0:64, :].unsqueeze(2)), h_stm + [h_ksu]))
            h_szc.append(P.v(lambda v: v.tensor_copy(
                state_z[64:128, :].rearrange("p (hd x) -> p hd x", x=65)[:, 1::2, 64:65],
                ksrun[64:128, :].unsqueeze(2)), h_stm + [h_ksu]))
            statez_done = h_szc
            h_hst = dma_g(hD[t0 + ct0:t0 + ct0 + 128, :], hh[:, :], deps=h_div)
            prev_div = h_div[-2:]
            prev_hst = h_hst
            pass_consumers.append(h_hst)
            pass_consumers.extend(h_stm)
            pass_consumers.append(h_ksu)

        hT = xT
        h_hT = [dma_y(hT[:, kc * 512:(kc + 1) * 512],
                      hD[t0:t0 + 512, kc * 128:(kc + 1) * 128],
                      deps=pass_consumers + qkv_done, tr=True) for kc in range(8)]
        for tt_ in range(PCH):
            ct0 = tt_ * 128
            h_x = dma_g(rr[:, :], xD[t0 + ct0:t0 + ct0 + 128, :],
                        deps=[d for d in (prev_ap, prev_sq) if d])
            h_wo = []
            for halfn in range(2):
                hmm = None
                for kc in range(8):
                    hmm = P.t(lambda t, tt_=tt_, halfn=halfn, kc=kc: t.matmul(
                        p_qkv[halfn][:, :],
                        hT[:, kc * 512 + tt_ * 128:kc * 512 + (tt_ + 1) * 128],
                        WoutT[:, kc * 1024 + halfn * 512:kc * 1024 + (halfn + 1) * 512],
                        start=(kc == 0), stop=(kc == 7)),
                        (h_hT + h_WoutT + h_stm +
                         [d for d in (prev_hr[halfn], bank_last[halfn]) if d])
                        if kc == 0 else [])
                h_wo.append(hmm)
            h_xb = P.v(lambda v: v.tensor_tensor(rr[:, :], rr[:, :], bout_rep[:, :],
                                                 op=ALU.add), [h_x, ld["bout"]])
            h_r = []
            for halfn in range(2):
                h_r.append(P.v(lambda v, halfn=halfn: v.scalar_tensor_tensor(
                    rr[:, halfn * 512:(halfn + 1) * 512], p_qkv[halfn][:, :], 1.0,
                    rr[:, halfn * 512:(halfn + 1) * 512], op0=ALU.mult, op1=ALU.add,
                    accum_out=stat[:, halfn:halfn + 1]), [h_xb, h_wo[halfn]]))
            prev_hr = list(h_r)
            for halfn in range(2):
                bank_last[halfn] = h_r[halfn]
            h_sq = P.s(lambda s: s.activation(trashb[:, :], rr[:, :], AF.Square,
                                              accum_out=stat[:, 2:3]),
                       h_r + [d for d in (prev_sq,) if d])
            prev_sq = h_sq
            h_m2 = P.v(lambda v: v.tensor_tensor(stat[:, 0:1], stat[:, 0:1],
                                                 stat[:, 1:2], op=ALU.add), h_r)
            h_m2b = P.v(lambda v: v.tensor_scalar(stat[:, 0:1], stat[:, 0:1],
                                                  1.0 / 1024.0, None, op0=ALU.mult),
                        [h_m2])
            h_m3 = P.v(lambda v: v.tensor_tensor(stat[:, 3:4], stat[:, 0:1],
                                                 stat[:, 0:1], op=ALU.mult), [h_m2b])
            h_m4 = P.v(lambda v: v.tensor_scalar(stat[:, 2:3], stat[:, 2:3],
                                                 1.0 / 1024.0, None, op0=ALU.mult),
                       [h_sq])
            h_m5 = P.v(lambda v: v.tensor_tensor(stat[:, 3:4], stat[:, 2:3],
                                                 stat[:, 3:4], op=ALU.subtract),
                       [h_m3, h_m4])
            h_m6 = P.v(lambda v: v.tensor_scalar(stat[:, 3:4], stat[:, 3:4], 1e-5,
                                                 None, op0=ALU.add), [h_m5])
            h_m7 = P.s(lambda s: s.activation(stat[:, 4:5], stat[:, 3:4], AF.Sqrt),
                       [h_m6] + [d for d in (prev_m7,) if d])
            prev_m7 = h_m7
            h_m8 = P.v(lambda v: v.reciprocal(stat[:, 4:5], stat[:, 4:5]), [h_m7])
            h_ap = P.v(lambda v: v.tensor_scalar(x1b[:, :], rr[:, :], stat[:, 0:1],
                                                 stat[:, 4:5], op0=ALU.subtract,
                                                 op1=ALU.mult),
                       [h_m8] + (h_x1_done[-1:] if h_x1_done else []) +
                       (h_router[-1:] if h_router else []))
            prev_ap = h_ap
            a_glob = pz * PCH + tt_
            h_rt = P.v(lambda v, a_glob=a_glob: v.scalar_tensor_tensor(
                trashv[:, :], x1b[:, :], 1.0, rw_rep[:, :], op0=ALU.mult,
                op1=ALU.mult, accum_out=wacc[:, a_glob:a_glob + 1]),
                [h_ap, ld["rw"]])
            h_router.append(h_rt)
            h_x1 = dma_g(x1D[t0 + ct0:t0 + ct0 + 128, :], x1b[:, :], deps=[h_ap])
            h_x1_done.append(h_x1)
            h_u1 = P.s(lambda s: s.activation(urr[:, :], x1b[:, :], AF.Copy,
                                              bias=0.0, scale=2.0,
                                              accum_out=ustat[:, 0:1]),
                       [h_ap, prev_m7] + prev_uh + h_uns[-1:])
            h_u2 = P.s(lambda s: s.activation(trashb[:, :], urr[:, :], AF.Square,
                                              accum_out=ustat[:, 2:3]),
                       [h_u1] + ([prev_sq] if prev_sq else []))
            prev_sq = h_u2
            h_u3 = P.v(lambda v: v.tensor_scalar(ustat[:, 0:1], ustat[:, 0:1],
                                                 1.0 / 1024.0, None, op0=ALU.mult),
                       [h_u1])
            h_u4 = P.v(lambda v: v.tensor_tensor(ustat[:, 3:4], ustat[:, 0:1],
                                                 ustat[:, 0:1], op=ALU.mult), [h_u3])
            h_u5 = P.v(lambda v: v.tensor_scalar(ustat[:, 2:3], ustat[:, 2:3],
                                                 1.0 / 1024.0, None, op0=ALU.mult),
                       [h_u2])
            h_u6 = P.v(lambda v: v.tensor_tensor(ustat[:, 3:4], ustat[:, 2:3],
                                                 ustat[:, 3:4], op=ALU.subtract),
                       [h_u4, h_u5])
            h_u7 = P.v(lambda v: v.tensor_scalar(ustat[:, 3:4], ustat[:, 3:4], 1e-5,
                                                 None, op0=ALU.add), [h_u6])
            h_u8 = P.s(lambda s: s.activation(ustat[:, 4:5], ustat[:, 3:4], AF.Sqrt),
                       [h_u7, prev_sq])
            prev_sq = h_u8
            h_u9 = P.v(lambda v: v.reciprocal(ustat[:, 4:5], ustat[:, 4:5]), [h_u8])
            h_u10 = P.v(lambda v: v.tensor_scalar(urr[:, :], urr[:, :], ustat[:, 0:1],
                                                  ustat[:, 4:5], op0=ALU.subtract,
                                                  op1=ALU.mult), [h_u9])
            h_ud = dma_s(outD[t0 + ct0:t0 + ct0 + 128, :], urr[:, :],
                         deps=[h_u10])
            h_uns.append(h_ud)
            prev_uh = [h_u10]
            pass_consumers = pass_consumers + [h_x1, h_rt, h_ap]
        prev_pass = pass_consumers

    h_wb = P.v(lambda v: v.tensor_scalar(wacc[:, :], wacc[:, :], lodc[:, 1:2], None,
                                         op0=ALU.add), [h_rbc] + h_router)
    h_ww = dma_g(wwD[:, :], wacc[:, :], deps=[h_wb])

    if debug_stage == 1:
        hdbg = [P.gd(lambda g, a=a: g.dma_start(
            out=outD[a * 128:(a + 1) * 128, :], in_=x1D[a * 128:(a + 1) * 128, :]),
            h_x1_done) for a in range(32)]
        P.gd(lambda g: g.dma_start(out=outD[T:T + 1, 0:32], in_=wacc[0:1, :]), [h_wb])
        with nc.Block() as block, ExitStack() as sstack:
            P.emit(nc, block, sstack)
        ctx.close()
        return nc

    msel = vst[:, 0:2048]
    m1T = vst[:, 2048:3072]
    oT = xT
    hn2T = qpT
    scs_bf = None
    _h1small = None

    def h1slot(ft):
        small = [(sinb, 0), (sinb, 512), (cosb, 0), (cosb, 512),
                 (aT, 0), (aT, 512), (x1b, 0), (x1b, 512),
                 (arow, 0), (trashv, 0), (trashv, 512),
                 (scs_bfv, 0), (scs_bfv, 512),
                 (selrep_bfv, 0), (selrep_bfv, 512), (po, 0)]
        if ft < 16:
            tns, base = small[ft]
            return tns[:, base:base + 512]
        tns = (kTb, kpT)[(ft - 16) // 8]
        return tns[:, ((ft - 16) % 8) * 512:((ft - 16) % 8 + 1) * 512]

    def xsel_t(c):
        return (awb, kptok)[c // 2][:, (c % 2) * 1024:(c % 2) * 1024 + 1024]
    h_wrep = dma_g(wrepx[:, :], wwD[0:1, :].partition_broadcast(128),
                   deps=[h_ww] + h_uns[-1:] + prev_uh)
    h_l1 = P.v(lambda v: v.tensor_reduce(cnt2[:, 0:1], wacc[:, :], axis=AX.X, op=ALU.min), [h_wb])
    h_l2 = P.v(lambda v: v.tensor_reduce(cnt2[:, 1:2], wacc[:, :], axis=AX.X, op=ALU.max), [h_wb, h_l1])
    h_t1 = P.t(lambda t: t.transpose(p_qkv[0][0:1, 0:128], cnt2[:, 0:1], idf128[:, :]),
               [h_l1, bank_last[0], ld["idf128"]])
    h_t2 = P.t(lambda t: t.transpose(p_qkv[1][0:1, 0:128], cnt2[:, 1:2], idf128[:, :]),
               [h_l2, bank_last[1]])
    bank_last[0], bank_last[1] = h_t1, h_t2
    h_c1 = P.v(lambda v: v.tensor_copy(tpc[0:1, :], p_qkv[0][0:1, 0:128]), [h_t1])
    h_c2 = P.v(lambda v: v.tensor_copy(tpp[0:1, :], p_qkv[1][0:1, 0:128]), [h_t2])
    h_s1 = P.v(lambda v: v.tensor_reduce(st8[0:1, 0:1], tpc[0:1, :], axis=AX.X, op=ALU.min), [h_c1])
    h_s2 = P.v(lambda v: v.tensor_reduce(st8[0:1, 1:2], tpp[0:1, :], axis=AX.X, op=ALU.max), [h_c2])
    h_s3 = P.v(lambda v: v.tensor_scalar(st8[0:1, 0:1], st8[0:1, 0:1], 0.001, None,
                                         op0=ALU.subtract), [h_s1])
    h_s4 = P.v(lambda v: v.memset(st8[0:1, 3:4], -1e30), [h_s2])
    bl = [h_s3, h_s4]
    h_cm_prev = None
    for it in range(NITER):
        h_d1 = P.v(lambda v: v.tensor_tensor(st8[0:1, 2:3], st8[0:1, 1:2], st8[0:1, 0:1],
                                             op=ALU.subtract), bl)
        h_d2 = P.v(lambda v: v.tensor_scalar(st8[0:1, 2:3], st8[0:1, 2:3], 1.0 / 129.0,
                                             None, op0=ALU.mult), [h_d1])
        h_bc = P.t(lambda t: t.matmul(p_qkv[0][:, 0:2], onesrow[0:1, :], st8[0:1, 0:3:2],
                                      start=True, stop=True),
                   [h_d2, bank_last[0], ld["onesrow"]])
        bank_last[0] = h_bc
        h_lc = P.v(lambda v: v.tensor_copy(lodc[:, 0:2], p_qkv[0][:, 0:2]),
                   [h_bc] + ([h_cm_prev] if h_cm_prev else []))
        h_pr = P.v(lambda v: v.scalar_tensor_tensor(probes[:, 0:1], icol[:, 0:1],
                                                    lodc[:, 1:2], lodc[:, 0:1],
                                                    op0=ALU.mult, op1=ALU.add),
                   [h_lc, ld["icol"]])
        h_cm = P.v(lambda v: v.tensor_scalar(cmpt[:, :], wrepx[:, :], probes[:, 0:1],
                                             None, op0=ALU.is_gt, op1=ALU.add,
                                             accum_out=cnt2[:, 0:1]),
                   [h_pr, h_wrep, h_l2])
        h_cm_prev = h_cm
        h_t3 = P.t(lambda t: t.transpose(p_qkv[0][0:1, 128:256], cnt2[:, 0:1],
                                         idf128[:, :]), [h_cm, bank_last[0]])
        h_t4 = P.t(lambda t: t.transpose(p_qkv[1][0:1, 0:128], probes[:, 0:1],
                                         idf128[:, :]), [h_pr, bank_last[1]])
        bank_last[0], bank_last[1] = h_t3, h_t4
        h_c3 = P.v(lambda v: v.tensor_copy(tpc[0:1, :], p_qkv[0][0:1, 128:256]),
                   [h_t3, h_s1])
        h_c4 = P.v(lambda v: v.tensor_copy(tpp[0:1, :], p_qkv[1][0:1, 0:128]),
                   [h_t4, h_s2])
        r1mf = r1m[0:1, :].bitcast(F32)

        def masked_red(cmp_op, thr_v, big_sign, red_op, dst_sl, deps):
            h1 = P.v(lambda v: v.tensor_scalar(brow[0:1, :], tpc[0:1, :], thr_v,
                                               None, op0=cmp_op), deps)
            h2 = P.v(lambda v: v.tensor_tensor(r1[0:1, :], brow[0:1, :], tpp[0:1, :],
                                               op=ALU.mult), [h1])
            h3 = P.v(lambda v: v.tensor_scalar(r1mf, brow[0:1, :],
                                               big_sign * 1e30, -big_sign * 1e30,
                                               op0=ALU.mult, op1=ALU.add), [h2])
            h4 = P.v(lambda v: v.tensor_tensor(r1[0:1, :], r1[0:1, :], r1mf,
                                               op=ALU.add), [h3])
            h5 = P.v(lambda v, dst_sl=dst_sl: v.tensor_reduce(
                st8[0:1, dst_sl:dst_sl + 1], r1[0:1, :], axis=AX.X, op=red_op), [h4])
            return h5

        h_e3 = masked_red(ALU.is_equal, float(KSEL), 1.0, ALU.max, 4,
                          [h_c3, h_c4, h_rb, h_rbm])
        h_e4 = P.v(lambda v: v.tensor_tensor(st8[0:1, 3:4], st8[0:1, 3:4],
                                             st8[0:1, 4:5], op=ALU.max), [h_e3])
        h_f3 = masked_red(ALU.is_ge, float(KSEL + 1), 1.0, ALU.max, 5, [h_e3])
        h_f4 = P.v(lambda v: v.tensor_tensor(st8[0:1, 0:1], st8[0:1, 0:1],
                                             st8[0:1, 5:6], op=ALU.max), [h_f3])
        h_g3 = masked_red(ALU.is_le, float(KSEL), -1.0, ALU.min, 6, [h_f3, h_f4])
        h_g4 = P.v(lambda v: v.tensor_tensor(st8[0:1, 1:2], st8[0:1, 1:2],
                                             st8[0:1, 6:7], op=ALU.min), [h_g3])
        bl = [h_e4, h_f4, h_g4]
    h_tb = P.t(lambda t: t.matmul(p_qkv[0][:, 2:3], onesrow[0:1, :], st8[0:1, 3:4],
                                  start=True, stop=True), bl + [bank_last[0]])
    bank_last[0] = h_tb
    h_tc = P.v(lambda v: v.tensor_copy(lodc[:, 0:1], p_qkv[0][:, 2:3]), [h_tb, h_g3])
    h_mm1 = P.v(lambda v: v.tensor_scalar(m01[:, :], wacc[:, :], lodc[:, 0:1], None,
                                          op0=ALU.is_gt), [h_tc, h_wb])
    h_sc1 = P.v(lambda v: v.tensor_tensor_scan(scan_t[:, :], m01[:, :], onesm[:, :],
                                               0.0, op0=ALU.add, op1=ALU.mult),
                [h_mm1, ld["onesm"]])
    h_rs1 = P.v(lambda v: v.tensor_copy(po[:, 0:1], cnt2[:, 0:1]), [h_cm_prev, h_c3])
    h_rs2 = P.v(lambda v: v.tensor_reduce(cnt2[:, 0:1], m01[:, :], axis=AX.X,
                                          op=ALU.add), [h_mm1, h_rs1])
    h_rs3 = P.v(lambda v: v.tensor_copy(po[:, 0:1], cnt2[:, 0:1]), [h_rs2])
    h_tri = P.t(lambda t: t.matmul(p_qkv[0][:, 3:4], tri[:, :], po[:, 0:1],
                                   start=True, stop=True),
                [h_rs3, ld["tri"], bank_last[0]])
    bank_last[0] = h_tri
    h_of1 = P.v(lambda v: v.tensor_copy(offs[:, 0:1], p_qkv[0][:, 3:4]), [h_tri])
    h_or1 = P.v(lambda v: v.scalar_tensor_tensor(ordp[:, :], scan_t[:, :], offs[:, 0:1],
                                                 m01[:, :], op0=ALU.add,
                                                 op1=ALU.subtract), [h_sc1, h_of1])
    h_or2 = P.v(lambda v: v.tensor_tensor(ordp[:, :], ordp[:, :], m01[:, :],
                                          op=ALU.mult), [h_or1])
    h_or3 = P.v(lambda v: v.tensor_scalar(tmp32[:, :], m01[:, :], -511.0, 511.0,
                                          op0=ALU.mult, op1=ALU.add), [h_or1])
    h_or4 = P.v(lambda v: v.tensor_tensor(ordp[:, :], ordp[:, :], tmp32[:, :],
                                          op=ALU.add), [h_or2, h_or3])
    h_p1 = P.v(lambda v: v.tensor_scalar(posi2[:, :], pos_s[:, :], 63, None,
                                         op0=ALU.bitwise_and), [ld["pos"]])
    h_p2 = P.v(lambda v: v.tensor_copy(tmp32[:, :], posi2[:, :]), [h_p1, h_or4])
    h_p3 = P.v(lambda v: v.tensor_scalar(posi2[:, :], pos_s[:, :], 6, None,
                                         op0=ALU.arith_shift_right), [h_p2])
    h_p4 = P.v(lambda v: v.tensor_copy(posf[:, :], posi2[:, :]), [h_p3])
    h_me = []
    h_me.append(P.v(lambda v: v.tensor_copy(meta[:, 0:192:6], thi[:, :]), [ld["thi"]]))
    h_me.append(P.v(lambda v: v.tensor_copy(meta[:, 1:192:6], tlo[:, :]), [ld["tlo"]]))
    h_me.append(P.v(lambda v: v.tensor_copy(meta[:, 2:192:6], posf[:, :]), [h_p4]))
    h_me.append(P.v(lambda v: v.tensor_copy(meta[:, 3:192:6], tmp32[:, :]), [h_p2, h_or4]))
    h_me.append(P.v(lambda v: v.tensor_copy(meta[:, 4:192:6], wacc[:, :]), [h_wb]))
    h_po_prev = None
    h_mm_prev = None
    for a in range(32):
        h_poa = P.v(lambda v, a=a: v.tensor_scalar(po[:, :], iota512[:, :],
                                                   ordp[:, a:a + 1], None,
                                                   op0=ALU.is_equal),
                    [h_or4, ld["iota512"]] + ([h_mm_prev] if h_mm_prev else [h_rs3, h_tri]))
        h_mm_prev = P.t(lambda t, a=a: t.matmul(p_qkv[1][0:5, 0:512],
                                                meta[:, 6 * a:6 * a + 5], po[:, :],
                                                start=(a == 0), stop=(a == 31),
                                                skip_group_check=True),
                        [h_poa, bank_last[1]] + h_me)
        h_po_prev = h_poa
    bank_last[1] = h_mm_prev
    h_s8 = P.v(lambda v: v.tensor_copy(sel8[0:5, :], p_qkv[1][0:5, 0:512]), [h_mm_prev])
    h_s8d = dma_g(sel8D[0:5, :], sel8[0:5, :], deps=[h_s8])
    h_r1d = dma_g(selrow[0:1, :], sel8D[0:1, :], deps=[h_s8d])
    h_r2d = dma_g(selrow2[0:1, :], sel8D[1:2, :], deps=[h_s8d])
    h_r3d = dma_g(posrow[0:1, :], sel8D[2:3, :], deps=[h_s8d])
    h_r4d = dma_g(selrow3[0:1, :], sel8D[3:4, :], deps=[h_s8d])
    h_r5d = dma_g(wrowt[0:1, :], sel8D[4:5, :], deps=[h_s8d])
    h_ri = P.v(lambda v: v.tensor_scalar(selrow[0:1, :], selrow[0:1, :], 64.0, None,
                                         op0=ALU.mult), [h_r1d])
    h_ri2 = P.v(lambda v: v.tensor_tensor(selrow[0:1, :], selrow[0:1, :],
                                          selrow2[0:1, :], op=ALU.add), [h_ri, h_r2d])
    h_sca = P.v(lambda v: v.tensor_scalar(scatrow[0:1, :], selrow[0:1, :],
                                          float(T), None, op0=ALU.min), [h_ri2])
    h_gat = P.v(lambda v: v.tensor_scalar(selrow[0:1, :], selrow[0:1, :],
                                          float(T - 1), None, op0=ALU.min), [h_sca])
    h_pi = P.v(lambda v: v.tensor_scalar(posrow[0:1, :], posrow[0:1, :], 64.0, None,
                                         op0=ALU.mult), [h_r3d])
    h_pi2 = P.v(lambda v: v.tensor_tensor(posrow[0:1, :], posrow[0:1, :],
                                          selrow3[0:1, :], op=ALU.add), [h_pi, h_r4d])
    h_rwd = dma_g(rowsD[0:1, :], selrow[0:1, :], deps=[h_gat])
    h_rwd2 = dma_g(rowsD[1:2, :], scatrow[0:1, :], deps=[h_sca])
    h_rwd3 = dma_g(rowsD[2:3, :], wrowt[0:1, :], deps=[h_r5d])
    h_cl1 = dma_g(selcol[:, 0:4], rowsD[0:1, :].rearrange("o (c p) -> (o p) c", p=128), allow_slow_non_contiguous=True,
                  deps=[h_rwd])
    h_cl2 = dma_g(wselc[:, 0:4], rowsD[2:3, :].rearrange("o (c p) -> (o p) c", p=128), allow_slow_non_contiguous=True,
                  deps=[h_rwd3])
    h_cl3 = dma_g(lodc[:, 0:2], rowsD[1:2, 0:256].rearrange("o (c p) -> (o p) c", p=128), allow_slow_non_contiguous=True,
                  deps=[h_rwd2, h_lc, h_tc, h_mm1])
    h_cl4 = dma_g(cnt2[:, 0:2], rowsD[1:2, 256:512].rearrange("o (c p) -> (o p) c", p=128), allow_slow_non_contiguous=True,
                  deps=[h_rwd2, h_rs2, h_cm_prev])
    h_ci = P.v(lambda v: v.tensor_copy(selcoli[:, :], selcol[:, :]), [h_cl1])
    h_si1 = P.v(lambda v: v.tensor_copy(scatcoli[:, 0:2], lodc[:, 0:2]), [h_cl3])
    h_si2 = P.v(lambda v: v.tensor_copy(scatcoli[:, 2:4], cnt2[:, 0:2]), [h_cl4])

    prev_u = [prev_ap, prev_sq, prev_m7]

    h_xg = []
    for c in range(4):
        h_xg.append(P.gd(lambda g, c=c: g.indirect_dma_start(
            xsel_t(c), None, x1D,
            IndirectOffsetOnAxis(selcoli[:, c:c + 1], 0)),
            [h_ci, aw_last[0], aw_last[1], statez_done[-1]] + h_x1_done + prev_div))
    h_m1 = []
    prev_mg = None
    for c in range(4):
        h_mg = P.gd(lambda g, c=c: g.indirect_dma_start(
            m1f[:, 0:4096], None, maskD,
            IndirectOffsetOnAxis(selcoli[:, c:c + 1], 0)),
            [h_ci, h_cm_prev] + ([prev_mg] if prev_mg else []))
        h_mc = P.v(lambda v, c=c: v.tensor_copy(cmpt[:, :], m1f[:, 0:4096]),
                   [h_mg] + ([h_cm_prev] if c == 0 else []) + h_m1[-1:])
        h_md = dma_g(m1D[c * 128:(c + 1) * 128, :], cmpt[:, :], deps=[h_mc])
        prev_mg = h_mc
        h_m1.append(h_md)

    wslots = {"fwq": BIGW[:, 0:8192], "fwk": BIGW[:, 8192:16384],
              "fwv": BIGW[:, 16384:24576], "fwo": BIGW[:, 24576:32768]}
    wsrc = {"fwq": fwqD, "fwk": fwkD, "fwv": fwvD, "fwo": fwoD}
    h_wffl = {}
    wdeps = list(prev_pass) + [h_mm_prev] + list(aw_last) + list(prev_div)
    for k in wslots:
        h_wffl[k] = [dma_g(wslots[k][:, kc * 1024:(kc + 1) * 1024],
                           wsrc[k][kc * 128:(kc + 1) * 128, :], deps=wdeps)
                     for kc in range(8)]

    h_msel_ev = []
    mselbanks = [p_qkv[0], p_qkv[1], p_awA, p_awB]
    mb_last = {0: bank_last[0], 1: bank_last[1], 2: aw_last[0], 3: aw_last[1]}
    h_selrep = P.t(lambda t: t.matmul(p_ctxA[:, 0:512], onesrow[0:1, :],
                                      selrow[0:1, :], start=True, stop=True),
                   [h_gat, prev_div[0] if prev_div else None, ld["onesrow"]])
    h_srv = P.v(lambda v: v.tensor_copy(selrep[:, :], p_ctxA[:, 0:512]),
                [h_selrep] + prev_div)
    h_m1T = [None, None]
    h_poc = None
    prev_ms = {0: None, 1: None, 2: None, 3: None}
    for a in range(32):
        buf = a % 2
        h_mt = dma_y(m1T[:, buf * 512:(buf + 1) * 512],
                     m1D[0:512, a * 128:(a + 1) * 128],
                     deps=h_m1 + ([h_m1T[buf]] if h_m1T[buf] else
                                  [h_mm_prev, statez_done[-1]]), tr=True)
        h_poc = P.v(lambda v, a=a: v.tensor_scalar(po[:, :], selrep[:, :],
                                                   tcol[:, a:a + 1], None,
                                                   op0=ALU.is_equal),
                    [h_srv, ld["tcol"], h_po_prev] +
                    ([h_poc] if h_poc else []) +
                    ([prev_ms[3]] if prev_ms[3] else []))
        for it_ in range(4):
            hmv = P.t(lambda t, a=a, it_=it_, buf=buf: t.matmul(
                mselbanks[it_][:, 0:512],
                m1T[:, buf * 512 + it_ * 128:buf * 512 + (it_ + 1) * 128],
                po[:, :], start=(a == 0), stop=(a == 31), skip_group_check=True),
                [h_mt, h_poc] + ([mb_last[it_]] if mb_last[it_] else []))
            mb_last[it_] = hmv
            h_m1T[buf] = hmv
        prev_ms[3] = mb_last[3]
    for it_ in range(4):
        hme = P.v(lambda v, it_=it_: v.tensor_copy(
            msel[:, it_ * 512:(it_ + 1) * 512], mselbanks[it_][:, 0:512]),
            [mb_last[it_], h_srv])
        hmz = P.v(lambda v, it_=it_: v.memset(msel[:, it_ * 512 + 511:it_ * 512 + 512],
                                              -1e30), [hme])
        h_msel_ev.append(hmz)
        mb_last[it_] = hmz

    h_hn = []
    prev_sq2 = None
    for c in range(4):
        h_q1 = P.s(lambda s, c=c: s.activation(trashb[:, :], xsel_t(c),
                                               AF.Square, accum_out=stat[:, 2:3]),
                   [h_xg[c]] + ([prev_sq2] if prev_sq2 else [prev_u[1]]) + h_uns[-1:])
        prev_sq2 = h_q1
        h_q2 = P.v(lambda v: v.tensor_scalar(stat[:, 3:4], stat[:, 2:3], 1.0 / 1024.0,
                                             None, op0=ALU.add if False else ALU.mult),
                   [h_q1] + h_uns[-1:])
        h_q3 = P.v(lambda v: v.tensor_scalar(stat[:, 3:4], stat[:, 3:4], 1e-6, None,
                                             op0=ALU.add), [h_q2])
        h_q4 = P.s(lambda s: s.activation(stat[:, 4:5], stat[:, 3:4], AF.Sqrt), [h_q3])
        h_q5 = P.v(lambda v: v.reciprocal(stat[:, 4:5], stat[:, 4:5]), [h_q4])
        h_q6 = P.v(lambda v, c=c: v.scalar_tensor_tensor(
            hnt[:, :], xsel_t(c), stat[:, 4:5],
            ln1_rep[:, :], op0=ALU.mult, op1=ALU.mult),
            [h_q5, ld["ln1"], h_xg[c]] + ([h_hn[-1]] if h_hn else prev_div))
        h_hnd = dma_g(hnD[c * 128:(c + 1) * 128, :], hnt[:, :], deps=[h_q6])
        h_hn.append(h_hnd)
    hnT = xT
    h_hnT = [dma_y(hnT[:, kc * 512:(kc + 1) * 512], hnD[0:512, kc * 128:(kc + 1) * 128],
                   deps=h_hn + h_hT, tr=True) for kc in range(8)]

    angs = m1f[:, 0:1024]
    angc = m1f[:, 1024:2048]
    angt = m1f[:, 2048:3072]
    angt2v = m1f[:, 3072:4096]
    angt3v = rr[:, 0:1024]
    posi128 = r2f[:, 0:1024].bitcast(I32)
    qkbanks = [p_ctxA, p_ctxB]
    qk_last = {0: h_srv, 1: prev_div[1] if len(prev_div) > 1 else None}
    h_qtok = {}
    prev_rope = []
    for wq_name, dstD in (("fwq", qtokD), ("fwk", ktokD)):
        for c in range(4):
            h_qm = []
            for halfn in range(2):
                hmm = None
                for kc in range(8):
                    hmm = P.t(lambda t, c=c, halfn=halfn, kc=kc, wq_name=wq_name:
                              t.matmul(
                        qkbanks[halfn][:, 0:512],
                        hnT[:, kc * 512 + c * 128:kc * 512 + (c + 1) * 128],
                        wslots[wq_name][:, kc * 1024 + halfn * 512:kc * 1024 + (halfn + 1) * 512],
                        start=(kc == 0), stop=(kc == 7)),
                        (h_hnT + h_wffl[wq_name] +
                         ([qk_last[halfn]] if qk_last[halfn] else []))
                        if kc == 0 else [])
                h_qm.append(hmm)
            h_a1 = P.t(lambda t, c=c: t.matmul(p_kv[0][:, 0:512],
                                               posrow[0:1, c * 128:(c + 1) * 128],
                                               invf[0:1, 0:512], start=True, stop=True),
                       [h_pi2, ld["invf"], statez_done[-1] if wq_name == "fwq" and c == 0 else None] +
                       ([prev_rope[-1]] if prev_rope else []))
            h_a2 = P.t(lambda t, c=c: t.matmul(p_kv[1][:, 0:512],
                                               posrow[0:1, c * 128:(c + 1) * 128],
                                               invf[0:1, 512:1024], start=True,
                                               stop=True),
                       [h_pi2, statez_done[-1] if wq_name == "fwq" and c == 0 else None])
            h_b1 = P.v(lambda v: v.tensor_copy(angs[:, 0:512], p_kv[0][:, 0:512]),
                       [h_a1] + prev_rope[-1:])
            h_b2 = P.v(lambda v: v.tensor_copy(angs[:, 512:1024], p_kv[1][:, 0:512]),
                       [h_a2])
            h_b3 = P.v(lambda v: v.tensor_scalar(angc[:, :], angs[:, :],
                                                 float(0.5 * np.pi), None,
                                                 op0=ALU.add), [h_b1, h_b2])
            hsin = None
            for dst_ap, hb in ((sinb, [h_b1, h_b2]), (cosb, [h_b3])):
                src_ang = angs if dst_ap is sinb else angc
                h_t1_ = P.v(lambda v, src_ang=src_ang: v.tensor_scalar(
                    angt[:, :], src_ang[:, :], float(1.0 / TWO_PI), None,
                    op0=ALU.mult), hb + ([hsin] if hsin else []))
                h_t2_ = P.v(lambda v: v.tensor_copy(posi128, angt[:, :]),
                            [h_t1_, h_srv] + prev_u)
                h_t3_ = P.v(lambda v: v.tensor_copy(angt2v[:, :], posi128), [h_t2_])
                h_t4_ = P.v(lambda v: v.tensor_tensor(angt3v[:, :], angt2v[:, :],
                                                      angt[:, :], op=ALU.is_gt),
                            [h_t3_])
                h_t5_ = P.v(lambda v: v.tensor_tensor(angt2v[:, :], angt2v[:, :],
                                                      angt3v[:, :], op=ALU.subtract),
                            [h_t4_])
                h_t6_ = P.v(lambda v, src_ang=src_ang: v.scalar_tensor_tensor(
                    angt[:, :], angt2v[:, :], float(-TWO_PI), src_ang[:, :],
                    op0=ALU.mult, op1=ALU.add), [h_t5_])
                h_t7_ = P.v(lambda v: v.tensor_scalar(angt3v[:, :], angt[:, :],
                                                      float(np.pi), None,
                                                      op0=ALU.is_gt), [h_t6_])
                h_t8_ = P.v(lambda v: v.scalar_tensor_tensor(
                    angt[:, :], angt3v[:, :], float(-TWO_PI), angt[:, :],
                    op0=ALU.mult, op1=ALU.add), [h_t7_])
                hsin = P.s(lambda s, src_ang=src_ang: s.activation(
                    src_ang[:, :], angt[:, :], AF.Sin), [h_t8_, prev_sq2])
                prev_rope.append(hsin)
            h_e1_ = P.v(lambda v: v.tensor_copy(qtok[:, 0:512], qkbanks[0][:, 0:512]),
                        [h_qm[0], hsin] + prev_u +
                        (h_qtok.get(wq_name, [])[-1:] if h_qtok.get(wq_name) else
                         (h_qtok.get("fwq", [])[-1:] if wq_name == "fwk" else [])))
            h_e2_ = P.v(lambda v: v.tensor_copy(qtok[:, 512:1024],
                                                qkbanks[1][:, 0:512]), [h_qm[1]])
            qk_last[0], qk_last[1] = h_e1_, h_e2_
            h_rc1 = P.v(lambda v: v.tensor_copy(
                rcop[:, :].rearrange("p (h a b) -> p h a b", a=2, b=32)[:, :, 0, :],
                qtok[:, :].rearrange("p (h a b) -> p h a b", a=2, b=32)[:, :, 1, :]),
                [h_e1_, h_e2_] + prev_rope[-2:])
            h_rc2 = P.v(lambda v: v.tensor_copy(
                rcop[:, :].rearrange("p (h a b) -> p h a b", a=2, b=32)[:, :, 1, :],
                qtok[:, :].rearrange("p (h a b) -> p h a b", a=2, b=32)[:, :, 0, :]),
                [h_rc1])
            h_rm1 = P.v(lambda v: v.tensor_tensor(rcop[:, :], rcop[:, :], angs[:, :],
                                                  op=ALU.mult), [h_rc2])
            h_rm2 = P.v(lambda v: v.tensor_tensor(qtok[:, :], qtok[:, :], angc[:, :],
                                                  op=ALU.mult), [h_rm1])
            h_rm3 = P.v(lambda v: v.tensor_tensor(qtok[:, :], qtok[:, :], rcop[:, :],
                                                  op=ALU.add), [h_rm2])
            prev_rope.append(h_rm3)
            h_qd = dma_g(dstD[c * 128:(c + 1) * 128, :], qtok[:, :], deps=[h_rm3])
            h_qtok.setdefault(wq_name, []).append(h_qd)
    h_qT = [dma_y(qTb[:, kc * 512:(kc + 1) * 512],
                  qtokD[0:512, kc * 128:(kc + 1) * 128],
                  deps=h_qtok["fwq"] + h_qtok["fwk"] + [prev_rope[-1]], tr=True)
            for kc in range(8)]
    h_kT = [dma_y(kTb[:, kc * 512:(kc + 1) * 512],
                  ktokD[0:512, kc * 128:(kc + 1) * 128],
                  deps=h_qtok["fwk"] + h_m1 + [h_cm_prev, prev_mg], tr=True)
            for kc in range(8)]
    h_vm = []
    for c in range(4):
        for halfn in range(2):
            hmm = None
            for kc in range(8):
                hmm = P.t(lambda t, c=c, halfn=halfn, kc=kc: t.matmul(
                    qkbanks[halfn][:, 0:512],
                    hnT[:, kc * 512 + c * 128:kc * 512 + (c + 1) * 128],
                    wslots["fwv"][:, kc * 1024 + halfn * 512:kc * 1024 + (halfn + 1) * 512],
                    start=(kc == 0), stop=(kc == 7)),
                    (h_hnT + h_wffl["fwv"] + [qk_last[halfn]]) if kc == 0 else [])
            hev = P.v(lambda v, c=c, halfn=halfn: v.tensor_copy(
                vselb[:, c * 1024 + halfn * 512:c * 1024 + (halfn + 1) * 512],
                qkbanks[halfn][:, 0:512]), [hmm])
            qk_last[halfn] = hev
            h_vm.append(hev)

    scbanks = [p_qkv[0], p_qkv[1], p_awA, p_awB]
    sc_last = {i: mb_last[i] for i in range(4)}
    obanks = [p_ctxA, p_ctxB, p_kv[0], p_kv[1]]
    ob_last = {0: qk_last[0], 1: qk_last[1], 2: None, 3: None}
    arow2 = trashv[:, 0:512]
    h_adma = {}
    h_oev = []
    aT_prev = [None, None]
    h_exp = [None, None]
    h_norm = [None, None]
    h_adl = [None, None]
    bank_i = 0
    h_scev_all = []

    def emit_o(hd):
        bk = hd % 4
        win = hd * 64 - 64 * (hd % 2)
        h_om = None
        for jc in range(4):
            buf = jc % 2
            dmaf = dma_y if jc % 2 == 0 else dma_s
            h_at = dmaf(aT[:, buf * 512:(buf + 1) * 512],
                        aDr[hd * 512:hd * 512 + 512, jc * 128:(jc + 1) * 128],
                        deps=h_adma[hd] + ([aT_prev[buf]] if aT_prev[buf] else []),
                        tr=True)
            h_om = P.t(lambda t, hd=hd, jc=jc, bk=bk, win=win, buf=buf: t.matmul(
                obanks[bk][:, 0:512],
                vselb[:, jc * 1024 + win:jc * 1024 + win + 128],
                aT[:, buf * 512:(buf + 1) * 512],
                start=(jc == 0), stop=(jc == 3), skip_group_check=True),
                [h_at] + h_vm + ([ob_last[bk]] if ob_last[bk] else []) +
                ([statez_done[-1]] if bk >= 2 else []))
            aT_prev[buf] = h_om
        ro = (hd % 2) * 64
        hoe = P.v(lambda v, hd=hd, bk=bk, ro=ro: v.tensor_copy(
            oT[ro:ro + 64, (hd // 2) * 512:(hd // 2 + 1) * 512],
            obanks[bk][ro:ro + 64, 0:512]), [h_om])
        ob_last[bk] = hoe
        h_oev.append(hoe)

    for hd in range(16):
        kc2, ro = hd // 2, (hd % 2) * 64
        for c in range(4):
            i = hd * 4 + c
            par = i % 2
            stt_b = stat if par == 0 else ustat
            rrs = rr[:, par * 512:(par + 1) * 512]
            arw = arow[:, :] if par == 0 else arow2
            bk = bank_i % 4
            bank_i += 1
            h_scm = P.t(lambda t, c=c, hd=hd, kc2=kc2, ro=ro, bk=bk: t.matmul(
                scbanks[bk][:, 0:512],
                qTb[ro:ro + 64, kc2 * 512 + c * 128:kc2 * 512 + (c + 1) * 128],
                kTb[ro:ro + 64, kc2 * 512:kc2 * 512 + 512],
                start=True, stop=True),
                h_qT + h_kT + [sc_last[bk]])
            h_s1_ = P.v(lambda v, c=c, bk=bk: v.scalar_tensor_tensor(
                scs[:, 0:512], scbanks[bk][:, 0:512], 0.125,
                msel[:, c * 512:(c + 1) * 512], op0=ALU.mult, op1=ALU.add),
                [h_scm, h_msel_ev[c]] +
                ([h_exp[1 - par]] if h_exp[1 - par] else []))
            h_s2_ = P.v(lambda v, stt_b=stt_b: v.tensor_reduce(
                stt_b[:, 6:7], scs[:, 0:512], axis=AX.X, op=ALU.max),
                [h_s1_] + ([h_exp[par]] if h_exp[par] else []))
            h_s3_ = P.v(lambda v, stt_b=stt_b: v.tensor_scalar(
                stt_b[:, 6:7], stt_b[:, 6:7], -1.0, None, op0=ALU.mult), [h_s2_])
            h_s4_ = P.s(lambda s, stt_b=stt_b, rrs=rrs: s.activation(
                rrs, scs[:, 0:512], AF.Exp, bias=stt_b[:, 6:7], scale=1.0,
                accum_out=stt_b[:, 7:8]),
                [h_s3_, prev_sq2] + ([h_norm[par]] if h_norm[par] else []) +
                (prev_u if i == 0 else []))
            h_exp[par] = h_s4_
            h_s5_ = P.v(lambda v, stt_b=stt_b: v.reciprocal(
                stt_b[:, 7:8], stt_b[:, 7:8]), [h_s4_])
            h_s6_ = P.v(lambda v, stt_b=stt_b, rrs=rrs, arw=arw: v.tensor_scalar(
                arw, rrs, stt_b[:, 7:8], None, op0=ALU.mult),
                [h_s5_] + ([h_adl[par]] if h_adl[par] else []))
            h_norm[par] = h_s6_
            h_ad = P.gd(lambda g, c=c, hd=hd, arw=arw: g.dma_start(
                out=aDr[hd * 512 + c * 128:hd * 512 + (c + 1) * 128, :],
                in_=arw), [h_s6_])
            h_adl[par] = h_ad
            h_adma.setdefault(hd, []).append(h_ad)
            h_scev_all.append(h_s6_)
            sc_last[bk] = h_s1_
        if hd >= 1:
            emit_o(hd - 1)
    emit_o(15)
    h_x2a = []
    wo_last = {0: ob_last[0], 1: ob_last[1]}
    for c in range(4):
        for halfn in range(2):
            hmm = None
            for kc in range(8):
                hmm = P.t(lambda t, c=c, halfn=halfn, kc=kc: t.matmul(
                    qkbanks[halfn][:, 0:512],
                    oT[:, kc * 512 + c * 128:kc * 512 + (c + 1) * 128],
                    wslots["fwo"][:, kc * 1024 + halfn * 512:kc * 1024 + (halfn + 1) * 512],
                    start=(kc == 0), stop=(kc == 7)),
                    (h_oev + h_wffl["fwo"] + [wo_last[halfn]]) if kc == 0 else [])
            hx2 = P.v(lambda v, c=c, halfn=halfn: v.scalar_tensor_tensor(
                x2a[:, c * 1024 + halfn * 512:c * 1024 + (halfn + 1) * 512],
                qkbanks[halfn][:, 0:512], 1.0,
                xsel_t(c)[:, halfn * 512:(halfn + 1) * 512],
                op0=ALU.mult, op1=ALU.add),
                [hmm, h_xg[c]] + ([prev_rope[-1]] if c == 0 and halfn == 0 else []))
            wo_last[halfn] = hx2
            h_x2a.append(hx2)

    h_hn2 = []
    prev_sq3 = prev_sq2
    for c in range(4):
        h_q1 = P.s(lambda s, c=c: s.activation(trashb[:, :],
                                               x2a[:, c * 1024:(c + 1) * 1024],
                                               AF.Square, accum_out=stat[:, 2:3]),
                   [h_x2a[2 * c], h_x2a[2 * c + 1], prev_sq3])
        prev_sq3 = h_q1
        h_q2 = P.v(lambda v: v.tensor_scalar(stat[:, 3:4], stat[:, 2:3], 1.0 / 1024.0,
                                             None, op0=ALU.mult), [h_q1])
        h_q3 = P.v(lambda v: v.tensor_scalar(stat[:, 3:4], stat[:, 3:4], 1e-6, None,
                                             op0=ALU.add), [h_q2])
        h_q4 = P.s(lambda s: s.activation(stat[:, 4:5], stat[:, 3:4], AF.Sqrt),
                   [h_q3, prev_sq3])
        prev_sq3 = h_q4
        h_q5 = P.v(lambda v: v.reciprocal(stat[:, 4:5], stat[:, 4:5]), [h_q4])
        h_q6 = P.v(lambda v, c=c: v.scalar_tensor_tensor(
            hnt[:, :], x2a[:, c * 1024:(c + 1) * 1024], stat[:, 4:5],
            ln2_rep[:, :], op0=ALU.mult, op1=ALU.mult),
            [h_q5, ld["ln2"]] + ([h_hn2[-1]] if h_hn2 else [h_hn[-1]]))
        h_h2d = dma_g(hn2D[c * 128:(c + 1) * 128, :], hnt[:, :], deps=[h_q6])
        h_hn2.append(h_h2d)
    h_hn2T = [dma_y(hn2T[:, kc * 512:(kc + 1) * 512],
                    hn2D[0:512, kc * 128:(kc + 1) * 128],
                    deps=h_hn2 + h_scev_all[-1:] + [h_mm_prev, prev_sq2],
                    tr=True) for kc in range(8)]
    w1deps = h_x2a + h_scev_all + h_oev
    h_w1l = [dma_g(BIGW[:, kc * 4096:(kc + 1) * 4096], w1D[kc * 128:(kc + 1) * 128, :],
                   deps=w1deps) for kc in range(8)]
    h_gel = []
    h_gcp = []
    g_last = {0: wo_last[0], 1: wo_last[1], 2: aw_last[0], 3: aw_last[1]}
    g_last = {0: wo_last[0], 1: wo_last[1], 2: sc_last[2], 3: sc_last[3]}
    h1_written = {}
    for ft in range(32):
        bk = ft % 4
        hmm = None
        for kc in range(8):
            hmm = P.t(lambda t, ft=ft, kc=kc, bk=bk: t.matmul(
                scbanks[bk][:, 0:512],
                BIGW[:, kc * 4096 + ft * 128:kc * 4096 + (ft + 1) * 128],
                hn2T[:, kc * 512:(kc + 1) * 512],
                start=(kc == 0), stop=(kc == 7)),
                (h_w1l + h_hn2T + [g_last[bk]] +
                 (h_kT if ft == 2 else []) + (h_vm if ft == 3 else []))
                if kc == 0 else [])
        gs = (ft % 2) * 512
        hge = P.s(lambda s, ft=ft, bk=bk, gs=gs: s.activation(
            rr[:, gs:gs + 512], scbanks[bk][:, 0:512],
            AF.Gelu_apprx_tanh), [hmm, prev_sq3, prev_rope[-1], h_oev[-1],
                                  h_adma[15][-1], h_adma[15][-2], h_srv,
                                  h_scev_all[-1]] +
            ([h_gel[-1]] if h_gel else []) +
            ([h_gcp[ft - 2]] if ft >= 2 else []))
        hcp = P.v(lambda v, ft=ft, gs=gs: v.tensor_copy(
            h1slot(ft), rr[:, gs:gs + 512]), [hge] + h_uns[-1:])
        g_last[bk] = hge
        h_gcp.append(hcp)
        h_gel.append(hge)
        h_gel.append(hcp)
        h1_written[ft] = hcp
    h_w2l = [dma_g(BIGW[:, fc * 1024:(fc + 1) * 1024], w2D[fc * 128:(fc + 1) * 128, :],
                   deps=h_gel) for fc in range(32)]
    h_sct = []
    w2_last = {0: g_last[0], 1: g_last[1]}
    prev_tail = list(prev_u) + [prev_rope[-1]]
    for c in range(4):
        h_ps = []
        for halfn in range(2):
            hmm = None
            for fc in range(32):
                hmm = P.t(lambda t, c=c, halfn=halfn, fc=fc: t.matmul(
                    qkbanks[halfn][:, 0:512],
                    h1slot(fc)[:, c * 128:(c + 1) * 128],
                    BIGW[:, fc * 1024 + halfn * 512:fc * 1024 + (halfn + 1) * 512],
                    start=(fc == 0), stop=(fc == 31)),
                    (h_w2l + h_gel + [w2_last[halfn]]) if fc == 0 else [])
            h_ps.append(hmm)
        h_v1 = []
        for halfn in range(2):
            h_v1.append(P.v(lambda v, c=c, halfn=halfn: v.scalar_tensor_tensor(
                rr[:, halfn * 512:(halfn + 1) * 512], qkbanks[halfn][:, 0:512], 1.0,
                x2a[:, c * 1024 + halfn * 512:c * 1024 + (halfn + 1) * 512],
                op0=ALU.mult, op1=ALU.add),
                [h_ps[halfn]] + prev_tail + ([h_x2a[-1]] if c == 0 and halfn == 0 else [])))
            w2_last[halfn] = h_v1[halfn]
        h_v2 = []
        for halfn in range(2):
            h_v2.append(P.v(lambda v, c=c, halfn=halfn: v.scalar_tensor_tensor(
                r2f[:, halfn * 512:(halfn + 1) * 512], rr[:, halfn * 512:(halfn + 1) * 512],
                wselc[:, c:c + 1],
                xsel_t(c)[:, halfn * 512:(halfn + 1) * 512],
                op0=ALU.mult, op1=ALU.add, accum_out=stat[:, halfn:halfn + 1]),
                [h_v1[halfn], h_cl2, h_xg[c]] + prev_tail))
        h_w1_ = P.s(lambda s: s.activation(trashb[:, :], r2f[:, :], AF.Square,
                                           accum_out=stat[:, 2:3]),
                    h_v2 + [prev_sq3] + ([h_gel[-1]] if h_gel else []))
        prev_sq3 = h_w1_
        h_w2_ = P.v(lambda v: v.tensor_tensor(stat[:, 0:1], stat[:, 0:1], stat[:, 1:2],
                                              op=ALU.add), h_v2)
        h_w3_ = P.v(lambda v: v.tensor_scalar(stat[:, 0:1], stat[:, 0:1], 1.0 / 1024.0,
                                              None, op0=ALU.mult), [h_w2_])
        h_w4_ = P.v(lambda v: v.tensor_tensor(stat[:, 3:4], stat[:, 0:1], stat[:, 0:1],
                                              op=ALU.mult), [h_w3_])
        h_w5_ = P.v(lambda v: v.tensor_scalar(stat[:, 2:3], stat[:, 2:3], 1.0 / 1024.0,
                                              None, op0=ALU.mult), [h_w1_])
        h_w6_ = P.v(lambda v: v.tensor_tensor(stat[:, 3:4], stat[:, 2:3], stat[:, 3:4],
                                              op=ALU.subtract), [h_w4_, h_w5_])
        h_w7_ = P.v(lambda v: v.tensor_scalar(stat[:, 3:4], stat[:, 3:4], 1e-5, None,
                                              op0=ALU.add), [h_w6_])
        h_w8_ = P.s(lambda s: s.activation(stat[:, 4:5], stat[:, 3:4], AF.Sqrt),
                    [h_w7_, prev_sq3])
        h_w9_ = P.v(lambda v: v.reciprocal(stat[:, 4:5], stat[:, 4:5]), [h_w8_])
        h_wA_ = P.v(lambda v: v.tensor_scalar(rr2[:, :], r2f[:, :], stat[:, 0:1],
                                              stat[:, 4:5], op0=ALU.subtract,
                                              op1=ALU.mult),
                    [h_w9_] + prev_tail + h_uns[-1:] + h_sct[-1:])
        h_sc_ = P.gd(lambda g, c=c: g.indirect_dma_start(
            outD, IndirectOffsetOnAxis(scatcoli[:, c:c + 1], 0), rr2[:, :], None),
            [h_wA_, h_si1, h_si2] + h_uns)
        h_sct.append(h_sc_)
        prev_tail = [h_wA_, h_w8_]
    P.g(lambda g: g.engine_nop(), h_sct)

    with nc.Block() as block, ExitStack() as sstack:
        P.emit(nc, block, sstack)
    ctx.close()
    nc._n_items = len(P.items)
    return nc


def _consts():
    t_grid = (np.arange(32)[None, :] * 128 + np.arange(128)[:, None]).astype(np.float32)
    inv = (1.0 / 10000.0 ** (np.arange(0, 64, 2, dtype=np.float64) / 64.0))
    invs = np.concatenate([-inv, inv])
    invf = np.tile(invs, 16).astype(np.float32)[None, :]
    return {
        "icol": np.arange(1, 129, dtype=np.float32).reshape(128, 1),
        "iota512": np.broadcast_to(np.arange(512, dtype=np.float32), (128, 512)).copy(),
        "thi": np.floor(t_grid / 64).astype(np.float32),
        "tlo": (t_grid % 64).astype(np.float32),
        "tcol": t_grid.copy(),
        "tri": np.tril(np.ones((128, 128)), -1).T.astype(ml_dtypes.bfloat16),
        "idf128": np.eye(128, dtype=np.float32),
        "onesrow": np.ones((1, 128), np.float32),
        "awmask": np.tile(np.triu(np.ones((128, 128), np.float32)), (1, 4)).astype(ml_dtypes.bfloat16),
        "invf": invf,
        "onesm": np.ones((128, 32), np.float32),
    }


def _stage_core(inputs, b):
    rep = lambda v: np.ascontiguousarray(np.broadcast_to(v.reshape(1, -1), (128, v.size)).astype(np.float32))
    pos = np.asarray(inputs["L0_ffn_position_ids"][b]).astype(np.int32)
    posb = np.ascontiguousarray(pos.reshape(32, 128).T)
    m = {
        "x": np.ascontiguousarray(inputs["L0_attn_x"][b], np.float32),
        "wqkv": np.ascontiguousarray(inputs["L0_attn_Wqkv"], np.float32),
        "bqkv": np.ascontiguousarray(inputs["L0_attn_bqkv"], np.float32).reshape(-1, 1),
        "proj": np.ascontiguousarray(inputs["L0_attn_proj"], np.float32),
        "wout": np.ascontiguousarray(inputs["L0_attn_Wout"], np.float32),
        "maskb": np.ascontiguousarray(inputs["L0_ffn_attention_mask"][b, 0], np.float32),
        "posb": posb,
        "rb": np.asarray(inputs["L0_ffn_router_b"], np.float32).reshape(1, 1),
        "fwq": np.ascontiguousarray(inputs["L0_ffn_wq"], np.float32),
        "fwk": np.ascontiguousarray(inputs["L0_ffn_wk"], np.float32),
        "fwv": np.ascontiguousarray(inputs["L0_ffn_wv"], np.float32),
        "fwo": np.ascontiguousarray(inputs["L0_ffn_wo"], np.float32),
        "w1": np.ascontiguousarray(inputs["L0_ffn_w1"], np.float32),
        "w2": np.ascontiguousarray(inputs["L0_ffn_w2"], np.float32),
        "bout_rep": rep(np.asarray(inputs["L0_attn_bout"])),
        "bv_rep": rep(np.asarray(inputs["L0_attn_bqkv"])[2048:]),
        "ln1_rep": rep(np.asarray(inputs["L0_ffn_ln1"])),
        "ln2_rep": rep(np.asarray(inputs["L0_ffn_ln2"])),
        "rw_rep": rep(np.asarray(inputs["L0_ffn_router_w"])[:, 0]),
    }
    m.update(_consts())
    return m


def kernel(**inputs):
    nc = build_nc(debug_stage=0)
    maps = [_stage_core(inputs, c >> 1) for c in range(8)]
    res = run_bass_kernel_spmd(nc, maps, core_ids=list(range(8)))
    out = np.stack([np.asarray(res.results[2 * b]["out"])[:T] for b in range(B)])
    return out


# version 06
# speedup vs baseline: 1.1137x; 1.0336x over previous
import numpy as np
import ml_dtypes
import concourse.bass as bass
import concourse.mybir as mybir
from concourse.bass_utils import run_bass_kernel_spmd
from concourse.bass_core_rust import IndirectOffsetOnAxis

F32 = mybir.dt.float32
BF16 = mybir.dt.bfloat16
I32 = mybir.dt.int32
U32 = mybir.dt.uint32
AF = mybir.ActivationFunctionType
ALU = mybir.AluOpType
AX = mybir.AxisListType

B, T, C = 4, 4096, 1024
H, M, CHUNK, D = 16, 64, 128, 64
NCH = T // CHUNK
KSEL = 511
FF = 4096
DELTA = 1e-6
PASS_T = 512
NPASS = T // PASS_T
PCH = PASS_T // CHUNK
NITER = 6
TWO_PI = float(2.0 * np.pi)


class Prog:

    CLASSES = ("t", "v", "s", "g", "gd", "yd", "sd")
    ENG = {"t": "tensor", "v": "vector", "s": "scalar", "g": "gpsimd",
           "gd": "gpsimd", "yd": "sync", "sd": "scalar"}
    DMA = ("gd", "yd", "sd")
    R = 16

    def __init__(self):
        self.items = []
        self.counts = {c: 0 for c in self.CLASSES}
        self.trunc = None
        self.dummyS = None
        self.dummyD = None

    def _add(self, cls, fn, deps):
        deps = tuple(d for d in deps if d is not None)
        self.counts[cls] += 1
        h = (cls, self.counts[cls])
        self.items.append((cls, fn, deps, h, len(self.items)))
        return h

    def t(self, fn, deps=()): return self._add("t", fn, deps)
    def v(self, fn, deps=()): return self._add("v", fn, deps)
    def s(self, fn, deps=()): return self._add("s", fn, deps)
    def g(self, fn, deps=()): return self._add("g", fn, deps)
    def gd(self, fn, deps=()): return self._add("gd", fn, deps)
    def yd(self, fn, deps=()): return self._add("yd", fn, deps)
    def sd(self, fn, deps=()): return self._add("sd", fn, deps)

    def _semkey(self, h):
        cls, idx = h
        if cls in self.DMA:
            return (cls, (idx - 1) % self.R), 16 * ((idx - 1) // self.R + 1)
        return (cls, 0), idx

    def emit(self, nc, block, stack):
        sems = {}
        for c in self.CLASSES:
            if c in self.DMA:
                for r in range(self.R):
                    sems[(c, r)] = stack.enter_context(nc.semaphore(f"pr_{c}{r}"))
            else:
                sems[(c, 0)] = stack.enter_context(nc.semaphore(f"pr_{c}"))
        per_eng = {}
        for it in self.items:
            per_eng.setdefault(self.ENG[it[0]], []).append(it)

        def run(eng_name, eng):
            waited = {}
            for cls, fn, deps, h, seq in per_eng.get(eng_name, []):
                if cls in self.DMA:
                    key, val = self._semkey(h)
                    prev = val - 16
                    if prev > 0 and waited.get(key, -1) < prev:
                        eng.wait_ge(sems[key], prev)
                        waited[key] = prev
                for d in deps:
                    key, val = self._semkey(d)
                    if waited.get(key, -1) >= val:
                        continue
                    eng.wait_ge(sems[key], val)
                    waited[key] = val
                key, val = self._semkey(h)
                if self.trunc is not None and seq >= self.trunc:
                    if cls in self.DMA:
                        fn = lambda e: e.dma_start(out=self.dummyS[0:1, 0:4],
                                                   in_=self.dummyD[0:1, 0:4])
                    else:
                        fn = lambda e: e.nop()
                fn(eng).then_inc(sems[key], 16 if cls in self.DMA else 1)

        @block.gpsimd
        def _(g): run("gpsimd", g)

        @block.sync
        def _(y): run("sync", y)

        @block.vector
        def _(v): run("vector", v)

        @block.tensor
        def _(t): run("tensor", t)

        @block.scalar
        def _(s): run("scalar", s)

        @block.scalar
        def _(s):
            for c in self.CLASSES:
                n = self.counts[c]
                if not n:
                    continue
                if c in self.DMA:
                    for r in range(self.R):
                        cnt = (n - 1 - r) // self.R + 1 if r < n else 0
                        if r < min(n, self.R) and cnt > 0:
                            s.wait_ge(sems[(c, r)], 16 * cnt)
                else:
                    s.wait_ge(sems[(c, 0)], n)


def build_nc(debug_stage=0, dbg=False, trunc=None):
    nc = bass.Bass("TRN2", target_bir_lowering=False, debug=dbg)
    from contextlib import ExitStack
    ctx = ExitStack()

    din = lambda n, shape, dt=F32: nc.dram_tensor(n, shape, dt, kind="ExternalInput").ap()
    dscr = lambda n, shape, dt: nc.dram_tensor(n, shape, dt).ap()
    sb = lambda n, shape, dt=F32: ctx.enter_context(nc.sbuf_tensor(n, shape, dt))
    psum = lambda n, shape: ctx.enter_context(nc.psum_tensor(n, shape, F32))

    xD = din("x", [T, C])
    wqkvD = din("wqkv", [3 * C, C])
    bqkvD = din("bqkv", [3 * C, 1])
    projD = din("proj", [H, D, M])
    woutD = din("wout", [C, C])
    maskD = din("maskb", [T, T])
    posD = din("posb", [128, 32], I32)
    rbD = din("rb", [1, 1])
    fwqD = din("fwq", [C, C])
    fwkD = din("fwk", [C, C])
    fwvD = din("fwv", [C, C])
    fwoD = din("fwo", [C, C])
    w1D = din("w1", [C, FF])
    w2D = din("w2", [FF, C])
    bout_repD = din("bout_rep", [128, C])
    bv_repD = din("bv_rep", [128, C])
    ln1_repD = din("ln1_rep", [128, C])
    ln2_repD = din("ln2_rep", [128, C])
    rw_repD = din("rw_rep", [128, C])
    icolD = din("icol", [128, 1])
    iota512D = din("iota512", [128, 512])
    thiD = din("thi", [128, 32])
    tloD = din("tlo", [128, 32])
    tcolD = din("tcol", [128, 32])
    triD = din("tri", [128, 128], BF16)
    idf128D = din("idf128", [128, 128])
    onesrowD = din("onesrow", [1, 128])
    awmaskD = din("awmask", [128, 512], BF16)
    invfD = din("invf", [1, C])
    onesmD = din("onesm", [128, 32])

    outD = nc.dram_tensor("out", [T + 1, C], F32, kind="ExternalOutput").ap()

    xbfD = dscr("xbf", [T, C], BF16)
    wvbfD = dscr("wvbf", [C, C], BF16)
    woutbfD = dscr("woutbf", [C, C], BF16)
    kpxD = dscr("kpx_d", [2 * C, T], BF16)
    hD = dscr("h_d", [T, C], BF16)
    x1D = dscr("x1_d", [T, C], BF16)
    wwD = dscr("ww_d", [1, 4096], F32)
    sel8D = dscr("sel8_d", [8, 512], F32)
    rowsD = dscr("rows_d", [4, 512], F32)
    m1D = dscr("m1_d", [512, T], BF16)
    hnD = dscr("hn_d", [512, C], BF16)
    qtokD = dscr("qtok_d", [512, C], BF16)
    ktokD = dscr("ktok_d", [512, C], BF16)
    hn2D = dscr("hn2_d", [512, C], BF16)
    aDr = dscr("a_d", [H * 512, 512], BF16)

    P = Prog()
    P.trunc = trunc
    P.dummyD = dscr("dummy_d", [1, 4], F32)

    BIGW = sb("BIGW", [128, 32768], BF16)
    P.dummyS = sb("dummy_s", [1, 4], F32)
    Wqp = BIGW[:, 0:8192]
    Wkp = BIGW[:, 8192:16384]
    WvT = BIGW[:, 16384:24576]
    WoutT = BIGW[:, 24576:32768]
    qpT = sb("qpT", [128, 8 * PASS_T], BF16)
    kpT = sb("kpT", [128, 8 * PASS_T], BF16)
    vst = sb("vst", [128, PCH * 1040], BF16)
    xTb = [sb("xT0", [128, 8 * 512], BF16),
           sb("xT1", [128, 8 * 512], BF16)]
    xT = xTb[0]
    kptok = sb("kptok", [128, 2048], BF16)
    state_z = sb("state_z", [128, 16 * 65], BF16)
    awb = sb("awb", [128, 2048], BF16)
    hh = sb("hh", [128, 1024], BF16)
    bqp = sb("bqp", [128, 8])
    bkp = sb("bkp", [128, 8])
    kspart = sb("kspart", [128, 8 * PCH])
    ksrun = sb("ksrun", [128, 8])
    recn = sb("recn", [128, 16])
    rr = sb("rr", [128, 1024])
    stat = sb("stat", [128, 8])
    ustat = sb("ustat", [128, 8])
    x1b = sb("x1b", [128, 1024], BF16)
    wacc = sb("wacc", [128, 32])
    trashb = sb("trashb", [128, 1024], BF16)
    trashv = sb("trashv", [128, 1024], BF16)
    zbf = sb("zbf", [128, 512], BF16)
    bout_rep = sb("bout_rep_s", [128, C])
    bv_rep = sb("bv_rep_s", [128, C])
    ln1_rep = sb("ln1_rep_s", [128, C])
    ln2_rep = sb("ln2_rep_s", [128, C])
    rw_rep = sb("rw_rep_s", [128, C])
    icol = sb("icol_s", [128, 1])
    iota512 = sb("iota512_s", [128, 512])
    thi = sb("thi_s", [128, 32])
    tlo = sb("tlo_s", [128, 32])
    tcol = sb("tcol_s", [128, 32])
    tri = sb("tri_s", [128, 128], BF16)
    idf128 = sb("idf128_s", [128, 128])
    onesrow = sb("onesrow_s", [1, 128])
    awmask = sb("awmask_s", [128, 512], BF16)
    invf = sb("invf_s", [1, C])
    neglarge = sb("neglarge", [1, 128])
    poslarge = sb("poslarge", [1, 128])
    czero = sb("czero", [128, 1])
    bstg = sb("bstg", [128, 1], BF16)
    brow = sb("brow", [1, 128])
    lodc = sb("lodc", [128, 2])
    pos_s = sb("pos_s", [128, 32], I32)
    wrepx = None
    cmpt = sb("cmpt", [128, 4096], BF16)
    cnt2 = sb("cnt2", [128, 2])
    tpc = sb("tpc", [1, 128])
    tpp = sb("tpp", [1, 128])
    st8 = sb("st8", [1, 8])
    probes = sb("probes", [128, 1])
    r1 = sb("r1", [1, 128])
    r1m = sb("r1m", [1, 128], U32)
    m01 = sb("m01", [128, 32])
    scan_t = sb("scan_t", [128, 32])
    offs = sb("offs", [128, 1])
    ordp = sb("ordp", [128, 32])
    meta = sb("meta", [128, 192], BF16)
    po = sb("po", [128, 512], BF16)
    sel8 = sb("sel8_s", [8, 512])
    selrow = sb("selrow", [1, 512])
    selrow2 = sb("selrow2", [1, 512])
    selrow3 = sb("selrow3", [1, 512])
    posrow = sb("posrow", [1, 512])
    wrowt = sb("wrowt", [1, 512])
    scatrow = sb("scatrow", [1, 512])
    selcol = sb("selcol", [128, 4])
    selcoli = sb("selcoli", [128, 4], I32)
    scatcoli = sb("scatcoli", [128, 4], I32)
    wselc = sb("wselc", [128, 4])
    selrep = sb("selrep", [128, 512])
    selrep_bfv = selrep[:, :].bitcast(BF16)
    posi2 = sb("posi2", [128, 32], I32)
    posf = sb("posf", [128, 32])
    tmp32 = sb("tmp32", [128, 32])
    onesm = sb("onesm_s", [128, 32])
    SCRATCH4 = sb("SCRATCH4", [128, 4096])
    qTb = SCRATCH4[:, 0:2048].bitcast(BF16)
    vselb = SCRATCH4[:, 2048:4096].bitcast(BF16)
    wrepx = SCRATCH4
    m1f = SCRATCH4
    x2a = SCRATCH4
    scs = sb("scs", [128, 512])
    scs_bfv = scs[:, :].bitcast(BF16)
    
    arow = sb("arow", [128, 512], BF16)
    aT = sb("aT", [128, 1024], BF16)
    kTb = cmpt
    rcop = hh
    hnt = hh
    qtok = x1b
    rr2 = bout_rep
    r2f = bv_rep
    msel = vst[:, 0:2048].bitcast(BF16) if False else None
    xsel = None

    p_qkv = [psum(f"p_qkv{i}", [128, 512]) for i in range(2)]
    p_awA = psum("p_awA", [128, 512])
    p_awB = psum("p_awB", [128, 512])
    p_ctxA = psum("p_ctxA", [128, 512])
    p_ctxB = psum("p_ctxB", [128, 512])
    p_kv = [psum(f"p_kv{i}", [128, 512]) for i in range(2)]

    def dma_g(dst, src, deps=(), allow_slow_non_contiguous=False):
        if allow_slow_non_contiguous:
            return P.gd(lambda g: g.dma_start(out=dst, in_=src,
                                              allow_slow_non_contiguous=True), deps)
        return P.gd(lambda g: g.dma_start(out=dst, in_=src), deps)

    def dma_y(dst, src, deps=(), tr=False):
        if tr:
            return P.yd(lambda y: y.dma_start(out=dst, in_=src, transpose=True), deps)
        return P.yd(lambda y: y.dma_start(out=dst, in_=src), deps)

    def dma_s(dst, src, deps=(), tr=False):
        if tr:
            return P.sd(lambda s: s.dma_start(out=dst, in_=src, transpose=True), deps)
        return P.sd(lambda s: s.dma_start(out=dst, in_=src), deps)

    ld = {}
    for name, dst, src in [
        ("icol", icol, icolD), ("iota512", iota512, iota512D), ("thi", thi, thiD),
        ("tlo", tlo, tloD), ("tcol", tcol, tcolD), ("tri", tri, triD),
        ("idf128", idf128, idf128D), ("onesrow", onesrow, onesrowD),
        ("awmask", awmask, awmaskD), ("invf", invf, invfD),
        ("bout", bout_rep, bout_repD), ("bv", bv_rep, bv_repD),
        ("ln1", ln1_rep, ln1_repD), ("ln2", ln2_rep, ln2_repD),
        ("rw", rw_rep, rw_repD), ("pos", pos_s, posD), ("onesm", onesm, onesmD),
    ]:
        ld[name] = dma_g(dst[:, :], src[:, :])
    h_cz = P.v(lambda v: v.memset(czero[:, :], 0.0))
    h_zb = P.v(lambda v: v.memset(zbf[:, :], 0.0))
    h_sz = P.v(lambda v: v.memset(state_z[:, :], 0.0))
    h_ks0 = P.v(lambda v: v.memset(ksrun[:, :], 0.0))
    P.v(lambda v: v.memset(neglarge[0:1, :], -1e30))
    P.v(lambda v: v.memset(poslarge[0:1, :], 1e30))

    h_xbf = [dma_g(xbfD[pz * PASS_T:(pz + 1) * PASS_T, :],
                   xD[pz * PASS_T:(pz + 1) * PASS_T, :]) for pz in range(NPASS)]
    h_wvb = dma_g(wvbfD[:, :], wqkvD[2 * C:3 * C, :])
    h_wob = dma_g(woutbfD[:, :], woutD[:, :])

    h_kpz = []
    for mt in range(16):
        for j in range(8):
            h_kpz.append(dma_g(
                kpxD[mt * 128 + 64:mt * 128 + 128, j * 512:(j + 1) * 512],
                zbf[0:64, :], deps=[h_zb]))

    stg = BIGW[:, 16384:32768]
    h_wqk = [dma_g(stg[:, r * 1024:(r + 1) * 1024],
                   wqkvD[r * 128:(r + 1) * 128, :]) for r in range(16)]
    projb = rr
    projb = awb
    h_projb = dma_g(projb[0:64, 0:1024].rearrange("d (h m) -> d h m", h=16),
                    projD.rearrange("h d m -> d h m"))
    bd = kptok
    h_bd0 = P.v(lambda v: v.memset(bd[:, 0:1024], 0.0))
    h_bdc = []
    for p in range(8):
        h_bdc.append(P.v(lambda v, p=p: v.tensor_copy(
            bd[0:64, p * 128:p * 128 + 64], projb[0:64, (2 * p) * 64:(2 * p + 1) * 64]),
            [h_bd0, h_projb]))
        h_bdc.append(P.v(lambda v, p=p: v.tensor_copy(
            bd[64:128, p * 128 + 64:p * 128 + 128],
            projb[0:64, (2 * p + 1) * 64:(2 * p + 2) * 64]), [h_bd0, h_projb]))
    h_fold = []
    prev_cp = None
    for wi in range(2):
        Wdst = Wqp if wi == 0 else Wkp
        for pr in range(8):
            for kc in range(8):
                hm = P.t(lambda t, wi=wi, pr=pr, kc=kc: t.matmul(
                    p_qkv[0][:, 0:128],
                    stg[:, (wi * 8 + pr) * 1024 + kc * 128:(wi * 8 + pr) * 1024 + (kc + 1) * 128],
                    bd[:, pr * 128:(pr + 1) * 128], start=True, stop=True),
                    [h_wqk[wi * 8 + pr], prev_cp] + (h_bdc if (wi == 0 and pr == 0 and kc == 0) else []))
                prev_cp = P.v(lambda v, Wdst=Wdst, pr=pr, kc=kc: v.tensor_copy(
                    Wdst[:, kc * 1024 + pr * 128:kc * 1024 + (pr + 1) * 128],
                    p_qkv[0][:, 0:128]), [hm])
                h_fold.append(prev_cp)
    h_bias = []
    prev_hm2 = None
    for bi in range(2):
        bdst = bqp if bi == 0 else bkp
        row0 = bi * 1024
        for pr in range(8):
            hb1 = dma_g(bstg[:, 0:1], bqkvD[row0 + pr * 128:row0 + (pr + 1) * 128, 0:1],
                        deps=([h_bias[-1], prev_hm2] if h_bias else []))
            hm = P.t(lambda t, pr=pr: t.matmul(p_qkv[1][0:1, 0:128], bstg[:, 0:1],
                                               bd[:, pr * 128:(pr + 1) * 128],
                                               start=True, stop=True),
                     [hb1, prev_hm2] + h_bdc)
            hv = P.v(lambda v: v.tensor_copy(brow[0:1, :], p_qkv[1][0:1, 0:128]),
                     [hm] + ([h_bias[-1]] if h_bias else []))
            hm2 = P.t(lambda t: t.matmul(p_qkv[1][:, 128:129], brow[0:1, :],
                                         idf128[0:1, 0:1], start=True, stop=True),
                      [hv, ld["idf128"]])
            prev_hm2 = hm2
            h_bias.append(P.v(lambda v, bdst=bdst, pr=pr: v.tensor_copy(
                bdst[:, pr:pr + 1], p_qkv[1][:, 128:129]), [hm2]))
    fold_done = h_fold + h_bias
    last_bias_mm = prev_hm2

    h_WvT, h_WoutT = [], []
    for kc in range(8):
        h_WvT.append(dma_y(WvT[:, kc * 1024:(kc + 1) * 1024],
                           wvbfD[:, kc * 128:(kc + 1) * 128],
                           deps=[h_wvb] + fold_done, tr=True))
        h_WoutT.append(dma_y(WoutT[:, kc * 1024:(kc + 1) * 1024],
                             woutbfD[:, kc * 128:(kc + 1) * 128],
                             deps=[h_wob] + fold_done, tr=True))

    h_rb = dma_g(brow[0:1, 0:1], rbD[0:1, 0:1], deps=[last_bias_mm, h_bias[-1]])
    h_rbm = P.t(lambda t: t.matmul(p_qkv[1][:, 129:130], onesrow[0:1, :],
                                   brow[0:1, 0:1], start=True, stop=True),
                [h_rb, ld["onesrow"]] + h_bias)
    h_rbc = P.v(lambda v: v.tensor_copy(lodc[:, 1:2], p_qkv[1][:, 129:130]), [h_rbm])

    vcol = lambda cc, hd: cc * 1040 + hd * 65
    urr = SCRATCH4[:, 0:1024]
    h_uns = []
    prev_uh = []
    h_x1_done = []
    h_router = []
    bank_last = [h_fold[-1], h_rbc]
    aw_last = [None, None]
    awb_readers = [[h_bdc[-1], h_projb], [h_bdc[-1], h_projb]]
    prev_div = []
    prev_hst = None
    prev_sq = None
    prev_m7 = None
    prev_hr = [None, None]
    first_in_kv = [True, True]
    statez_done = [h_sz]
    prev_pass = []
    prev_ap = None

    wout_mms_hist = []
    kpacc_last_hist = []
    kpst_hist = []
    last_ksu = None
    for pz in range(NPASS):
        t0 = pz * PASS_T
        xT = xTb[pz % 2]
        h_vones = P.v(lambda v: v.memset(
            vst[:, :].rearrange("p (cc h x) -> p cc h x", cc=PCH, h=H)[:, :, :, 64:65],
            1.0), statez_done[-1:])
        hxt = [dma_y(xT[:, kc * 512:(kc + 1) * 512],
                     xbfD[t0:t0 + 512, kc * 128:(kc + 1) * 128],
                     deps=[h_xbf[pz]] + (wout_mms_hist[pz - 2] if pz >= 2 else []),
                     tr=True) for kc in range(8)]
        qkv_done = []
        kp_evacs = []
        kp_accs = []
        pq = 0
        for wi in range(2):
            W, bias, dst = (Wqp, bqp, qpT) if wi == 0 else (Wkp, bkp, kpT)
            for mi in range(8):
                hmm = None
                for kc in range(8):
                    hmm = P.t(lambda t, W=W, mi=mi, kc=kc, pq=pq, xT=xT: t.matmul(
                        p_qkv[pq % 2][:, :],
                        W[:, kc * 1024 + mi * 128:kc * 1024 + (mi + 1) * 128],
                        xT[:, kc * 512:(kc + 1) * 512],
                        start=(kc == 0), stop=(kc == 7)),
                        (hxt + fold_done + [bank_last[pq % 2]])
                        if kc == 0 else [])
                hev = P.v(lambda v, dst=dst, bias=bias, mi=mi, pq=pq: v.tensor_scalar(
                    dst[:, mi * PASS_T:(mi + 1) * PASS_T], p_qkv[pq % 2][:, :],
                    bias[:, mi:mi + 1], 0.0, op0=ALU.add, op1=ALU.max),
                    [hmm] + h_bias +
                    ((kpst_hist[pz - 1] + kpacc_last_hist[pz - 1])
                     if (wi == 1 and pz) else []))
                qkv_done.append(hev)
                bank_last[pq % 2] = hev
                if wi == 1:
                    kp_evacs.append(hev)
                    for j in range(PCH):
                        hacc = P.s(lambda s, dst=dst, mi=mi, j=j: s.activation(
                            trashb[:, 0:128],
                            dst[:, mi * PASS_T + j * 128:mi * PASS_T + (j + 1) * 128],
                            AF.Copy, bias=0.0, scale=1.0,
                            accum_out=kspart[:, mi * PCH + j:mi * PCH + j + 1]),
                            [hev] + ([last_ksu] if last_ksu else []) +
                            ([prev_sq] if prev_sq else []))
                        prev_sq = hacc
                        kp_accs.append(hacc)
                        qkv_done.append(hacc)
                pq += 1
        kpacc_last_hist.append([prev_sq])
        for tt_ in range(PCH):
            for halfn in range(2):
                hmm = None
                for kc in range(8):
                    hmm = P.t(lambda t, tt_=tt_, halfn=halfn, kc=kc, pq=pq,
                              xT=xT: t.matmul(
                        p_qkv[pq % 2][:, :],
                        xT[:, kc * 512 + tt_ * 128:kc * 512 + (tt_ + 1) * 128],
                        WvT[:, kc * 1024 + halfn * 512:kc * 1024 + (halfn + 1) * 512],
                        start=(kc == 0), stop=(kc == 7)),
                        (hxt + h_WvT + [bank_last[pq % 2]]) if kc == 0 else [])
                hev = P.v(lambda v, tt_=tt_, halfn=halfn, pq=pq: v.scalar_tensor_tensor(
                    vst[:, :].rearrange("p (cc h x) -> p cc h x", cc=PCH, h=H)[
                        :, tt_, halfn * 8:(halfn + 1) * 8, 0:64],
                    p_qkv[pq % 2][:, :].rearrange("p (h x) -> p h x", h=8),
                    1.0,
                    bv_rep[:, halfn * 512:(halfn + 1) * 512].rearrange("p (h x) -> p h x", h=8),
                    op0=ALU.mult, op1=ALU.add), [hmm, h_vones, ld["bv"]])
                qkv_done.append(hev)
                bank_last[pq % 2] = hev
                pq += 1
        h_kpst = []
        for mt in range(8):
            dst = kpxD[mt * 256:(mt + 1) * 256, t0:t0 + PASS_T].rearrange(
                "(two sub) t -> two sub t", sub=128)[:, 0:64, :]
            src = kpT[:, mt * PASS_T:(mt + 1) * PASS_T].rearrange(
                "p (two t) -> p two t", two=1)[:, 0, :].rearrange(
                "p t -> p t")
            h_kpst.append(dma_g(dst, kpT[:, mt * PASS_T:(mt + 1) * PASS_T].rearrange(
                "p t -> p t"), deps=kp_evacs))
        kpst_hist.append(h_kpst)

        pass_consumers = []
        h_hst_all = []
        for cc in range(PCH):
            ch = pz * PCH + cc
            ct0 = cc * 128
            hkptok = dma_y(kptok[:, :], kpxD[:, t0 + ct0:t0 + ct0 + 128],
                           deps=h_kpst + h_kpz + [statez_done[-1]] +
                           ([h_rbc, h_rbm] if ch == 0 else []), tr=True)
            h_ksu = P.v(lambda v, cc=cc: v.tensor_tensor(
                ksrun[:, :], ksrun[:, :], kspart[:, cc::PCH], op=ALU.add),
                kp_accs + [statez_done[-1]] + ([h_ks0] if ch == 0 else []))
            last_ksu = h_ksu
            h_div = []
            statez_prev = list(statez_done)
            h_stm = []
            for rnd in range(2):
                h0 = rnd * 8
                h_awm = []
                for k in range(8):
                    hd = h0 + k
                    mt, ro = hd // 2, (hd % 2) * 64
                    bankp = p_awA if (k % 2 == 0) else p_awB
                    slot = (k // 2) * 128
                    h_awm.append(P.t(lambda t, mt=mt, ro=ro, bankp=bankp, slot=slot,
                                     ct0=ct0: t.matmul(
                        bankp[:, slot:slot + 128],
                        kpT[ro:ro + 64, mt * PASS_T + ct0:mt * PASS_T + ct0 + 128],
                        qpT[ro:ro + 64, mt * PASS_T + ct0:mt * PASS_T + ct0 + 128],
                        start=True, stop=True),
                        (qkv_done if (k == 0 and rnd == 0) else []) +
                        ([aw_last[k % 2]] if k < 2 else [])))
                h_awe = []
                for bi, bankp in enumerate((p_awA, p_awB)):
                    h_awe.append(P.v(lambda v, bankp=bankp, h0=h0, bi=bi: v.tensor_tensor(
                        awb[:, :].rearrange("p (b x) -> p b x", x=128)[
                            :, h0 + bi:h0 + 8:2, :],
                        bankp[:, :].rearrange("p (b x) -> p b x", x=128),
                        awmask[:, :].rearrange("p (b x) -> p b x", x=128),
                        op=ALU.mult), h_awm + [ld["awmask"]] + awb_readers[rnd]))
                for bi in range(2):
                    aw_last[bi] = h_awe[bi]
                h_ctxm = []
                for k in range(8):
                    hd = h0 + k
                    pcx = p_ctxA if k < 4 else p_ctxB
                    slot = (k % 4) * 65
                    mt = hd // 2
                    h_in = P.t(lambda t, hd=hd, cc=cc, pcx=pcx, slot=slot: t.matmul(
                        pcx[:, slot:slot + 65],
                        awb[:, hd * 128:(hd + 1) * 128],
                        vst[:, vcol(cc, hd):vcol(cc, hd) + 65],
                        start=True, stop=False),
                        h_awe + (h_div[-2:] if h_div else prev_div))
                    h_hi = P.t(lambda t, mt=mt, ct0=ct0, hd=hd, pcx=pcx, slot=slot:
                               t.matmul(
                        pcx[:, slot:slot + 65],
                        qpT[:, mt * PASS_T + ct0:mt * PASS_T + ct0 + 128],
                        state_z[:, hd * 65:(hd + 1) * 65],
                        start=False, stop=True), [h_in] + statez_prev)
                    h_ctxm.append(h_hi)
                awb_readers[rnd] = [h_ctxm[-1]]
                h_rec1 = P.v(lambda v, rnd=rnd: v.tensor_scalar(
                    recn[:, rnd * 8:rnd * 8 + 4], p_ctxA[:, 64:260:65],
                    float(DELTA), None, op0=ALU.add), h_ctxm + prev_div)
                h_rec2 = P.v(lambda v, rnd=rnd: v.tensor_scalar(
                    recn[:, rnd * 8 + 4:rnd * 8 + 8], p_ctxB[:, 64:260:65],
                    float(DELTA), None, op0=ALU.add), h_ctxm + prev_div)
                h_rec3 = P.v(lambda v, rnd=rnd: v.reciprocal(
                    recn[:, rnd * 8:rnd * 8 + 8], recn[:, rnd * 8:rnd * 8 + 8]),
                    [h_rec1, h_rec2])
                for k in range(8):
                    hd = h0 + k
                    pcx = p_ctxA if k < 4 else p_ctxB
                    slot = (k % 4) * 65
                    hdv = P.v(lambda v, hd=hd, pcx=pcx, slot=slot: v.tensor_scalar(
                        hh[:, hd * 64:(hd + 1) * 64], pcx[:, slot:slot + 64],
                        recn[:, hd:hd + 1], None, op0=ALU.mult),
                        [h_rec3, prev_hst])
                    h_div.append(hdv)
                for k in range(8):
                    hd = h0 + k
                    bank = hd // 8
                    hst = P.t(lambda t, hd=hd, cc=cc, bank=bank,
                              stf=first_in_kv[bank]: t.matmul(
                        p_kv[bank][:, (hd % 8) * 64:(hd % 8 + 1) * 64],
                        kptok[:, hd * 128 - 64 * (hd % 2):hd * 128 - 64 * (hd % 2) + 128],
                        vst[:, vcol(cc, hd):vcol(cc, hd) + 64],
                        start=stf, stop=(ch == NCH - 1 and hd % 8 == 7),
                        skip_group_check=True),
                        [hkptok] + (h_ctxm if k == 0 and rnd == 1 else []) +
                        (statez_prev if ch else []))
                    first_in_kv[bank] = False
                    h_stm.append(hst)
            h_szc = []
            for bank in range(2):
                h_szc.append(P.v(lambda v, bank=bank: v.tensor_copy(
                    state_z[:, :].rearrange("p (hd x) -> p hd x", x=65)[
                        :, bank * 8:(bank + 1) * 8, 0:64],
                    p_kv[bank][:, :].rearrange("p (hd x) -> p hd x", x=64)),
                    h_stm + statez_prev))
            h_szc.append(P.v(lambda v: v.tensor_copy(
                state_z[0:64, :].rearrange("p (hd x) -> p hd x", x=65)[:, 0::2, 64:65],
                ksrun[0:64, :].unsqueeze(2)), h_stm + [h_ksu]))
            h_szc.append(P.v(lambda v: v.tensor_copy(
                state_z[64:128, :].rearrange("p (hd x) -> p hd x", x=65)[:, 1::2, 64:65],
                ksrun[64:128, :].unsqueeze(2)), h_stm + [h_ksu]))
            statez_done = h_szc
            h_hst = dma_g(hD[t0 + ct0:t0 + ct0 + 128, :], hh[:, :], deps=h_div)
            h_hst_all.append(h_hst)
            prev_div = h_div[-2:]
            prev_hst = h_hst
            pass_consumers.append(h_hst)
            pass_consumers.extend(h_stm)
            pass_consumers.append(h_ksu)

        hT = xT
        h_hT = [dma_y(hT[:, kc * 512:(kc + 1) * 512],
                      hD[t0:t0 + 512, kc * 128:(kc + 1) * 128],
                      deps=h_hst_all + qkv_done, tr=True) for kc in range(8)]
        h_wout_mms = []
        for tt_ in range(PCH):
            ct0 = tt_ * 128
            h_x = dma_g(rr[:, :], xD[t0 + ct0:t0 + ct0 + 128, :],
                        deps=[d for d in (prev_ap, prev_sq) if d])
            h_wo = []
            for halfn in range(2):
                hmm = None
                for kc in range(8):
                    hmm = P.t(lambda t, tt_=tt_, halfn=halfn, kc=kc, hT=hT: t.matmul(
                        p_qkv[halfn][:, :],
                        hT[:, kc * 512 + tt_ * 128:kc * 512 + (tt_ + 1) * 128],
                        WoutT[:, kc * 1024 + halfn * 512:kc * 1024 + (halfn + 1) * 512],
                        start=(kc == 0), stop=(kc == 7)),
                        (h_hT + h_WoutT + h_stm +
                         [d for d in (prev_hr[halfn], bank_last[halfn]) if d])
                        if kc == 0 else [])
                h_wo.append(hmm)
                h_wout_mms.append(hmm)
            h_xb = P.v(lambda v: v.tensor_tensor(rr[:, :], rr[:, :], bout_rep[:, :],
                                                 op=ALU.add), [h_x, ld["bout"]])
            h_r = []
            for halfn in range(2):
                h_r.append(P.v(lambda v, halfn=halfn: v.scalar_tensor_tensor(
                    rr[:, halfn * 512:(halfn + 1) * 512], p_qkv[halfn][:, :], 1.0,
                    rr[:, halfn * 512:(halfn + 1) * 512], op0=ALU.mult, op1=ALU.add,
                    accum_out=stat[:, halfn:halfn + 1]), [h_xb, h_wo[halfn]]))
            prev_hr = list(h_r)
            for halfn in range(2):
                bank_last[halfn] = h_r[halfn]
            h_sq = P.s(lambda s: s.activation(trashb[:, :], rr[:, :], AF.Square,
                                              accum_out=stat[:, 2:3]),
                       h_r + [d for d in (prev_sq,) if d])
            prev_sq = h_sq
            h_m2 = P.v(lambda v: v.tensor_tensor(stat[:, 0:1], stat[:, 0:1],
                                                 stat[:, 1:2], op=ALU.add), h_r)
            h_m2b = P.v(lambda v: v.tensor_scalar(stat[:, 0:1], stat[:, 0:1],
                                                  1.0 / 1024.0, None, op0=ALU.mult),
                        [h_m2])
            h_m3 = P.v(lambda v: v.tensor_tensor(stat[:, 3:4], stat[:, 0:1],
                                                 stat[:, 0:1], op=ALU.mult), [h_m2b])
            h_m4 = P.v(lambda v: v.tensor_scalar(stat[:, 2:3], stat[:, 2:3],
                                                 1.0 / 1024.0, None, op0=ALU.mult),
                       [h_sq])
            h_m5 = P.v(lambda v: v.tensor_tensor(stat[:, 3:4], stat[:, 2:3],
                                                 stat[:, 3:4], op=ALU.subtract),
                       [h_m3, h_m4])
            h_m6 = P.v(lambda v: v.tensor_scalar(stat[:, 3:4], stat[:, 3:4], 1e-5,
                                                 None, op0=ALU.add), [h_m5])
            h_m7 = P.s(lambda s: s.activation(stat[:, 4:5], stat[:, 3:4], AF.Sqrt),
                       [h_m6] + [d for d in (prev_m7,) if d])
            prev_m7 = h_m7
            h_m8 = P.v(lambda v: v.reciprocal(stat[:, 4:5], stat[:, 4:5]), [h_m7])
            h_ap = P.v(lambda v: v.tensor_scalar(x1b[:, :], rr[:, :], stat[:, 0:1],
                                                 stat[:, 4:5], op0=ALU.subtract,
                                                 op1=ALU.mult),
                       [h_m8] + (h_x1_done[-1:] if h_x1_done else []) +
                       (h_router[-1:] if h_router else []))
            prev_ap = h_ap
            a_glob = pz * PCH + tt_
            h_rt = P.v(lambda v, a_glob=a_glob: v.scalar_tensor_tensor(
                trashv[:, :], x1b[:, :], 1.0, rw_rep[:, :], op0=ALU.mult,
                op1=ALU.mult, accum_out=wacc[:, a_glob:a_glob + 1]),
                [h_ap, ld["rw"]])
            h_router.append(h_rt)
            h_x1 = dma_g(x1D[t0 + ct0:t0 + ct0 + 128, :], x1b[:, :], deps=[h_ap])
            h_x1_done.append(h_x1)
            h_u1 = P.s(lambda s: s.activation(urr[:, :], x1b[:, :], AF.Copy,
                                              bias=0.0, scale=2.0,
                                              accum_out=ustat[:, 0:1]),
                       [h_ap, prev_m7] + prev_uh + h_uns[-1:])
            h_u2 = P.s(lambda s: s.activation(trashb[:, :], urr[:, :], AF.Square,
                                              accum_out=ustat[:, 2:3]),
                       [h_u1] + ([prev_sq] if prev_sq else []))
            prev_sq = h_u2
            h_u3 = P.v(lambda v: v.tensor_scalar(ustat[:, 0:1], ustat[:, 0:1],
                                                 1.0 / 1024.0, None, op0=ALU.mult),
                       [h_u1])
            h_u4 = P.v(lambda v: v.tensor_tensor(ustat[:, 3:4], ustat[:, 0:1],
                                                 ustat[:, 0:1], op=ALU.mult), [h_u3])
            h_u5 = P.v(lambda v: v.tensor_scalar(ustat[:, 2:3], ustat[:, 2:3],
                                                 1.0 / 1024.0, None, op0=ALU.mult),
                       [h_u2])
            h_u6 = P.v(lambda v: v.tensor_tensor(ustat[:, 3:4], ustat[:, 2:3],
                                                 ustat[:, 3:4], op=ALU.subtract),
                       [h_u4, h_u5])
            h_u7 = P.v(lambda v: v.tensor_scalar(ustat[:, 3:4], ustat[:, 3:4], 1e-5,
                                                 None, op0=ALU.add), [h_u6])
            h_u8 = P.s(lambda s: s.activation(ustat[:, 4:5], ustat[:, 3:4], AF.Sqrt),
                       [h_u7, prev_sq])
            prev_sq = h_u8
            h_u9 = P.v(lambda v: v.reciprocal(ustat[:, 4:5], ustat[:, 4:5]), [h_u8])
            h_u10 = P.v(lambda v: v.tensor_scalar(urr[:, :], urr[:, :], ustat[:, 0:1],
                                                  ustat[:, 4:5], op0=ALU.subtract,
                                                  op1=ALU.mult), [h_u9])
            h_ud = dma_s(outD[t0 + ct0:t0 + ct0 + 128, :], urr[:, :],
                         deps=[h_u10])
            h_uns.append(h_ud)
            prev_uh = [h_u10]
            pass_consumers = pass_consumers + [h_x1, h_rt, h_ap]
        wout_mms_hist.append(h_wout_mms)
        prev_pass = pass_consumers

    h_wb = P.v(lambda v: v.tensor_scalar(wacc[:, :], wacc[:, :], lodc[:, 1:2], None,
                                         op0=ALU.add), [h_rbc] + h_router)
    h_ww = dma_g(wwD[:, :], wacc[:, :], deps=[h_wb])

    if debug_stage in (1, 3):
        srcdbg = x1D if debug_stage == 1 else hD
        hdbg = [P.gd(lambda g, a=a, srcdbg=srcdbg: g.dma_start(
            out=outD[a * 128:(a + 1) * 128, :], in_=srcdbg[a * 128:(a + 1) * 128, :]),
            h_x1_done) for a in range(32)]
        P.gd(lambda g: g.dma_start(out=outD[T:T + 1, 0:32], in_=wacc[0:1, :]), [h_wb])
        with nc.Block() as block, ExitStack() as sstack:
            P.emit(nc, block, sstack)
        ctx.close()
        return nc

    msel = vst[:, 0:2048]
    m1T = vst[:, 2048:3072]
    oT = xTb[0]
    hn2T = qpT
    scs_bf = None
    _h1small = None

    def h1slot(ft):
        small = [(msel, 0), (msel, 512), (msel, 1024), (msel, 1536),
                 (aT, 0), (aT, 512), (x1b, 0), (x1b, 512),
                 (arow, 0), (trashv, 0), (trashv, 512),
                 (scs_bfv, 0), (scs_bfv, 512),
                 (selrep_bfv, 0), (selrep_bfv, 512), (po, 0)]
        if ft < 16:
            tns, base = small[ft]
            return tns[:, base:base + 512]
        tns = (kTb, kpT)[(ft - 16) // 8]
        return tns[:, ((ft - 16) % 8) * 512:((ft - 16) % 8 + 1) * 512]

    def xsel_t(c):
        return (awb, kptok)[c // 2][:, (c % 2) * 1024:(c % 2) * 1024 + 1024]
    h_wrep = dma_g(wrepx[:, :], wwD[0:1, :].partition_broadcast(128),
                   deps=[h_ww] + h_uns[-1:] + prev_uh)
    h_l1 = P.v(lambda v: v.tensor_reduce(cnt2[:, 0:1], wacc[:, :], axis=AX.X, op=ALU.min), [h_wb])
    h_l2 = P.v(lambda v: v.tensor_reduce(cnt2[:, 1:2], wacc[:, :], axis=AX.X, op=ALU.max), [h_wb, h_l1])
    h_t1 = P.t(lambda t: t.transpose(p_qkv[0][0:1, 0:128], cnt2[:, 0:1], idf128[:, :]),
               [h_l1, bank_last[0], ld["idf128"]])
    h_t2 = P.t(lambda t: t.transpose(p_qkv[1][0:1, 0:128], cnt2[:, 1:2], idf128[:, :]),
               [h_l2, bank_last[1]])
    bank_last[0], bank_last[1] = h_t1, h_t2
    h_c1 = P.v(lambda v: v.tensor_copy(tpc[0:1, :], p_qkv[0][0:1, 0:128]), [h_t1])
    h_c2 = P.v(lambda v: v.tensor_copy(tpp[0:1, :], p_qkv[1][0:1, 0:128]), [h_t2])
    h_s1 = P.v(lambda v: v.tensor_reduce(st8[0:1, 0:1], tpc[0:1, :], axis=AX.X, op=ALU.min), [h_c1])
    h_s2 = P.v(lambda v: v.tensor_reduce(st8[0:1, 1:2], tpp[0:1, :], axis=AX.X, op=ALU.max), [h_c2])
    h_s3 = P.v(lambda v: v.tensor_scalar(st8[0:1, 0:1], st8[0:1, 0:1], 0.001, None,
                                         op0=ALU.subtract), [h_s1])
    h_s4 = P.v(lambda v: v.memset(st8[0:1, 3:4], -1e30), [h_s2])
    bl = [h_s3, h_s4]
    h_cm_prev = None
    for it in range(NITER):
        h_d1 = P.v(lambda v: v.tensor_tensor(st8[0:1, 2:3], st8[0:1, 1:2], st8[0:1, 0:1],
                                             op=ALU.subtract), bl)
        h_d2 = P.v(lambda v: v.tensor_scalar(st8[0:1, 2:3], st8[0:1, 2:3], 1.0 / 129.0,
                                             None, op0=ALU.mult), [h_d1])
        h_bc = P.t(lambda t: t.matmul(p_qkv[0][:, 0:2], onesrow[0:1, :], st8[0:1, 0:3:2],
                                      start=True, stop=True),
                   [h_d2, bank_last[0], ld["onesrow"]])
        bank_last[0] = h_bc
        h_lc = P.v(lambda v: v.tensor_copy(lodc[:, 0:2], p_qkv[0][:, 0:2]),
                   [h_bc] + ([h_cm_prev] if h_cm_prev else []))
        h_pr = P.v(lambda v: v.scalar_tensor_tensor(probes[:, 0:1], icol[:, 0:1],
                                                    lodc[:, 1:2], lodc[:, 0:1],
                                                    op0=ALU.mult, op1=ALU.add),
                   [h_lc, ld["icol"]])
        h_cm = P.v(lambda v: v.tensor_scalar(cmpt[:, :], wrepx[:, :], probes[:, 0:1],
                                             None, op0=ALU.is_gt, op1=ALU.add,
                                             accum_out=cnt2[:, 0:1]),
                   [h_pr, h_wrep, h_l2])
        h_cm_prev = h_cm
        h_t3 = P.t(lambda t: t.transpose(p_qkv[0][0:1, 128:256], cnt2[:, 0:1],
                                         idf128[:, :]), [h_cm, bank_last[0]])
        h_t4 = P.t(lambda t: t.transpose(p_qkv[1][0:1, 0:128], probes[:, 0:1],
                                         idf128[:, :]), [h_pr, bank_last[1]])
        bank_last[0], bank_last[1] = h_t3, h_t4
        h_c3 = P.v(lambda v: v.tensor_copy(tpc[0:1, :], p_qkv[0][0:1, 128:256]),
                   [h_t3, h_s1])
        h_c4 = P.v(lambda v: v.tensor_copy(tpp[0:1, :], p_qkv[1][0:1, 0:128]),
                   [h_t4, h_s2])
        r1mf = r1m[0:1, :].bitcast(F32)

        def masked_red(cmp_op, thr_v, big_sign, red_op, dst_sl, deps):
            h1 = P.v(lambda v: v.tensor_scalar(brow[0:1, :], tpc[0:1, :], thr_v,
                                               None, op0=cmp_op), deps)
            h2 = P.v(lambda v: v.tensor_tensor(r1[0:1, :], brow[0:1, :], tpp[0:1, :],
                                               op=ALU.mult), [h1])
            h3 = P.v(lambda v: v.tensor_scalar(r1mf, brow[0:1, :],
                                               big_sign * 1e30, -big_sign * 1e30,
                                               op0=ALU.mult, op1=ALU.add), [h2])
            h4 = P.v(lambda v: v.tensor_tensor(r1[0:1, :], r1[0:1, :], r1mf,
                                               op=ALU.add), [h3])
            h5 = P.v(lambda v, dst_sl=dst_sl: v.tensor_reduce(
                st8[0:1, dst_sl:dst_sl + 1], r1[0:1, :], axis=AX.X, op=red_op), [h4])
            return h5

        h_e3 = masked_red(ALU.is_equal, float(KSEL), 1.0, ALU.max, 4,
                          [h_c3, h_c4, h_rb, h_rbm])
        h_e4 = P.v(lambda v: v.tensor_tensor(st8[0:1, 3:4], st8[0:1, 3:4],
                                             st8[0:1, 4:5], op=ALU.max), [h_e3])
        h_f3 = masked_red(ALU.is_ge, float(KSEL + 1), 1.0, ALU.max, 5, [h_e3])
        h_f4 = P.v(lambda v: v.tensor_tensor(st8[0:1, 0:1], st8[0:1, 0:1],
                                             st8[0:1, 5:6], op=ALU.max), [h_f3])
        h_g3 = masked_red(ALU.is_le, float(KSEL), -1.0, ALU.min, 6, [h_f3, h_f4])
        h_g4 = P.v(lambda v: v.tensor_tensor(st8[0:1, 1:2], st8[0:1, 1:2],
                                             st8[0:1, 6:7], op=ALU.min), [h_g3])
        bl = [h_e4, h_f4, h_g4]
    h_tb = P.t(lambda t: t.matmul(p_qkv[0][:, 2:3], onesrow[0:1, :], st8[0:1, 3:4],
                                  start=True, stop=True), bl + [bank_last[0]])
    bank_last[0] = h_tb
    h_tc = P.v(lambda v: v.tensor_copy(lodc[:, 0:1], p_qkv[0][:, 2:3]), [h_tb, h_g3])
    h_mm1 = P.v(lambda v: v.tensor_scalar(m01[:, :], wacc[:, :], lodc[:, 0:1], None,
                                          op0=ALU.is_gt), [h_tc, h_wb])
    h_sc1 = P.v(lambda v: v.tensor_tensor_scan(scan_t[:, :], m01[:, :], onesm[:, :],
                                               0.0, op0=ALU.add, op1=ALU.mult),
                [h_mm1, ld["onesm"]])
    h_rs1 = P.v(lambda v: v.tensor_copy(po[:, 0:1], cnt2[:, 0:1]), [h_cm_prev, h_c3])
    h_rs2 = P.v(lambda v: v.tensor_reduce(cnt2[:, 0:1], m01[:, :], axis=AX.X,
                                          op=ALU.add), [h_mm1, h_rs1])
    h_rs3 = P.v(lambda v: v.tensor_copy(po[:, 0:1], cnt2[:, 0:1]), [h_rs2])
    h_tri = P.t(lambda t: t.matmul(p_qkv[0][:, 3:4], tri[:, :], po[:, 0:1],
                                   start=True, stop=True),
                [h_rs3, ld["tri"], bank_last[0]])
    bank_last[0] = h_tri
    h_of1 = P.v(lambda v: v.tensor_copy(offs[:, 0:1], p_qkv[0][:, 3:4]), [h_tri])
    h_or1 = P.v(lambda v: v.scalar_tensor_tensor(ordp[:, :], scan_t[:, :], offs[:, 0:1],
                                                 m01[:, :], op0=ALU.add,
                                                 op1=ALU.subtract), [h_sc1, h_of1])
    h_or2 = P.v(lambda v: v.tensor_tensor(ordp[:, :], ordp[:, :], m01[:, :],
                                          op=ALU.mult), [h_or1])
    h_or3 = P.v(lambda v: v.tensor_scalar(tmp32[:, :], m01[:, :], -511.0, 511.0,
                                          op0=ALU.mult, op1=ALU.add), [h_or1])
    h_or4 = P.v(lambda v: v.tensor_tensor(ordp[:, :], ordp[:, :], tmp32[:, :],
                                          op=ALU.add), [h_or2, h_or3])
    h_p1 = P.v(lambda v: v.tensor_scalar(posi2[:, :], pos_s[:, :], 63, None,
                                         op0=ALU.bitwise_and), [ld["pos"]])
    h_p2 = P.v(lambda v: v.tensor_copy(tmp32[:, :], posi2[:, :]), [h_p1, h_or4])
    h_p3 = P.v(lambda v: v.tensor_scalar(posi2[:, :], pos_s[:, :], 6, None,
                                         op0=ALU.arith_shift_right), [h_p2])
    h_p4 = P.v(lambda v: v.tensor_copy(posf[:, :], posi2[:, :]), [h_p3])
    h_me = []
    h_me.append(P.v(lambda v: v.tensor_copy(meta[:, 0:192:6], thi[:, :]), [ld["thi"]]))
    h_me.append(P.v(lambda v: v.tensor_copy(meta[:, 1:192:6], tlo[:, :]), [ld["tlo"]]))
    h_me.append(P.v(lambda v: v.tensor_copy(meta[:, 2:192:6], posf[:, :]), [h_p4]))
    h_me.append(P.v(lambda v: v.tensor_copy(meta[:, 3:192:6], tmp32[:, :]), [h_p2, h_or4]))
    h_me.append(P.v(lambda v: v.tensor_copy(meta[:, 4:192:6], wacc[:, :]), [h_wb]))
    h_po_prev = None
    h_mm_prev = None
    for a in range(32):
        h_poa = P.v(lambda v, a=a: v.tensor_scalar(po[:, :], iota512[:, :],
                                                   ordp[:, a:a + 1], None,
                                                   op0=ALU.is_equal),
                    [h_or4, ld["iota512"]] + ([h_mm_prev] if h_mm_prev else [h_rs3, h_tri]))
        h_mm_prev = P.t(lambda t, a=a: t.matmul(p_qkv[1][0:5, 0:512],
                                                meta[:, 6 * a:6 * a + 5], po[:, :],
                                                start=(a == 0), stop=(a == 31),
                                                skip_group_check=True),
                        [h_poa, bank_last[1]] + h_me)
        h_po_prev = h_poa
    bank_last[1] = h_mm_prev
    h_s8 = P.v(lambda v: v.tensor_copy(sel8[0:5, :], p_qkv[1][0:5, 0:512]), [h_mm_prev])
    h_s8d = dma_g(sel8D[0:5, :], sel8[0:5, :], deps=[h_s8])
    h_r1d = dma_g(selrow[0:1, :], sel8D[0:1, :], deps=[h_s8d])
    h_r2d = dma_g(selrow2[0:1, :], sel8D[1:2, :], deps=[h_s8d])
    h_r3d = dma_g(posrow[0:1, :], sel8D[2:3, :], deps=[h_s8d])
    h_r4d = dma_g(selrow3[0:1, :], sel8D[3:4, :], deps=[h_s8d])
    h_r5d = dma_g(wrowt[0:1, :], sel8D[4:5, :], deps=[h_s8d])
    h_ri = P.v(lambda v: v.tensor_scalar(selrow[0:1, :], selrow[0:1, :], 64.0, None,
                                         op0=ALU.mult), [h_r1d])
    h_ri2 = P.v(lambda v: v.tensor_tensor(selrow[0:1, :], selrow[0:1, :],
                                          selrow2[0:1, :], op=ALU.add), [h_ri, h_r2d])
    h_sca = P.v(lambda v: v.tensor_scalar(scatrow[0:1, :], selrow[0:1, :],
                                          float(T), None, op0=ALU.min), [h_ri2])
    h_gat = P.v(lambda v: v.tensor_scalar(selrow[0:1, :], selrow[0:1, :],
                                          float(T - 1), None, op0=ALU.min), [h_sca])
    h_pi = P.v(lambda v: v.tensor_scalar(posrow[0:1, :], posrow[0:1, :], 64.0, None,
                                         op0=ALU.mult), [h_r3d])
    h_pi2 = P.v(lambda v: v.tensor_tensor(posrow[0:1, :], posrow[0:1, :],
                                          selrow3[0:1, :], op=ALU.add), [h_pi, h_r4d])
    h_rwd = dma_g(rowsD[0:1, :], selrow[0:1, :], deps=[h_gat])
    h_rwd2 = dma_g(rowsD[1:2, :], scatrow[0:1, :], deps=[h_sca])
    h_rwd3 = dma_g(rowsD[2:3, :], wrowt[0:1, :], deps=[h_r5d])
    h_cl1 = dma_g(selcol[:, 0:4], rowsD[0:1, :].rearrange("o (c p) -> (o p) c", p=128), allow_slow_non_contiguous=True,
                  deps=[h_rwd])
    h_cl2 = dma_g(wselc[:, 0:4], rowsD[2:3, :].rearrange("o (c p) -> (o p) c", p=128), allow_slow_non_contiguous=True,
                  deps=[h_rwd3])
    h_cl3 = dma_g(lodc[:, 0:2], rowsD[1:2, 0:256].rearrange("o (c p) -> (o p) c", p=128), allow_slow_non_contiguous=True,
                  deps=[h_rwd2, h_lc, h_tc, h_mm1])
    h_cl4 = dma_g(cnt2[:, 0:2], rowsD[1:2, 256:512].rearrange("o (c p) -> (o p) c", p=128), allow_slow_non_contiguous=True,
                  deps=[h_rwd2, h_rs2, h_cm_prev])
    h_ci = P.v(lambda v: v.tensor_copy(selcoli[:, :], selcol[:, :]), [h_cl1])
    h_si1 = P.v(lambda v: v.tensor_copy(scatcoli[:, 0:2], lodc[:, 0:2]), [h_cl3])
    h_si2 = P.v(lambda v: v.tensor_copy(scatcoli[:, 2:4], cnt2[:, 0:2]), [h_cl4])

    prev_u = [prev_ap, prev_sq, prev_m7]

    h_xg = []
    for c in range(4):
        h_xg.append(P.gd(lambda g, c=c: g.indirect_dma_start(
            xsel_t(c), None, x1D,
            IndirectOffsetOnAxis(selcoli[:, c:c + 1], 0)),
            [h_ci, aw_last[0], aw_last[1], statez_done[-1]] + h_x1_done + prev_div))
    h_m1 = []
    prev_mg = None
    for c in range(4):
        h_mg = P.gd(lambda g, c=c: g.indirect_dma_start(
            m1f[:, 0:4096], None, maskD,
            IndirectOffsetOnAxis(selcoli[:, c:c + 1], 0)),
            [h_ci, h_cm_prev] + ([prev_mg] if prev_mg else []))
        h_mc = P.v(lambda v, c=c: v.tensor_copy(cmpt[:, :], m1f[:, 0:4096]),
                   [h_mg] + ([h_cm_prev] if c == 0 else []) + h_m1[-1:])
        h_md = dma_g(m1D[c * 128:(c + 1) * 128, :], cmpt[:, :], deps=[h_mc])
        prev_mg = h_mc
        h_m1.append(h_md)

    wslots = {"fwq": BIGW[:, 0:8192], "fwk": BIGW[:, 8192:16384],
              "fwv": BIGW[:, 16384:24576], "fwo": BIGW[:, 24576:32768]}
    wsrc = {"fwq": fwqD, "fwk": fwkD, "fwv": fwvD, "fwo": fwoD}
    h_wffl = {}
    wdeps = list(prev_pass) + [h_mm_prev] + list(aw_last) + list(prev_div)
    for k in wslots:
        h_wffl[k] = [dma_g(wslots[k][:, kc * 1024:(kc + 1) * 1024],
                           wsrc[k][kc * 128:(kc + 1) * 128, :], deps=wdeps)
                     for kc in range(8)]

    h_msel_ev = []
    mselbanks = [p_qkv[0], p_qkv[1], p_awA, p_awB]
    mb_last = {0: bank_last[0], 1: bank_last[1], 2: aw_last[0], 3: aw_last[1]}
    h_selrep = P.t(lambda t: t.matmul(p_ctxA[:, 0:512], onesrow[0:1, :],
                                      selrow[0:1, :], start=True, stop=True),
                   [h_gat, prev_div[0] if prev_div else None, ld["onesrow"]])
    h_srv = P.v(lambda v: v.tensor_copy(selrep[:, :], p_ctxA[:, 0:512]),
                [h_selrep] + prev_div)
    h_m1T = [None, None]
    h_poc = None
    prev_ms = {0: None, 1: None, 2: None, 3: None}
    for a in range(32):
        buf = a % 2
        h_mt = dma_y(m1T[:, buf * 512:(buf + 1) * 512],
                     m1D[0:512, a * 128:(a + 1) * 128],
                     deps=h_m1 + ([h_m1T[buf]] if h_m1T[buf] else
                                  [h_mm_prev, statez_done[-1]]), tr=True)
        h_poc = P.v(lambda v, a=a: v.tensor_scalar(po[:, :], selrep[:, :],
                                                   tcol[:, a:a + 1], None,
                                                   op0=ALU.is_equal),
                    [h_srv, ld["tcol"], h_po_prev] +
                    ([h_poc] if h_poc else []) +
                    ([prev_ms[3]] if prev_ms[3] else []))
        for it_ in range(4):
            hmv = P.t(lambda t, a=a, it_=it_, buf=buf: t.matmul(
                mselbanks[it_][:, 0:512],
                m1T[:, buf * 512 + it_ * 128:buf * 512 + (it_ + 1) * 128],
                po[:, :], start=(a == 0), stop=(a == 31), skip_group_check=True),
                [h_mt, h_poc] + ([mb_last[it_]] if mb_last[it_] else []))
            mb_last[it_] = hmv
            h_m1T[buf] = hmv
        prev_ms[3] = mb_last[3]
    for it_ in range(4):
        hme = P.v(lambda v, it_=it_: v.tensor_copy(
            msel[:, it_ * 512:(it_ + 1) * 512], mselbanks[it_][:, 0:512]),
            [mb_last[it_], h_srv])
        hmz = P.v(lambda v, it_=it_: v.memset(msel[:, it_ * 512 + 511:it_ * 512 + 512],
                                              -1e30), [hme])
        h_msel_ev.append(hmz)
        mb_last[it_] = hmz

    h_hn = []
    prev_sq2 = None
    for c in range(4):
        h_q1 = P.s(lambda s, c=c: s.activation(trashb[:, :], xsel_t(c),
                                               AF.Square, accum_out=stat[:, 2:3]),
                   [h_xg[c]] + ([prev_sq2] if prev_sq2 else [prev_u[1]]) + h_uns[-1:])
        prev_sq2 = h_q1
        h_q2 = P.v(lambda v: v.tensor_scalar(stat[:, 3:4], stat[:, 2:3], 1.0 / 1024.0,
                                             None, op0=ALU.add if False else ALU.mult),
                   [h_q1] + h_uns[-1:])
        h_q3 = P.v(lambda v: v.tensor_scalar(stat[:, 3:4], stat[:, 3:4], 1e-6, None,
                                             op0=ALU.add), [h_q2])
        h_q4 = P.s(lambda s: s.activation(stat[:, 4:5], stat[:, 3:4], AF.Sqrt), [h_q3])
        h_q5 = P.v(lambda v: v.reciprocal(stat[:, 4:5], stat[:, 4:5]), [h_q4])
        h_q6 = P.v(lambda v, c=c: v.scalar_tensor_tensor(
            hnt[:, :], xsel_t(c), stat[:, 4:5],
            ln1_rep[:, :], op0=ALU.mult, op1=ALU.mult),
            [h_q5, ld["ln1"], h_xg[c]] + ([h_hn[-1]] if h_hn else prev_div))
        h_hnd = dma_g(hnD[c * 128:(c + 1) * 128, :], hnt[:, :], deps=[h_q6])
        h_hn.append(h_hnd)
    hnT = xTb[1]
    h_hnT = [dma_y(hnT[:, kc * 512:(kc + 1) * 512], hnD[0:512, kc * 128:(kc + 1) * 128],
                   deps=h_hn + wout_mms_hist[-1], tr=True) for kc in range(8)]

    angs = m1f[:, 0:1024]
    angc = m1f[:, 1024:2048]
    angt = m1f[:, 2048:3072]
    angt2v = m1f[:, 3072:4096]
    angt3v = rr[:, 0:1024]
    posi128 = r2f[:, 0:1024].bitcast(I32)
    qkbanks = [p_ctxA, p_ctxB]
    qk_last = {0: h_srv, 1: prev_div[1] if len(prev_div) > 1 else None}
    h_qtok = {}
    prev_rope = []
    wslot_mm = {"fwq": [], "fwk": [], "fwv": [], "fwo": []}
    for wq_name, dstD in (("fwq", qtokD), ("fwk", ktokD)):
        for c in range(4):
            h_qm = []
            for halfn in range(2):
                hmm = None
                for kc in range(8):
                    hmm = P.t(lambda t, c=c, halfn=halfn, kc=kc, wq_name=wq_name:
                              t.matmul(
                        qkbanks[halfn][:, 0:512],
                        hnT[:, kc * 512 + c * 128:kc * 512 + (c + 1) * 128],
                        wslots[wq_name][:, kc * 1024 + halfn * 512:kc * 1024 + (halfn + 1) * 512],
                        start=(kc == 0), stop=(kc == 7)),
                        (h_hnT + h_wffl[wq_name] +
                         ([qk_last[halfn]] if qk_last[halfn] else []))
                        if kc == 0 else [])
                h_qm.append(hmm)
                wslot_mm[wq_name].append(hmm)
            h_a1 = P.t(lambda t, c=c: t.matmul(p_kv[0][:, 0:512],
                                               posrow[0:1, c * 128:(c + 1) * 128],
                                               invf[0:1, 0:512], start=True, stop=True),
                       [h_pi2, ld["invf"], statez_done[-1] if wq_name == "fwq" and c == 0 else None] +
                       ([prev_rope[-1]] if prev_rope else []))
            h_a2 = P.t(lambda t, c=c: t.matmul(p_kv[1][:, 0:512],
                                               posrow[0:1, c * 128:(c + 1) * 128],
                                               invf[0:1, 512:1024], start=True,
                                               stop=True),
                       [h_pi2, statez_done[-1] if wq_name == "fwq" and c == 0 else None])
            h_b1 = P.v(lambda v: v.tensor_copy(angs[:, 0:512], p_kv[0][:, 0:512]),
                       [h_a1] + prev_rope[-1:])
            h_b2 = P.v(lambda v: v.tensor_copy(angs[:, 512:1024], p_kv[1][:, 0:512]),
                       [h_a2])
            h_b3 = P.v(lambda v: v.tensor_scalar(angc[:, :], angs[:, :],
                                                 float(0.5 * np.pi), None,
                                                 op0=ALU.add), [h_b1, h_b2])
            hsin = None
            for which, hb in (("sin", [h_b1, h_b2]), ("cos", [h_b3])):
                src_ang = angs if which == "sin" else angc
                h_t1_ = P.v(lambda v, src_ang=src_ang: v.tensor_scalar(
                    angt[:, :], src_ang[:, :], float(1.0 / TWO_PI), None,
                    op0=ALU.mult), hb + ([hsin] if hsin else []))
                h_t2_ = P.v(lambda v: v.tensor_copy(posi128, angt[:, :]),
                            [h_t1_, h_srv] + prev_u)
                h_t3_ = P.v(lambda v: v.tensor_copy(angt2v[:, :], posi128), [h_t2_])
                h_t4_ = P.v(lambda v: v.tensor_tensor(angt3v[:, :], angt2v[:, :],
                                                      angt[:, :], op=ALU.is_gt),
                            [h_t3_])
                h_t5_ = P.v(lambda v: v.tensor_tensor(angt2v[:, :], angt2v[:, :],
                                                      angt3v[:, :], op=ALU.subtract),
                            [h_t4_])
                h_t6_ = P.v(lambda v, src_ang=src_ang: v.scalar_tensor_tensor(
                    angt[:, :], angt2v[:, :], float(-TWO_PI), src_ang[:, :],
                    op0=ALU.mult, op1=ALU.add), [h_t5_])
                h_t7_ = P.v(lambda v: v.tensor_scalar(angt3v[:, :], angt[:, :],
                                                      float(np.pi), None,
                                                      op0=ALU.is_gt), [h_t6_])
                h_t8_ = P.v(lambda v: v.scalar_tensor_tensor(
                    angt[:, :], angt3v[:, :], float(-TWO_PI), angt[:, :],
                    op0=ALU.mult, op1=ALU.add), [h_t7_])
                hsin = P.s(lambda s, src_ang=src_ang: s.activation(
                    src_ang[:, :], angt[:, :], AF.Sin), [h_t8_, prev_sq2])
                prev_rope.append(hsin)
            h_e1_ = P.v(lambda v: v.tensor_copy(qtok[:, 0:512], qkbanks[0][:, 0:512]),
                        [h_qm[0], hsin] + prev_u +
                        (h_qtok.get(wq_name, [])[-1:] if h_qtok.get(wq_name) else
                         (h_qtok.get("fwq", [])[-1:] if wq_name == "fwk" else [])))
            h_e2_ = P.v(lambda v: v.tensor_copy(qtok[:, 512:1024],
                                                qkbanks[1][:, 0:512]), [h_qm[1]])
            qk_last[0], qk_last[1] = h_e1_, h_e2_
            h_rc1 = P.v(lambda v: v.tensor_copy(
                rcop[:, :].rearrange("p (h a b) -> p h a b", a=2, b=32)[:, :, 0, :],
                qtok[:, :].rearrange("p (h a b) -> p h a b", a=2, b=32)[:, :, 1, :]),
                [h_e1_, h_e2_] + prev_rope[-2:])
            h_rc2 = P.v(lambda v: v.tensor_copy(
                rcop[:, :].rearrange("p (h a b) -> p h a b", a=2, b=32)[:, :, 1, :],
                qtok[:, :].rearrange("p (h a b) -> p h a b", a=2, b=32)[:, :, 0, :]),
                [h_rc1])
            h_rm1 = P.v(lambda v: v.tensor_tensor(rcop[:, :], rcop[:, :], angs[:, :],
                                                  op=ALU.mult), [h_rc2])
            h_rm2 = P.v(lambda v: v.tensor_tensor(qtok[:, :], qtok[:, :], angc[:, :],
                                                  op=ALU.mult), [h_rm1])
            h_rm3 = P.v(lambda v: v.tensor_tensor(qtok[:, :], qtok[:, :], rcop[:, :],
                                                  op=ALU.add), [h_rm2])
            prev_rope.append(h_rm3)
            h_qd = dma_g(dstD[c * 128:(c + 1) * 128, :], qtok[:, :], deps=[h_rm3])
            h_qtok.setdefault(wq_name, []).append(h_qd)
    h_qT = [dma_y(qTb[:, kc * 512:(kc + 1) * 512],
                  qtokD[0:512, kc * 128:(kc + 1) * 128],
                  deps=h_qtok["fwq"] + h_qtok["fwk"] + [prev_rope[-1]], tr=True)
            for kc in range(8)]
    h_kT = [dma_y(kTb[:, kc * 512:(kc + 1) * 512],
                  ktokD[0:512, kc * 128:(kc + 1) * 128],
                  deps=h_qtok["fwk"] + h_m1 + [h_cm_prev, prev_mg], tr=True)
            for kc in range(8)]
    h_vm = []
    for c in range(4):
        for halfn in range(2):
            hmm = None
            for kc in range(8):
                hmm = P.t(lambda t, c=c, halfn=halfn, kc=kc: t.matmul(
                    qkbanks[halfn][:, 0:512],
                    hnT[:, kc * 512 + c * 128:kc * 512 + (c + 1) * 128],
                    wslots["fwv"][:, kc * 1024 + halfn * 512:kc * 1024 + (halfn + 1) * 512],
                    start=(kc == 0), stop=(kc == 7)),
                    (h_hnT + h_wffl["fwv"] + [qk_last[halfn]]) if kc == 0 else [])
            wslot_mm["fwv"].append(hmm)
            hev = P.v(lambda v, c=c, halfn=halfn: v.tensor_copy(
                vselb[:, c * 1024 + halfn * 512:c * 1024 + (halfn + 1) * 512],
                qkbanks[halfn][:, 0:512]), [hmm])
            qk_last[halfn] = hev
            h_vm.append(hev)

    slot_of = ["fwq", "fwq", "fwk", "fwk", "fwv", "fwv", "fwo", "fwo"]
    h_w1l = [dma_g(BIGW[:, kc * 4096:(kc + 1) * 4096],
                   w1D[kc * 128:(kc + 1) * 128, :],
                   deps=wslot_mm[slot_of[kc]]) for kc in range(6)]

    scbanks = [p_qkv[0], p_qkv[1], p_awA, p_awB]
    sc_last = {i: mb_last[i] for i in range(4)}
    obanks = [p_ctxA, p_ctxB, p_kv[0], p_kv[1]]
    ob_last = {0: qk_last[0], 1: qk_last[1], 2: None, 3: None}
    arow2 = trashv[:, 0:512]
    h_adma = {}
    h_oev = []
    aT_prev = [None, None]
    h_exp = [None, None]
    h_norm = [None, None]
    h_adl = [None, None]
    bank_i = 0
    h_scev_all = []

    def emit_o(hd):
        bk = hd % 4
        win = hd * 64 - 64 * (hd % 2)
        h_om = None
        for jc in range(4):
            buf = jc % 2
            dmaf = dma_y if jc % 2 == 0 else dma_s
            h_at = dmaf(aT[:, buf * 512:(buf + 1) * 512],
                        aDr[hd * 512:hd * 512 + 512, jc * 128:(jc + 1) * 128],
                        deps=h_adma[hd] + ([aT_prev[buf]] if aT_prev[buf] else []),
                        tr=True)
            h_om = P.t(lambda t, hd=hd, jc=jc, bk=bk, win=win, buf=buf: t.matmul(
                obanks[bk][:, 0:512],
                vselb[:, jc * 1024 + win:jc * 1024 + win + 128],
                aT[:, buf * 512:(buf + 1) * 512],
                start=(jc == 0), stop=(jc == 3), skip_group_check=True),
                [h_at] + h_vm + ([ob_last[bk]] if ob_last[bk] else []) +
                ([statez_done[-1]] if bk >= 2 else []))
            aT_prev[buf] = h_om
        ro = (hd % 2) * 64
        hoe = P.v(lambda v, hd=hd, bk=bk, ro=ro: v.tensor_copy(
            oT[ro:ro + 64, (hd // 2) * 512:(hd // 2 + 1) * 512],
            obanks[bk][ro:ro + 64, 0:512]), [h_om])
        ob_last[bk] = hoe
        h_oev.append(hoe)

    for hd in range(16):
        kc2, ro = hd // 2, (hd % 2) * 64
        for c in range(4):
            i = hd * 4 + c
            par = i % 2
            stt_b = stat if par == 0 else ustat
            rrs = rr[:, par * 512:(par + 1) * 512]
            arw = arow[:, :] if par == 0 else arow2
            bk = bank_i % 4
            bank_i += 1
            h_scm = P.t(lambda t, c=c, hd=hd, kc2=kc2, ro=ro, bk=bk: t.matmul(
                scbanks[bk][:, 0:512],
                qTb[ro:ro + 64, kc2 * 512 + c * 128:kc2 * 512 + (c + 1) * 128],
                kTb[ro:ro + 64, kc2 * 512:kc2 * 512 + 512],
                start=True, stop=True),
                h_qT + h_kT + [sc_last[bk]])
            h_s1_ = P.v(lambda v, c=c, bk=bk: v.scalar_tensor_tensor(
                scs[:, 0:512], scbanks[bk][:, 0:512], 0.125,
                msel[:, c * 512:(c + 1) * 512], op0=ALU.mult, op1=ALU.add),
                [h_scm, h_msel_ev[c]] +
                ([h_exp[1 - par]] if h_exp[1 - par] else []))
            h_s2_ = P.v(lambda v, stt_b=stt_b: v.tensor_reduce(
                stt_b[:, 6:7], scs[:, 0:512], axis=AX.X, op=ALU.max),
                [h_s1_] + ([h_exp[par]] if h_exp[par] else []))
            h_s3_ = P.v(lambda v, stt_b=stt_b: v.tensor_scalar(
                stt_b[:, 6:7], stt_b[:, 6:7], -1.0, None, op0=ALU.mult), [h_s2_])
            h_s4_ = P.s(lambda s, stt_b=stt_b, rrs=rrs: s.activation(
                rrs, scs[:, 0:512], AF.Exp, bias=stt_b[:, 6:7], scale=1.0,
                accum_out=stt_b[:, 7:8]),
                [h_s3_, prev_sq2] + ([h_norm[par]] if h_norm[par] else []) +
                (prev_u if i == 0 else []))
            h_exp[par] = h_s4_
            h_s5_ = P.v(lambda v, stt_b=stt_b: v.reciprocal(
                stt_b[:, 7:8], stt_b[:, 7:8]), [h_s4_])
            h_s6_ = P.v(lambda v, stt_b=stt_b, rrs=rrs, arw=arw: v.tensor_scalar(
                arw, rrs, stt_b[:, 7:8], None, op0=ALU.mult),
                [h_s5_] + ([h_adl[par]] if h_adl[par] else []))
            h_norm[par] = h_s6_
            h_ad = P.gd(lambda g, c=c, hd=hd, arw=arw: g.dma_start(
                out=aDr[hd * 512 + c * 128:hd * 512 + (c + 1) * 128, :],
                in_=arw), [h_s6_])
            h_adl[par] = h_ad
            h_adma.setdefault(hd, []).append(h_ad)
            h_scev_all.append(h_s6_)
            sc_last[bk] = h_s1_
        if hd >= 1:
            emit_o(hd - 1)
    emit_o(15)
    h_x2a = []
    wo_last = {0: ob_last[0], 1: ob_last[1]}
    for c in range(4):
        for halfn in range(2):
            hmm = None
            for kc in range(8):
                hmm = P.t(lambda t, c=c, halfn=halfn, kc=kc: t.matmul(
                    qkbanks[halfn][:, 0:512],
                    oT[:, kc * 512 + c * 128:kc * 512 + (c + 1) * 128],
                    wslots["fwo"][:, kc * 1024 + halfn * 512:kc * 1024 + (halfn + 1) * 512],
                    start=(kc == 0), stop=(kc == 7)),
                    (h_oev + h_wffl["fwo"] + [wo_last[halfn]]) if kc == 0 else [])
            wslot_mm["fwo"].append(hmm)
            hx2 = P.v(lambda v, c=c, halfn=halfn: v.scalar_tensor_tensor(
                x2a[:, c * 1024 + halfn * 512:c * 1024 + (halfn + 1) * 512],
                qkbanks[halfn][:, 0:512], 1.0,
                xsel_t(c)[:, halfn * 512:(halfn + 1) * 512],
                op0=ALU.mult, op1=ALU.add),
                [hmm, h_xg[c]] + ([prev_rope[-1]] if c == 0 and halfn == 0 else []))
            wo_last[halfn] = hx2
            h_x2a.append(hx2)

    h_hn2 = []
    prev_sq3 = prev_sq2
    for c in range(4):
        h_q1 = P.s(lambda s, c=c: s.activation(trashb[:, :],
                                               x2a[:, c * 1024:(c + 1) * 1024],
                                               AF.Square, accum_out=stat[:, 2:3]),
                   [h_x2a[2 * c], h_x2a[2 * c + 1], prev_sq3])
        prev_sq3 = h_q1
        h_q2 = P.v(lambda v: v.tensor_scalar(stat[:, 3:4], stat[:, 2:3], 1.0 / 1024.0,
                                             None, op0=ALU.mult), [h_q1])
        h_q3 = P.v(lambda v: v.tensor_scalar(stat[:, 3:4], stat[:, 3:4], 1e-6, None,
                                             op0=ALU.add), [h_q2])
        h_q4 = P.s(lambda s: s.activation(stat[:, 4:5], stat[:, 3:4], AF.Sqrt),
                   [h_q3, prev_sq3])
        prev_sq3 = h_q4
        h_q5 = P.v(lambda v: v.reciprocal(stat[:, 4:5], stat[:, 4:5]), [h_q4])
        h_q6 = P.v(lambda v, c=c: v.scalar_tensor_tensor(
            hnt[:, :], x2a[:, c * 1024:(c + 1) * 1024], stat[:, 4:5],
            ln2_rep[:, :], op0=ALU.mult, op1=ALU.mult),
            [h_q5, ld["ln2"]] + ([h_hn2[-1]] if h_hn2 else [h_hn[-1]]))
        h_h2d = dma_g(hn2D[c * 128:(c + 1) * 128, :], hnt[:, :], deps=[h_q6])
        h_hn2.append(h_h2d)
    h_hn2T = [dma_y(hn2T[:, kc * 512:(kc + 1) * 512],
                    hn2D[0:512, kc * 128:(kc + 1) * 128],
                    deps=h_hn2 + h_scev_all[-1:] + [h_mm_prev, prev_sq2],
                    tr=True) for kc in range(8)]
    h_w1l += [dma_g(BIGW[:, kc * 4096:(kc + 1) * 4096],
                    w1D[kc * 128:(kc + 1) * 128, :],
                    deps=wslot_mm["fwo"]) for kc in (6, 7)]
    h_gel = []
    h_gcp = []
    g_last = {0: wo_last[0], 1: wo_last[1], 2: aw_last[0], 3: aw_last[1]}
    g_last = {0: wo_last[0], 1: wo_last[1], 2: sc_last[2], 3: sc_last[3]}
    h1_written = {}
    h_w1mm = []
    for ft in range(32):
        bk = ft % 4
        hmm = None
        for kc in range(8):
            hmm = P.t(lambda t, ft=ft, kc=kc, bk=bk: t.matmul(
                scbanks[bk][:, 0:512],
                BIGW[:, kc * 4096 + ft * 128:kc * 4096 + (ft + 1) * 128],
                hn2T[:, kc * 512:(kc + 1) * 512],
                start=(kc == 0), stop=(kc == 7)),
                (h_hn2T + [g_last[bk], h_w1l[kc]] +
                 (h_kT if ft == 2 else []) + (h_vm if ft == 3 else []))
                if kc == 0 else [h_w1l[kc]])
        gs = (ft % 2) * 512
        hge = P.s(lambda s, ft=ft, bk=bk, gs=gs: s.activation(
            rr[:, gs:gs + 512], scbanks[bk][:, 0:512],
            AF.Gelu_apprx_tanh), [hmm, prev_sq3, prev_rope[-1], h_oev[-1],
                                  h_adma[15][-1], h_adma[15][-2], h_srv,
                                  h_scev_all[-1]] +
            ([h_gel[-1]] if h_gel else []) +
            ([h_gcp[ft - 2]] if ft >= 2 else []))
        hcp = P.v(lambda v, ft=ft, gs=gs: v.tensor_copy(
            h1slot(ft), rr[:, gs:gs + 512]), [hge] + h_uns[-1:])
        g_last[bk] = hge
        h_gcp.append(hcp)
        h_gel.append(hge)
        h_gel.append(hcp)
        h1_written[ft] = hcp
        h_w1mm.append(hmm)
    h_w2l = [dma_g(BIGW[:, fc * 1024:(fc + 1) * 1024], w2D[fc * 128:(fc + 1) * 128, :],
                   deps=[h_w1mm[ft] for ft in range((fc % 4) * 8, (fc % 4) * 8 + 8)])
             for fc in range(32)]
    h_sct = []
    w2_last = {0: g_last[0], 1: g_last[1]}
    prev_tail = list(prev_u) + [prev_rope[-1]]
    for c in range(4):
        h_ps = []
        for halfn in range(2):
            hmm = None
            for fc in range(32):
                hmm = P.t(lambda t, c=c, halfn=halfn, fc=fc: t.matmul(
                    qkbanks[halfn][:, 0:512],
                    h1slot(fc)[:, c * 128:(c + 1) * 128],
                    BIGW[:, fc * 1024 + halfn * 512:fc * 1024 + (halfn + 1) * 512],
                    start=(fc == 0), stop=(fc == 31)),
                    [h_w2l[fc], h1_written[fc]] +
                    ([w2_last[halfn]] if fc == 0 else []))
            h_ps.append(hmm)
        h_v1 = []
        for halfn in range(2):
            h_v1.append(P.v(lambda v, c=c, halfn=halfn: v.scalar_tensor_tensor(
                rr[:, halfn * 512:(halfn + 1) * 512], qkbanks[halfn][:, 0:512], 1.0,
                x2a[:, c * 1024 + halfn * 512:c * 1024 + (halfn + 1) * 512],
                op0=ALU.mult, op1=ALU.add),
                [h_ps[halfn]] + prev_tail + ([h_x2a[-1]] if c == 0 and halfn == 0 else [])))
            w2_last[halfn] = h_v1[halfn]
        h_v2 = []
        for halfn in range(2):
            h_v2.append(P.v(lambda v, c=c, halfn=halfn: v.scalar_tensor_tensor(
                r2f[:, halfn * 512:(halfn + 1) * 512], rr[:, halfn * 512:(halfn + 1) * 512],
                wselc[:, c:c + 1],
                xsel_t(c)[:, halfn * 512:(halfn + 1) * 512],
                op0=ALU.mult, op1=ALU.add, accum_out=stat[:, halfn:halfn + 1]),
                [h_v1[halfn], h_cl2, h_xg[c]] + prev_tail))
        h_w1_ = P.s(lambda s: s.activation(trashb[:, :], r2f[:, :], AF.Square,
                                           accum_out=stat[:, 2:3]),
                    h_v2 + [prev_sq3] + ([h_gel[-1]] if h_gel else []))
        prev_sq3 = h_w1_
        h_w2_ = P.v(lambda v: v.tensor_tensor(stat[:, 0:1], stat[:, 0:1], stat[:, 1:2],
                                              op=ALU.add), h_v2)
        h_w3_ = P.v(lambda v: v.tensor_scalar(stat[:, 0:1], stat[:, 0:1], 1.0 / 1024.0,
                                              None, op0=ALU.mult), [h_w2_])
        h_w4_ = P.v(lambda v: v.tensor_tensor(stat[:, 3:4], stat[:, 0:1], stat[:, 0:1],
                                              op=ALU.mult), [h_w3_])
        h_w5_ = P.v(lambda v: v.tensor_scalar(stat[:, 2:3], stat[:, 2:3], 1.0 / 1024.0,
                                              None, op0=ALU.mult), [h_w1_])
        h_w6_ = P.v(lambda v: v.tensor_tensor(stat[:, 3:4], stat[:, 2:3], stat[:, 3:4],
                                              op=ALU.subtract), [h_w4_, h_w5_])
        h_w7_ = P.v(lambda v: v.tensor_scalar(stat[:, 3:4], stat[:, 3:4], 1e-5, None,
                                              op0=ALU.add), [h_w6_])
        h_w8_ = P.s(lambda s: s.activation(stat[:, 4:5], stat[:, 3:4], AF.Sqrt),
                    [h_w7_, prev_sq3])
        h_w9_ = P.v(lambda v: v.reciprocal(stat[:, 4:5], stat[:, 4:5]), [h_w8_])
        h_wA_ = P.v(lambda v: v.tensor_scalar(rr2[:, :], r2f[:, :], stat[:, 0:1],
                                              stat[:, 4:5], op0=ALU.subtract,
                                              op1=ALU.mult),
                    [h_w9_] + prev_tail + h_uns[-1:] + h_sct[-1:])
        h_sc_ = P.gd(lambda g, c=c: g.indirect_dma_start(
            outD, IndirectOffsetOnAxis(scatcoli[:, c:c + 1], 0), rr2[:, :], None),
            [h_wA_, h_si1, h_si2] + h_uns)
        h_sct.append(h_sc_)
        prev_tail = [h_wA_, h_w8_]
    P.g(lambda g: g.engine_nop(), h_sct)

    with nc.Block() as block, ExitStack() as sstack:
        P.emit(nc, block, sstack)
    ctx.close()
    nc._n_items = len(P.items)
    return nc


def _consts():
    t_grid = (np.arange(32)[None, :] * 128 + np.arange(128)[:, None]).astype(np.float32)
    inv = (1.0 / 10000.0 ** (np.arange(0, 64, 2, dtype=np.float64) / 64.0))
    invs = np.concatenate([-inv, inv])
    invf = np.tile(invs, 16).astype(np.float32)[None, :]
    return {
        "icol": np.arange(1, 129, dtype=np.float32).reshape(128, 1),
        "iota512": np.broadcast_to(np.arange(512, dtype=np.float32), (128, 512)).copy(),
        "thi": np.floor(t_grid / 64).astype(np.float32),
        "tlo": (t_grid % 64).astype(np.float32),
        "tcol": t_grid.copy(),
        "tri": np.tril(np.ones((128, 128)), -1).T.astype(ml_dtypes.bfloat16),
        "idf128": np.eye(128, dtype=np.float32),
        "onesrow": np.ones((1, 128), np.float32),
        "awmask": np.tile(np.triu(np.ones((128, 128), np.float32)), (1, 4)).astype(ml_dtypes.bfloat16),
        "invf": invf,
        "onesm": np.ones((128, 32), np.float32),
    }


def _stage_core(inputs, b):
    rep = lambda v: np.ascontiguousarray(np.broadcast_to(v.reshape(1, -1), (128, v.size)).astype(np.float32))
    pos = np.asarray(inputs["L0_ffn_position_ids"][b]).astype(np.int32)
    posb = np.ascontiguousarray(pos.reshape(32, 128).T)
    m = {
        "x": np.ascontiguousarray(inputs["L0_attn_x"][b], np.float32),
        "wqkv": np.ascontiguousarray(inputs["L0_attn_Wqkv"], np.float32),
        "bqkv": np.ascontiguousarray(inputs["L0_attn_bqkv"], np.float32).reshape(-1, 1),
        "proj": np.ascontiguousarray(inputs["L0_attn_proj"], np.float32),
        "wout": np.ascontiguousarray(inputs["L0_attn_Wout"], np.float32),
        "maskb": np.ascontiguousarray(inputs["L0_ffn_attention_mask"][b, 0], np.float32),
        "posb": posb,
        "rb": np.asarray(inputs["L0_ffn_router_b"], np.float32).reshape(1, 1),
        "fwq": np.ascontiguousarray(inputs["L0_ffn_wq"], np.float32),
        "fwk": np.ascontiguousarray(inputs["L0_ffn_wk"], np.float32),
        "fwv": np.ascontiguousarray(inputs["L0_ffn_wv"], np.float32),
        "fwo": np.ascontiguousarray(inputs["L0_ffn_wo"], np.float32),
        "w1": np.ascontiguousarray(inputs["L0_ffn_w1"], np.float32),
        "w2": np.ascontiguousarray(inputs["L0_ffn_w2"], np.float32),
        "bout_rep": rep(np.asarray(inputs["L0_attn_bout"])),
        "bv_rep": rep(np.asarray(inputs["L0_attn_bqkv"])[2048:]),
        "ln1_rep": rep(np.asarray(inputs["L0_ffn_ln1"])),
        "ln2_rep": rep(np.asarray(inputs["L0_ffn_ln2"])),
        "rw_rep": rep(np.asarray(inputs["L0_ffn_router_w"])[:, 0]),
    }
    m.update(_consts())
    return m


def kernel(**inputs):
    nc = build_nc(debug_stage=0)
    maps = [_stage_core(inputs, c >> 1) for c in range(8)]
    res = run_bass_kernel_spmd(nc, maps, core_ids=list(range(8)))
    out = np.stack([np.asarray(res.results[2 * b]["out"])[:T] for b in range(B)])
    return out


# version 07 (final)
# speedup vs baseline: 1.1521x; 1.0344x over previous
import numpy as np
import ml_dtypes
import concourse.bass as bass
import concourse.mybir as mybir
from concourse.bass_utils import run_bass_kernel_spmd
from concourse.bass_core_rust import IndirectOffsetOnAxis

F32 = mybir.dt.float32
BF16 = mybir.dt.bfloat16
I32 = mybir.dt.int32
U32 = mybir.dt.uint32
AF = mybir.ActivationFunctionType
ALU = mybir.AluOpType
AX = mybir.AxisListType

B, T, C = 4, 4096, 1024
H, M, CHUNK, D = 16, 64, 128, 64
NCH = T // CHUNK
KSEL = 511
FF = 4096
DELTA = 1e-6
PASS_T = 512
NPASS = T // PASS_T
PCH = PASS_T // CHUNK
NITER = 6
TWO_PI = float(2.0 * np.pi)


class Prog:

    CLASSES = ("t", "v", "s", "g", "gd", "yd", "sd")
    ENG = {"t": "tensor", "v": "vector", "s": "scalar", "g": "gpsimd",
           "gd": "gpsimd", "yd": "sync", "sd": "scalar"}
    DMA = ("gd", "yd", "sd")
    R = 16

    def __init__(self):
        self.items = []
        self.counts = {c: 0 for c in self.CLASSES}
        self.trunc = None
        self.dummyS = None
        self.dummyD = None

    def _add(self, cls, fn, deps):
        deps = tuple(d for d in deps if d is not None)
        self.counts[cls] += 1
        h = (cls, self.counts[cls])
        self.items.append((cls, fn, deps, h, len(self.items)))
        return h

    def t(self, fn, deps=()): return self._add("t", fn, deps)
    def v(self, fn, deps=()): return self._add("v", fn, deps)
    def s(self, fn, deps=()): return self._add("s", fn, deps)
    def g(self, fn, deps=()): return self._add("g", fn, deps)
    def gd(self, fn, deps=()): return self._add("gd", fn, deps)
    def yd(self, fn, deps=()): return self._add("yd", fn, deps)
    def sd(self, fn, deps=()): return self._add("sd", fn, deps)

    def _semkey(self, h):
        cls, idx = h
        if cls in self.DMA:
            return (cls, (idx - 1) % self.R), 16 * ((idx - 1) // self.R + 1)
        return (cls, 0), idx

    def emit(self, nc, block, stack):
        sems = {}
        for c in self.CLASSES:
            if c in self.DMA:
                for r in range(self.R):
                    sems[(c, r)] = stack.enter_context(nc.semaphore(f"pr_{c}{r}"))
            else:
                sems[(c, 0)] = stack.enter_context(nc.semaphore(f"pr_{c}"))
        per_eng = {}
        for it in self.items:
            per_eng.setdefault(self.ENG[it[0]], []).append(it)

        def run(eng_name, eng):
            waited = {}
            for cls, fn, deps, h, seq in per_eng.get(eng_name, []):
                if cls in self.DMA:
                    key, val = self._semkey(h)
                    prev = val - 16
                    if prev > 0 and waited.get(key, -1) < prev:
                        eng.wait_ge(sems[key], prev)
                        waited[key] = prev
                for d in deps:
                    key, val = self._semkey(d)
                    if waited.get(key, -1) >= val:
                        continue
                    eng.wait_ge(sems[key], val)
                    waited[key] = val
                key, val = self._semkey(h)
                if self.trunc is not None and seq >= self.trunc:
                    if cls in self.DMA:
                        fn = lambda e: e.dma_start(out=self.dummyS[0:1, 0:4],
                                                   in_=self.dummyD[0:1, 0:4])
                    else:
                        fn = lambda e: e.nop()
                fn(eng).then_inc(sems[key], 16 if cls in self.DMA else 1)

        @block.gpsimd
        def _(g): run("gpsimd", g)

        @block.sync
        def _(y): run("sync", y)

        @block.vector
        def _(v): run("vector", v)

        @block.tensor
        def _(t): run("tensor", t)

        @block.scalar
        def _(s): run("scalar", s)

        @block.scalar
        def _(s):
            for c in self.CLASSES:
                n = self.counts[c]
                if not n:
                    continue
                if c in self.DMA:
                    for r in range(self.R):
                        cnt = (n - 1 - r) // self.R + 1 if r < n else 0
                        if r < min(n, self.R) and cnt > 0:
                            s.wait_ge(sems[(c, r)], 16 * cnt)
                else:
                    s.wait_ge(sems[(c, 0)], n)


def build_nc(debug_stage=0, dbg=False, trunc=None):
    nc = bass.Bass("TRN2", target_bir_lowering=False, debug=dbg)
    from contextlib import ExitStack
    ctx = ExitStack()

    din = lambda n, shape, dt=F32: nc.dram_tensor(n, shape, dt, kind="ExternalInput").ap()
    dscr = lambda n, shape, dt: nc.dram_tensor(n, shape, dt).ap()
    sb = lambda n, shape, dt=F32: ctx.enter_context(nc.sbuf_tensor(n, shape, dt))
    psum = lambda n, shape: ctx.enter_context(nc.psum_tensor(n, shape, F32))

    xD = din("x", [T, C])
    wqkvD = din("wqkv", [3 * C, C])
    bqkvD = din("bqkv", [3 * C, 1])
    projD = din("proj", [H, D, M])
    woutD = din("wout", [C, C])
    maskD = din("maskb", [T, T])
    posD = din("posb", [128, 32], I32)
    rbD = din("rb", [1, 1])
    fwqD = din("fwq", [C, C])
    fwkD = din("fwk", [C, C])
    fwvD = din("fwv", [C, C])
    fwoD = din("fwo", [C, C])
    w1D = din("w1", [C, FF])
    w2D = din("w2", [FF, C])
    bout_repD = din("bout_rep", [128, C])
    bv_repD = din("bv_rep", [128, C])
    ln1_repD = din("ln1_rep", [128, C])
    ln2_repD = din("ln2_rep", [128, C])
    rw_repD = din("rw_rep", [128, C])
    icolD = din("icol", [128, 1])
    iota512D = din("iota512", [128, 512])
    thiD = din("thi", [128, 32])
    tloD = din("tlo", [128, 32])
    tcolD = din("tcol", [128, 32])
    triD = din("tri", [128, 128], BF16)
    idf128D = din("idf128", [128, 128])
    onesrowD = din("onesrow", [1, 128])
    awmaskD = din("awmask", [128, 512], BF16)
    invfD = din("invf", [1, C])
    onesmD = din("onesm", [128, 32])

    outD = nc.dram_tensor("out", [T + 1, C], F32, kind="ExternalOutput").ap()

    xbfD = dscr("xbf", [T, C], BF16)
    wvbfD = dscr("wvbf", [C, C], BF16)
    woutbfD = dscr("woutbf", [C, C], BF16)
    kpxD = dscr("kpx_d", [2 * C, T], BF16)
    hD = dscr("h_d", [T, C], BF16)
    x1D = dscr("x1_d", [T, C], BF16)
    wwD = dscr("ww_d", [1, 4096], F32)
    sel8D = dscr("sel8_d", [8, 512], F32)
    rowsD = dscr("rows_d", [4, 512], F32)
    m1D = dscr("m1_d", [512, T], BF16)
    hnD = dscr("hn_d", [512, C], BF16)
    qtokD = dscr("qtok_d", [512, C], BF16)
    ktokD = dscr("ktok_d", [512, C], BF16)
    hn2D = dscr("hn2_d", [512, C], BF16)
    aDr = dscr("a_d", [H * 512, 512], BF16)

    P = Prog()
    P.trunc = trunc
    P.dummyD = dscr("dummy_d", [1, 4], F32)

    BIGW = sb("BIGW", [128, 32768], BF16)
    P.dummyS = sb("dummy_s", [1, 4], F32)
    Wqp = BIGW[:, 0:8192]
    Wkp = BIGW[:, 8192:16384]
    WvT = BIGW[:, 16384:24576]
    WoutT = BIGW[:, 24576:32768]
    qpT = sb("qpT", [128, 8 * PASS_T], BF16)
    kpT = sb("kpT", [128, 8 * PASS_T], BF16)
    vst = sb("vst", [128, PCH * 1040], BF16)
    xTb = [sb("xT0", [128, 8 * 512], BF16),
           sb("xT1", [128, 8 * 512], BF16)]
    xT = xTb[0]
    kptok = sb("kptok", [128, 2048], BF16)
    state_z = sb("state_z", [128, 16 * 65], BF16)
    awb = sb("awb", [128, 2048], BF16)
    hh = sb("hh", [128, 1024], BF16)
    bqp = sb("bqp", [128, 8])
    bkp = sb("bkp", [128, 8])
    kspart = sb("kspart", [128, 8 * PCH])
    ksrun = sb("ksrun", [128, 8])
    recn = sb("recn", [128, 16])
    rr = sb("rr", [128, 1024])
    stat = sb("stat", [128, 8])
    ustat = sb("ustat", [128, 8])
    x1b = sb("x1b", [128, 1024], BF16)
    wacc = sb("wacc", [128, 32])
    trashb = sb("trashb", [128, 1024], BF16)
    trashv = sb("trashv", [128, 1024], BF16)
    zbf = sb("zbf", [128, 512], BF16)
    bout_rep = sb("bout_rep_s", [128, C])
    bv_rep = sb("bv_rep_s", [128, C])
    ln1_rep = sb("ln1_rep_s", [128, C])
    ln2_rep = sb("ln2_rep_s", [128, C])
    rw_rep = sb("rw_rep_s", [128, C])
    icol = sb("icol_s", [128, 1])
    iota512 = sb("iota512_s", [128, 512])
    thi = sb("thi_s", [128, 32])
    tlo = sb("tlo_s", [128, 32])
    tcol = sb("tcol_s", [128, 32])
    tri = sb("tri_s", [128, 128], BF16)
    idf128 = sb("idf128_s", [128, 128])
    onesrow = sb("onesrow_s", [1, 128])
    awmask = sb("awmask_s", [128, 512], BF16)
    invf = sb("invf_s", [1, C])
    neglarge = sb("neglarge", [1, 128])
    poslarge = sb("poslarge", [1, 128])
    czero = sb("czero", [128, 1])
    bstg = sb("bstg", [128, 1], BF16)
    brow = sb("brow", [1, 128])
    lodc = sb("lodc", [128, 2])
    pos_s = sb("pos_s", [128, 32], I32)
    wrepx = None
    cmpt = sb("cmpt", [128, 4096], BF16)
    cnt2 = sb("cnt2", [128, 2])
    tpc = sb("tpc", [1, 128])
    tpp = sb("tpp", [1, 128])
    st8 = sb("st8", [1, 8])
    probes = sb("probes", [128, 1])
    r1 = sb("r1", [1, 128])
    r1m = sb("r1m", [1, 128], U32)
    m01 = sb("m01", [128, 32])
    scan_t = sb("scan_t", [128, 32])
    offs = sb("offs", [128, 1])
    ordp = sb("ordp", [128, 32])
    meta = sb("meta", [128, 192], BF16)
    po = sb("po", [128, 512], BF16)
    sel8 = sb("sel8_s", [8, 512])
    selrow = sb("selrow", [1, 512])
    selrow2 = sb("selrow2", [1, 512])
    selrow3 = sb("selrow3", [1, 512])
    posrow = sb("posrow", [1, 512])
    wrowt = sb("wrowt", [1, 512])
    scatrow = sb("scatrow", [1, 512])
    selcol = sb("selcol", [128, 4])
    selcoli = sb("selcoli", [128, 4], I32)
    scatcoli = sb("scatcoli", [128, 4], I32)
    wselc = sb("wselc", [128, 4])
    selrep = sb("selrep", [128, 512])
    selrep_bfv = selrep[:, :].bitcast(BF16)
    posi2 = sb("posi2", [128, 32], I32)
    posf = sb("posf", [128, 32])
    tmp32 = sb("tmp32", [128, 32])
    onesm = sb("onesm_s", [128, 32])
    SCRATCH4 = sb("SCRATCH4", [128, 4096])
    qTb = SCRATCH4[:, 0:2048].bitcast(BF16)
    vselb = SCRATCH4[:, 2048:4096].bitcast(BF16)
    wrepx = SCRATCH4
    m1f = SCRATCH4
    x2a = SCRATCH4
    scs = sb("scs", [128, 512])
    scs_bfv = scs[:, :].bitcast(BF16)
    
    arow = sb("arow", [128, 512], BF16)
    aT = sb("aT", [128, 1024], BF16)
    kTb = cmpt
    rcop = hh
    hnt = hh
    qtok = x1b
    rr2 = bout_rep
    r2f = bv_rep
    msel = vst[:, 0:2048].bitcast(BF16) if False else None
    xsel = None

    p_qkv = [psum(f"p_qkv{i}", [128, 512]) for i in range(2)]
    p_awA = psum("p_awA", [128, 512])
    p_awB = psum("p_awB", [128, 512])
    p_ctxA = psum("p_ctxA", [128, 512])
    p_ctxB = psum("p_ctxB", [128, 512])
    p_kv = [psum(f"p_kv{i}", [128, 512]) for i in range(2)]

    def dma_g(dst, src, deps=(), allow_slow_non_contiguous=False):
        if allow_slow_non_contiguous:
            return P.gd(lambda g: g.dma_start(out=dst, in_=src,
                                              allow_slow_non_contiguous=True), deps)
        return P.gd(lambda g: g.dma_start(out=dst, in_=src), deps)

    def dma_y(dst, src, deps=(), tr=False):
        if tr:
            return P.yd(lambda y: y.dma_start(out=dst, in_=src, transpose=True), deps)
        return P.yd(lambda y: y.dma_start(out=dst, in_=src), deps)

    def dma_s(dst, src, deps=(), tr=False):
        if tr:
            return P.sd(lambda s: s.dma_start(out=dst, in_=src, transpose=True), deps)
        return P.sd(lambda s: s.dma_start(out=dst, in_=src), deps)

    ld = {}
    for name, dst, src in [
        ("icol", icol, icolD), ("iota512", iota512, iota512D), ("thi", thi, thiD),
        ("tlo", tlo, tloD), ("tcol", tcol, tcolD), ("tri", tri, triD),
        ("idf128", idf128, idf128D), ("onesrow", onesrow, onesrowD),
        ("awmask", awmask, awmaskD), ("invf", invf, invfD),
        ("bout", bout_rep, bout_repD), ("bv", bv_rep, bv_repD),
        ("ln1", ln1_rep, ln1_repD), ("ln2", ln2_rep, ln2_repD),
        ("rw", rw_rep, rw_repD), ("pos", pos_s, posD), ("onesm", onesm, onesmD),
    ]:
        ld[name] = dma_g(dst[:, :], src[:, :])
    h_cz = P.v(lambda v: v.memset(czero[:, :], 0.0))
    h_zb = P.v(lambda v: v.memset(zbf[:, :], 0.0))
    h_sz = P.v(lambda v: v.memset(state_z[:, :], 0.0))
    h_ks0 = P.v(lambda v: v.memset(ksrun[:, :], 0.0))
    P.v(lambda v: v.memset(neglarge[0:1, :], -1e30))
    P.v(lambda v: v.memset(poslarge[0:1, :], 1e30))

    h_xbf = [dma_g(xbfD[pz * PASS_T:(pz + 1) * PASS_T, :],
                   xD[pz * PASS_T:(pz + 1) * PASS_T, :]) for pz in range(NPASS)]
    h_wvb = dma_g(wvbfD[:, :], wqkvD[2 * C:3 * C, :])
    h_wob = dma_g(woutbfD[:, :], woutD[:, :])

    h_kpz = []
    for mt in range(16):
        for j in range(8):
            h_kpz.append(dma_g(
                kpxD[mt * 128 + 64:mt * 128 + 128, j * 512:(j + 1) * 512],
                zbf[0:64, :], deps=[h_zb]))

    stg = BIGW[:, 16384:32768]
    h_wqk = [dma_g(stg[:, r * 1024:(r + 1) * 1024],
                   wqkvD[r * 128:(r + 1) * 128, :]) for r in range(16)]
    projb = rr
    projb = awb
    h_projb = dma_g(projb[0:64, 0:1024].rearrange("d (h m) -> d h m", h=16),
                    projD.rearrange("h d m -> d h m"))
    bd = kptok
    h_bd0 = P.v(lambda v: v.memset(bd[:, 0:1024], 0.0))
    h_bdc = []
    for p in range(8):
        h_bdc.append(P.v(lambda v, p=p: v.tensor_copy(
            bd[0:64, p * 128:p * 128 + 64], projb[0:64, (2 * p) * 64:(2 * p + 1) * 64]),
            [h_bd0, h_projb]))
        h_bdc.append(P.v(lambda v, p=p: v.tensor_copy(
            bd[64:128, p * 128 + 64:p * 128 + 128],
            projb[0:64, (2 * p + 1) * 64:(2 * p + 2) * 64]), [h_bd0, h_projb]))
    h_fold = []
    prev_cp = None
    for wi in range(2):
        Wdst = Wqp if wi == 0 else Wkp
        for pr in range(8):
            for kc in range(8):
                hm = P.t(lambda t, wi=wi, pr=pr, kc=kc: t.matmul(
                    p_qkv[0][:, 0:128],
                    stg[:, (wi * 8 + pr) * 1024 + kc * 128:(wi * 8 + pr) * 1024 + (kc + 1) * 128],
                    bd[:, pr * 128:(pr + 1) * 128], start=True, stop=True),
                    [h_wqk[wi * 8 + pr], prev_cp] + (h_bdc if (wi == 0 and pr == 0 and kc == 0) else []))
                prev_cp = P.v(lambda v, Wdst=Wdst, pr=pr, kc=kc: v.tensor_copy(
                    Wdst[:, kc * 1024 + pr * 128:kc * 1024 + (pr + 1) * 128],
                    p_qkv[0][:, 0:128]), [hm])
                h_fold.append(prev_cp)
    h_bias = []
    prev_hm2 = None
    for bi in range(2):
        bdst = bqp if bi == 0 else bkp
        row0 = bi * 1024
        for pr in range(8):
            hb1 = dma_g(bstg[:, 0:1], bqkvD[row0 + pr * 128:row0 + (pr + 1) * 128, 0:1],
                        deps=([h_bias[-1], prev_hm2] if h_bias else []))
            hm = P.t(lambda t, pr=pr: t.matmul(p_qkv[1][0:1, 0:128], bstg[:, 0:1],
                                               bd[:, pr * 128:(pr + 1) * 128],
                                               start=True, stop=True),
                     [hb1, prev_hm2] + h_bdc)
            hv = P.v(lambda v: v.tensor_copy(brow[0:1, :], p_qkv[1][0:1, 0:128]),
                     [hm] + ([h_bias[-1]] if h_bias else []))
            hm2 = P.t(lambda t: t.matmul(p_qkv[1][:, 128:129], brow[0:1, :],
                                         idf128[0:1, 0:1], start=True, stop=True),
                      [hv, ld["idf128"]])
            prev_hm2 = hm2
            h_bias.append(P.v(lambda v, bdst=bdst, pr=pr: v.tensor_copy(
                bdst[:, pr:pr + 1], p_qkv[1][:, 128:129]), [hm2]))
    fold_done = h_fold + h_bias
    last_bias_mm = prev_hm2

    h_WvT, h_WoutT = [], []
    for kc in range(8):
        h_WvT.append(dma_y(WvT[:, kc * 1024:(kc + 1) * 1024],
                           wvbfD[:, kc * 128:(kc + 1) * 128],
                           deps=[h_wvb] + fold_done, tr=True))
        h_WoutT.append(dma_y(WoutT[:, kc * 1024:(kc + 1) * 1024],
                             woutbfD[:, kc * 128:(kc + 1) * 128],
                             deps=[h_wob] + fold_done, tr=True))

    h_rb = dma_g(brow[0:1, 0:1], rbD[0:1, 0:1], deps=[last_bias_mm, h_bias[-1]])
    h_rbm = P.t(lambda t: t.matmul(p_qkv[1][:, 129:130], onesrow[0:1, :],
                                   brow[0:1, 0:1], start=True, stop=True),
                [h_rb, ld["onesrow"]] + h_bias)
    h_rbc = P.v(lambda v: v.tensor_copy(lodc[:, 1:2], p_qkv[1][:, 129:130]), [h_rbm])

    vcol = lambda cc, hd: cc * 1040 + hd * 65
    urr = SCRATCH4[:, 0:1024]
    h_uns = []
    prev_uh = []
    h_x1_done = []
    h_router = []
    bank_last = [h_fold[-1], h_rbc]
    aw_last = [None, None]
    awb_readers = [[h_bdc[-1], h_projb], [h_bdc[-1], h_projb]]
    prev_div = []
    prev_hst = None
    prev_sq = None
    prev_m7 = None
    prev_hr = [None, None]
    first_in_kv = [True, True]
    statez_done = [h_sz]
    prev_pass = []
    prev_ap = None

    wout_mms_hist = []
    kpacc_last_hist = []
    kpst_hist = []
    last_ksu = None
    def emit_Q(pz):
        nonlocal prev_sq, last_ksu
        t0 = pz * PASS_T
        xT = xTb[pz % 2]
        h_vones = P.v(lambda v: v.memset(
            vst[:, :].rearrange("p (cc h x) -> p cc h x", cc=PCH, h=H)[:, :, :, 64:65],
            1.0), statez_done[-1:])
        hxt = [dma_y(xT[:, kc * 512:(kc + 1) * 512],
                     xbfD[t0:t0 + 512, kc * 128:(kc + 1) * 128],
                     deps=[h_xbf[pz]] + (wout_mms_hist[pz - 2] if pz >= 2 else []),
                     tr=True) for kc in range(8)]
        qkv_done = []
        kp_evacs = []
        kp_accs = []
        pq = 0
        for wi in range(2):
            W, bias, dst = (Wqp, bqp, qpT) if wi == 0 else (Wkp, bkp, kpT)
            for mi in range(8):
                hmm = None
                for kc in range(8):
                    hmm = P.t(lambda t, W=W, mi=mi, kc=kc, pq=pq, xT=xT: t.matmul(
                        p_qkv[pq % 2][:, :],
                        W[:, kc * 1024 + mi * 128:kc * 1024 + (mi + 1) * 128],
                        xT[:, kc * 512:(kc + 1) * 512],
                        start=(kc == 0), stop=(kc == 7)),
                        (hxt + fold_done + [bank_last[pq % 2]])
                        if kc == 0 else [])
                hev = P.v(lambda v, dst=dst, bias=bias, mi=mi, pq=pq: v.tensor_scalar(
                    dst[:, mi * PASS_T:(mi + 1) * PASS_T], p_qkv[pq % 2][:, :],
                    bias[:, mi:mi + 1], 0.0, op0=ALU.add, op1=ALU.max),
                    [hmm] + h_bias +
                    ((kpst_hist[pz - 1] + kpacc_last_hist[pz - 1])
                     if (wi == 1 and pz) else []))
                qkv_done.append(hev)
                bank_last[pq % 2] = hev
                if wi == 1:
                    kp_evacs.append(hev)
                    for j in range(PCH):
                        hacc = P.s(lambda s, dst=dst, mi=mi, j=j: s.activation(
                            trashb[:, 0:128],
                            dst[:, mi * PASS_T + j * 128:mi * PASS_T + (j + 1) * 128],
                            AF.Copy, bias=0.0, scale=1.0,
                            accum_out=kspart[:, mi * PCH + j:mi * PCH + j + 1]),
                            [hev] + ([last_ksu] if last_ksu else []) +
                            ([prev_sq] if prev_sq else []))
                        prev_sq = hacc
                        kp_accs.append(hacc)
                        qkv_done.append(hacc)
                pq += 1
        kpacc_last_hist.append([prev_sq])
        for tt_ in range(PCH):
            for halfn in range(2):
                hmm = None
                for kc in range(8):
                    hmm = P.t(lambda t, tt_=tt_, halfn=halfn, kc=kc, pq=pq,
                              xT=xT: t.matmul(
                        p_qkv[pq % 2][:, :],
                        xT[:, kc * 512 + tt_ * 128:kc * 512 + (tt_ + 1) * 128],
                        WvT[:, kc * 1024 + halfn * 512:kc * 1024 + (halfn + 1) * 512],
                        start=(kc == 0), stop=(kc == 7)),
                        (hxt + h_WvT + [bank_last[pq % 2]]) if kc == 0 else [])
                hev = P.v(lambda v, tt_=tt_, halfn=halfn, pq=pq: v.scalar_tensor_tensor(
                    vst[:, :].rearrange("p (cc h x) -> p cc h x", cc=PCH, h=H)[
                        :, tt_, halfn * 8:(halfn + 1) * 8, 0:64],
                    p_qkv[pq % 2][:, :].rearrange("p (h x) -> p h x", h=8),
                    1.0,
                    bv_rep[:, halfn * 512:(halfn + 1) * 512].rearrange("p (h x) -> p h x", h=8),
                    op0=ALU.mult, op1=ALU.add), [hmm, h_vones, ld["bv"]])
                qkv_done.append(hev)
                bank_last[pq % 2] = hev
                pq += 1
        h_kpst = []
        for mt in range(8):
            dst = kpxD[mt * 256:(mt + 1) * 256, t0:t0 + PASS_T].rearrange(
                "(two sub) t -> two sub t", sub=128)[:, 0:64, :]
            src = kpT[:, mt * PASS_T:(mt + 1) * PASS_T].rearrange(
                "p (two t) -> p two t", two=1)[:, 0, :].rearrange(
                "p t -> p t")
            h_kpst.append(dma_g(dst, kpT[:, mt * PASS_T:(mt + 1) * PASS_T].rearrange(
                "p t -> p t"), deps=kp_evacs))
        kpst_hist.append(h_kpst)
        return dict(xT=xT, t0=t0, hxt=hxt, qkv_done=qkv_done,
                    kp_evacs=kp_evacs, kp_accs=kp_accs, h_kpst=h_kpst)

    Qs = emit_Q(0)
    for pz in range(NPASS):
        Q = Qs
        t0 = Q['t0']; xT = Q['xT']; qkv_done = Q['qkv_done']
        kp_accs = Q['kp_accs']; h_kpst = Q['h_kpst']

        pass_consumers = []
        h_hst_all = []
        for cc in range(PCH):
            ch = pz * PCH + cc
            ct0 = cc * 128
            hkptok = dma_y(kptok[:, :], kpxD[:, t0 + ct0:t0 + ct0 + 128],
                           deps=h_kpst + h_kpz + [statez_done[-1]] +
                           ([h_rbc, h_rbm] if ch == 0 else []), tr=True)
            h_ksu = P.v(lambda v, cc=cc: v.tensor_tensor(
                ksrun[:, :], ksrun[:, :], kspart[:, cc::PCH], op=ALU.add),
                kp_accs + [statez_done[-1]] + ([h_ks0] if ch == 0 else []))
            last_ksu = h_ksu
            h_div = []
            statez_prev = list(statez_done)
            h_stm = []
            for rnd in range(2):
                h0 = rnd * 8
                h_awm = []
                for k in range(8):
                    hd = h0 + k
                    mt, ro = hd // 2, (hd % 2) * 64
                    bankp = p_awA if (k % 2 == 0) else p_awB
                    slot = (k // 2) * 128
                    h_awm.append(P.t(lambda t, mt=mt, ro=ro, bankp=bankp, slot=slot,
                                     ct0=ct0: t.matmul(
                        bankp[:, slot:slot + 128],
                        kpT[ro:ro + 64, mt * PASS_T + ct0:mt * PASS_T + ct0 + 128],
                        qpT[ro:ro + 64, mt * PASS_T + ct0:mt * PASS_T + ct0 + 128],
                        start=True, stop=True),
                        (qkv_done if (k == 0 and rnd == 0) else []) +
                        ([aw_last[k % 2]] if k < 2 else [])))
                h_awe = []
                for bi, bankp in enumerate((p_awA, p_awB)):
                    h_awe.append(P.v(lambda v, bankp=bankp, h0=h0, bi=bi: v.tensor_tensor(
                        awb[:, :].rearrange("p (b x) -> p b x", x=128)[
                            :, h0 + bi:h0 + 8:2, :],
                        bankp[:, :].rearrange("p (b x) -> p b x", x=128),
                        awmask[:, :].rearrange("p (b x) -> p b x", x=128),
                        op=ALU.mult), h_awm + [ld["awmask"]] + awb_readers[rnd]))
                for bi in range(2):
                    aw_last[bi] = h_awe[bi]
                h_ctxm = []
                for k in range(8):
                    hd = h0 + k
                    pcx = p_ctxA if k < 4 else p_ctxB
                    slot = (k % 4) * 65
                    mt = hd // 2
                    h_in = P.t(lambda t, hd=hd, cc=cc, pcx=pcx, slot=slot: t.matmul(
                        pcx[:, slot:slot + 65],
                        awb[:, hd * 128:(hd + 1) * 128],
                        vst[:, vcol(cc, hd):vcol(cc, hd) + 65],
                        start=True, stop=False),
                        h_awe + (h_div[-2:] if h_div else prev_div))
                    h_hi = P.t(lambda t, mt=mt, ct0=ct0, hd=hd, pcx=pcx, slot=slot:
                               t.matmul(
                        pcx[:, slot:slot + 65],
                        qpT[:, mt * PASS_T + ct0:mt * PASS_T + ct0 + 128],
                        state_z[:, hd * 65:(hd + 1) * 65],
                        start=False, stop=True), [h_in] + statez_prev)
                    h_ctxm.append(h_hi)
                awb_readers[rnd] = [h_ctxm[-1]]
                h_rec1 = P.v(lambda v, rnd=rnd: v.tensor_scalar(
                    recn[:, rnd * 8:rnd * 8 + 4], p_ctxA[:, 64:260:65],
                    float(DELTA), None, op0=ALU.add), h_ctxm + prev_div)
                h_rec2 = P.v(lambda v, rnd=rnd: v.tensor_scalar(
                    recn[:, rnd * 8 + 4:rnd * 8 + 8], p_ctxB[:, 64:260:65],
                    float(DELTA), None, op0=ALU.add), h_ctxm + prev_div)
                h_rec3 = P.v(lambda v, rnd=rnd: v.reciprocal(
                    recn[:, rnd * 8:rnd * 8 + 8], recn[:, rnd * 8:rnd * 8 + 8]),
                    [h_rec1, h_rec2])
                for k in range(8):
                    hd = h0 + k
                    pcx = p_ctxA if k < 4 else p_ctxB
                    slot = (k % 4) * 65
                    hdv = P.v(lambda v, hd=hd, pcx=pcx, slot=slot: v.tensor_scalar(
                        hh[:, hd * 64:(hd + 1) * 64], pcx[:, slot:slot + 64],
                        recn[:, hd:hd + 1], None, op0=ALU.mult),
                        [h_rec3, prev_hst])
                    h_div.append(hdv)
                for k in range(8):
                    hd = h0 + k
                    bank = hd // 8
                    hst = P.t(lambda t, hd=hd, cc=cc, bank=bank,
                              stf=first_in_kv[bank]: t.matmul(
                        p_kv[bank][:, (hd % 8) * 64:(hd % 8 + 1) * 64],
                        kptok[:, hd * 128 - 64 * (hd % 2):hd * 128 - 64 * (hd % 2) + 128],
                        vst[:, vcol(cc, hd):vcol(cc, hd) + 64],
                        start=stf, stop=(ch == NCH - 1 and hd % 8 == 7),
                        skip_group_check=True),
                        [hkptok] + (h_ctxm if k == 0 and rnd == 1 else []) +
                        (statez_prev if ch else []))
                    first_in_kv[bank] = False
                    h_stm.append(hst)
            h_szc = []
            for bank in range(2):
                h_szc.append(P.v(lambda v, bank=bank: v.tensor_copy(
                    state_z[:, :].rearrange("p (hd x) -> p hd x", x=65)[
                        :, bank * 8:(bank + 1) * 8, 0:64],
                    p_kv[bank][:, :].rearrange("p (hd x) -> p hd x", x=64)),
                    h_stm + statez_prev))
            h_szc.append(P.v(lambda v: v.tensor_copy(
                state_z[0:64, :].rearrange("p (hd x) -> p hd x", x=65)[:, 0::2, 64:65],
                ksrun[0:64, :].unsqueeze(2)), h_stm + [h_ksu]))
            h_szc.append(P.v(lambda v: v.tensor_copy(
                state_z[64:128, :].rearrange("p (hd x) -> p hd x", x=65)[:, 1::2, 64:65],
                ksrun[64:128, :].unsqueeze(2)), h_stm + [h_ksu]))
            statez_done = h_szc
            h_hst = dma_g(hD[t0 + ct0:t0 + ct0 + 128, :], hh[:, :], deps=h_div)
            h_hst_all.append(h_hst)
            prev_div = h_div[-2:]
            prev_hst = h_hst
            pass_consumers.append(h_hst)
            pass_consumers.extend(h_stm)
            pass_consumers.append(h_ksu)

        if pz < NPASS - 1:
            Qs = emit_Q(pz + 1)
        hT = xT
        h_hT = [dma_y(hT[:, kc * 512:(kc + 1) * 512],
                      hD[t0:t0 + 512, kc * 128:(kc + 1) * 128],
                      deps=h_hst_all + qkv_done, tr=True) for kc in range(8)]
        h_wout_mms = []
        for tt_ in range(PCH):
            ct0 = tt_ * 128
            h_x = dma_g(rr[:, :], xD[t0 + ct0:t0 + ct0 + 128, :],
                        deps=[d for d in (prev_ap, prev_sq) if d])
            h_wo = []
            for halfn in range(2):
                hmm = None
                for kc in range(8):
                    hmm = P.t(lambda t, tt_=tt_, halfn=halfn, kc=kc, hT=hT: t.matmul(
                        p_qkv[halfn][:, :],
                        hT[:, kc * 512 + tt_ * 128:kc * 512 + (tt_ + 1) * 128],
                        WoutT[:, kc * 1024 + halfn * 512:kc * 1024 + (halfn + 1) * 512],
                        start=(kc == 0), stop=(kc == 7)),
                        (h_hT + h_WoutT + h_stm +
                         [d for d in (prev_hr[halfn], bank_last[halfn]) if d])
                        if kc == 0 else [])
                h_wo.append(hmm)
                h_wout_mms.append(hmm)
            h_xb = P.v(lambda v: v.tensor_tensor(rr[:, :], rr[:, :], bout_rep[:, :],
                                                 op=ALU.add), [h_x, ld["bout"]])
            h_r = []
            for halfn in range(2):
                h_r.append(P.v(lambda v, halfn=halfn: v.scalar_tensor_tensor(
                    rr[:, halfn * 512:(halfn + 1) * 512], p_qkv[halfn][:, :], 1.0,
                    rr[:, halfn * 512:(halfn + 1) * 512], op0=ALU.mult, op1=ALU.add,
                    accum_out=stat[:, halfn:halfn + 1]), [h_xb, h_wo[halfn]]))
            prev_hr = list(h_r)
            for halfn in range(2):
                bank_last[halfn] = h_r[halfn]
            h_sq = P.s(lambda s: s.activation(trashb[:, :], rr[:, :], AF.Square,
                                              accum_out=stat[:, 2:3]),
                       h_r + [d for d in (prev_sq,) if d])
            prev_sq = h_sq
            h_m2 = P.v(lambda v: v.tensor_tensor(stat[:, 0:1], stat[:, 0:1],
                                                 stat[:, 1:2], op=ALU.add), h_r)
            h_m2b = P.v(lambda v: v.tensor_scalar(stat[:, 0:1], stat[:, 0:1],
                                                  1.0 / 1024.0, None, op0=ALU.mult),
                        [h_m2])
            h_m3 = P.v(lambda v: v.tensor_tensor(stat[:, 3:4], stat[:, 0:1],
                                                 stat[:, 0:1], op=ALU.mult), [h_m2b])
            h_m4 = P.v(lambda v: v.tensor_scalar(stat[:, 2:3], stat[:, 2:3],
                                                 1.0 / 1024.0, None, op0=ALU.mult),
                       [h_sq])
            h_m5 = P.v(lambda v: v.tensor_tensor(stat[:, 3:4], stat[:, 2:3],
                                                 stat[:, 3:4], op=ALU.subtract),
                       [h_m3, h_m4])
            h_m6 = P.v(lambda v: v.tensor_scalar(stat[:, 3:4], stat[:, 3:4], 1e-5,
                                                 None, op0=ALU.add), [h_m5])
            h_m7 = P.s(lambda s: s.activation(stat[:, 4:5], stat[:, 3:4], AF.Sqrt),
                       [h_m6] + [d for d in (prev_m7,) if d])
            prev_m7 = h_m7
            h_m8 = P.v(lambda v: v.reciprocal(stat[:, 4:5], stat[:, 4:5]), [h_m7])
            h_ap = P.v(lambda v: v.tensor_scalar(x1b[:, :], rr[:, :], stat[:, 0:1],
                                                 stat[:, 4:5], op0=ALU.subtract,
                                                 op1=ALU.mult),
                       [h_m8] + (h_x1_done[-1:] if h_x1_done else []) +
                       (h_router[-1:] if h_router else []))
            prev_ap = h_ap
            a_glob = pz * PCH + tt_
            h_rt = P.v(lambda v, a_glob=a_glob: v.scalar_tensor_tensor(
                trashv[:, :], x1b[:, :], 1.0, rw_rep[:, :], op0=ALU.mult,
                op1=ALU.mult, accum_out=wacc[:, a_glob:a_glob + 1]),
                [h_ap, ld["rw"]])
            h_router.append(h_rt)
            h_x1 = dma_g(x1D[t0 + ct0:t0 + ct0 + 128, :], x1b[:, :], deps=[h_ap])
            h_x1_done.append(h_x1)
            h_u1 = P.s(lambda s: s.activation(urr[:, :], x1b[:, :], AF.Copy,
                                              bias=0.0, scale=2.0,
                                              accum_out=ustat[:, 0:1]),
                       [h_ap, prev_m7] + prev_uh + h_uns[-1:])
            h_u2 = P.s(lambda s: s.activation(trashb[:, :], urr[:, :], AF.Square,
                                              accum_out=ustat[:, 2:3]),
                       [h_u1] + ([prev_sq] if prev_sq else []))
            prev_sq = h_u2
            h_u3 = P.v(lambda v: v.tensor_scalar(ustat[:, 0:1], ustat[:, 0:1],
                                                 1.0 / 1024.0, None, op0=ALU.mult),
                       [h_u1])
            h_u4 = P.v(lambda v: v.tensor_tensor(ustat[:, 3:4], ustat[:, 0:1],
                                                 ustat[:, 0:1], op=ALU.mult), [h_u3])
            h_u5 = P.v(lambda v: v.tensor_scalar(ustat[:, 2:3], ustat[:, 2:3],
                                                 1.0 / 1024.0, None, op0=ALU.mult),
                       [h_u2])
            h_u6 = P.v(lambda v: v.tensor_tensor(ustat[:, 3:4], ustat[:, 2:3],
                                                 ustat[:, 3:4], op=ALU.subtract),
                       [h_u4, h_u5])
            h_u7 = P.v(lambda v: v.tensor_scalar(ustat[:, 3:4], ustat[:, 3:4], 1e-5,
                                                 None, op0=ALU.add), [h_u6])
            h_u8 = P.s(lambda s: s.activation(ustat[:, 4:5], ustat[:, 3:4], AF.Sqrt),
                       [h_u7, prev_sq])
            prev_sq = h_u8
            h_u9 = P.v(lambda v: v.reciprocal(ustat[:, 4:5], ustat[:, 4:5]), [h_u8])
            h_u10 = P.v(lambda v: v.tensor_scalar(urr[:, :], urr[:, :], ustat[:, 0:1],
                                                  ustat[:, 4:5], op0=ALU.subtract,
                                                  op1=ALU.mult), [h_u9])
            h_ud = dma_s(outD[t0 + ct0:t0 + ct0 + 128, :], urr[:, :],
                         deps=[h_u10])
            h_uns.append(h_ud)
            prev_uh = [h_u10]
            pass_consumers = pass_consumers + [h_x1, h_rt, h_ap]
        wout_mms_hist.append(h_wout_mms)
        prev_pass = pass_consumers

    h_wb = P.v(lambda v: v.tensor_scalar(wacc[:, :], wacc[:, :], lodc[:, 1:2], None,
                                         op0=ALU.add), [h_rbc] + h_router)
    h_ww = dma_g(wwD[:, :], wacc[:, :], deps=[h_wb])

    if debug_stage in (1, 3):
        srcdbg = x1D if debug_stage == 1 else hD
        hdbg = [P.gd(lambda g, a=a, srcdbg=srcdbg: g.dma_start(
            out=outD[a * 128:(a + 1) * 128, :], in_=srcdbg[a * 128:(a + 1) * 128, :]),
            h_x1_done) for a in range(32)]
        P.gd(lambda g: g.dma_start(out=outD[T:T + 1, 0:32], in_=wacc[0:1, :]), [h_wb])
        with nc.Block() as block, ExitStack() as sstack:
            P.emit(nc, block, sstack)
        ctx.close()
        return nc

    msel = vst[:, 0:2048]
    m1T = vst[:, 2048:3072]
    oT = xTb[0]
    hn2T = qpT
    scs_bf = None
    _h1small = None

    def h1slot(ft):
        small = [(msel, 0), (msel, 512), (msel, 1024), (msel, 1536),
                 (aT, 0), (aT, 512), (x1b, 0), (x1b, 512),
                 (arow, 0), (trashv, 0), (trashv, 512),
                 (scs_bfv, 0), (scs_bfv, 512),
                 (selrep_bfv, 0), (selrep_bfv, 512), (po, 0)]
        if ft < 16:
            tns, base = small[ft]
            return tns[:, base:base + 512]
        tns = (kTb, kpT)[(ft - 16) // 8]
        return tns[:, ((ft - 16) % 8) * 512:((ft - 16) % 8 + 1) * 512]

    def xsel_t(c):
        return (awb, kptok)[c // 2][:, (c % 2) * 1024:(c % 2) * 1024 + 1024]
    h_wrep = dma_g(wrepx[:, :], wwD[0:1, :].partition_broadcast(128),
                   deps=[h_ww] + h_uns[-1:] + prev_uh)
    h_l1 = P.v(lambda v: v.tensor_reduce(cnt2[:, 0:1], wacc[:, :], axis=AX.X, op=ALU.min), [h_wb])
    h_l2 = P.v(lambda v: v.tensor_reduce(cnt2[:, 1:2], wacc[:, :], axis=AX.X, op=ALU.max), [h_wb, h_l1])
    h_t1 = P.t(lambda t: t.transpose(p_qkv[0][0:1, 0:128], cnt2[:, 0:1], idf128[:, :]),
               [h_l1, bank_last[0], ld["idf128"]])
    h_t2 = P.t(lambda t: t.transpose(p_qkv[1][0:1, 0:128], cnt2[:, 1:2], idf128[:, :]),
               [h_l2, bank_last[1]])
    bank_last[0], bank_last[1] = h_t1, h_t2
    h_c1 = P.v(lambda v: v.tensor_copy(tpc[0:1, :], p_qkv[0][0:1, 0:128]), [h_t1])
    h_c2 = P.v(lambda v: v.tensor_copy(tpp[0:1, :], p_qkv[1][0:1, 0:128]), [h_t2])
    h_s1 = P.v(lambda v: v.tensor_reduce(st8[0:1, 0:1], tpc[0:1, :], axis=AX.X, op=ALU.min), [h_c1])
    h_s2 = P.v(lambda v: v.tensor_reduce(st8[0:1, 1:2], tpp[0:1, :], axis=AX.X, op=ALU.max), [h_c2])
    h_s3 = P.v(lambda v: v.tensor_scalar(st8[0:1, 0:1], st8[0:1, 0:1], 0.001, None,
                                         op0=ALU.subtract), [h_s1])
    h_s4 = P.v(lambda v: v.memset(st8[0:1, 3:4], -1e30), [h_s2])
    bl = [h_s3, h_s4]
    h_cm_prev = None
    for it in range(NITER):
        h_d1 = P.v(lambda v: v.tensor_tensor(st8[0:1, 2:3], st8[0:1, 1:2], st8[0:1, 0:1],
                                             op=ALU.subtract), bl)
        h_d2 = P.v(lambda v: v.tensor_scalar(st8[0:1, 2:3], st8[0:1, 2:3], 1.0 / 129.0,
                                             None, op0=ALU.mult), [h_d1])
        h_bc = P.t(lambda t: t.matmul(p_qkv[0][:, 0:2], onesrow[0:1, :], st8[0:1, 0:3:2],
                                      start=True, stop=True),
                   [h_d2, bank_last[0], ld["onesrow"]])
        bank_last[0] = h_bc
        h_lc = P.v(lambda v: v.tensor_copy(lodc[:, 0:2], p_qkv[0][:, 0:2]),
                   [h_bc] + ([h_cm_prev] if h_cm_prev else []))
        h_pr = P.v(lambda v: v.scalar_tensor_tensor(probes[:, 0:1], icol[:, 0:1],
                                                    lodc[:, 1:2], lodc[:, 0:1],
                                                    op0=ALU.mult, op1=ALU.add),
                   [h_lc, ld["icol"]])
        h_cm = P.v(lambda v: v.tensor_scalar(cmpt[:, :], wrepx[:, :], probes[:, 0:1],
                                             None, op0=ALU.is_gt, op1=ALU.add,
                                             accum_out=cnt2[:, 0:1]),
                   [h_pr, h_wrep, h_l2])
        h_cm_prev = h_cm
        h_t3 = P.t(lambda t: t.transpose(p_qkv[0][0:1, 128:256], cnt2[:, 0:1],
                                         idf128[:, :]), [h_cm, bank_last[0]])
        h_t4 = P.t(lambda t: t.transpose(p_qkv[1][0:1, 0:128], probes[:, 0:1],
                                         idf128[:, :]), [h_pr, bank_last[1]])
        bank_last[0], bank_last[1] = h_t3, h_t4
        h_c3 = P.v(lambda v: v.tensor_copy(tpc[0:1, :], p_qkv[0][0:1, 128:256]),
                   [h_t3, h_s1])
        h_c4 = P.v(lambda v: v.tensor_copy(tpp[0:1, :], p_qkv[1][0:1, 0:128]),
                   [h_t4, h_s2])
        r1mf = r1m[0:1, :].bitcast(F32)

        def masked_red(cmp_op, thr_v, big_sign, red_op, dst_sl, deps):
            h1 = P.v(lambda v: v.tensor_scalar(brow[0:1, :], tpc[0:1, :], thr_v,
                                               None, op0=cmp_op), deps)
            h2 = P.v(lambda v: v.tensor_tensor(r1[0:1, :], brow[0:1, :], tpp[0:1, :],
                                               op=ALU.mult), [h1])
            h3 = P.v(lambda v: v.tensor_scalar(r1mf, brow[0:1, :],
                                               big_sign * 1e30, -big_sign * 1e30,
                                               op0=ALU.mult, op1=ALU.add), [h2])
            h4 = P.v(lambda v: v.tensor_tensor(r1[0:1, :], r1[0:1, :], r1mf,
                                               op=ALU.add), [h3])
            h5 = P.v(lambda v, dst_sl=dst_sl: v.tensor_reduce(
                st8[0:1, dst_sl:dst_sl + 1], r1[0:1, :], axis=AX.X, op=red_op), [h4])
            return h5

        h_e3 = masked_red(ALU.is_equal, float(KSEL), 1.0, ALU.max, 4,
                          [h_c3, h_c4, h_rb, h_rbm])
        h_e4 = P.v(lambda v: v.tensor_tensor(st8[0:1, 3:4], st8[0:1, 3:4],
                                             st8[0:1, 4:5], op=ALU.max), [h_e3])
        h_f3 = masked_red(ALU.is_ge, float(KSEL + 1), 1.0, ALU.max, 5, [h_e3])
        h_f4 = P.v(lambda v: v.tensor_tensor(st8[0:1, 0:1], st8[0:1, 0:1],
                                             st8[0:1, 5:6], op=ALU.max), [h_f3])
        h_g3 = masked_red(ALU.is_le, float(KSEL), -1.0, ALU.min, 6, [h_f3, h_f4])
        h_g4 = P.v(lambda v: v.tensor_tensor(st8[0:1, 1:2], st8[0:1, 1:2],
                                             st8[0:1, 6:7], op=ALU.min), [h_g3])
        bl = [h_e4, h_f4, h_g4]
    h_tb = P.t(lambda t: t.matmul(p_qkv[0][:, 2:3], onesrow[0:1, :], st8[0:1, 3:4],
                                  start=True, stop=True), bl + [bank_last[0]])
    bank_last[0] = h_tb
    h_tc = P.v(lambda v: v.tensor_copy(lodc[:, 0:1], p_qkv[0][:, 2:3]), [h_tb, h_g3])
    h_mm1 = P.v(lambda v: v.tensor_scalar(m01[:, :], wacc[:, :], lodc[:, 0:1], None,
                                          op0=ALU.is_gt), [h_tc, h_wb])
    h_sc1 = P.v(lambda v: v.tensor_tensor_scan(scan_t[:, :], m01[:, :], onesm[:, :],
                                               0.0, op0=ALU.add, op1=ALU.mult),
                [h_mm1, ld["onesm"]])
    h_rs1 = P.v(lambda v: v.tensor_copy(po[:, 0:1], cnt2[:, 0:1]), [h_cm_prev, h_c3])
    h_rs2 = P.v(lambda v: v.tensor_reduce(cnt2[:, 0:1], m01[:, :], axis=AX.X,
                                          op=ALU.add), [h_mm1, h_rs1])
    h_rs3 = P.v(lambda v: v.tensor_copy(po[:, 0:1], cnt2[:, 0:1]), [h_rs2])
    h_tri = P.t(lambda t: t.matmul(p_qkv[0][:, 3:4], tri[:, :], po[:, 0:1],
                                   start=True, stop=True),
                [h_rs3, ld["tri"], bank_last[0]])
    bank_last[0] = h_tri
    h_of1 = P.v(lambda v: v.tensor_copy(offs[:, 0:1], p_qkv[0][:, 3:4]), [h_tri])
    h_or1 = P.v(lambda v: v.scalar_tensor_tensor(ordp[:, :], scan_t[:, :], offs[:, 0:1],
                                                 m01[:, :], op0=ALU.add,
                                                 op1=ALU.subtract), [h_sc1, h_of1])
    h_or2 = P.v(lambda v: v.tensor_tensor(ordp[:, :], ordp[:, :], m01[:, :],
                                          op=ALU.mult), [h_or1])
    h_or3 = P.v(lambda v: v.tensor_scalar(tmp32[:, :], m01[:, :], -511.0, 511.0,
                                          op0=ALU.mult, op1=ALU.add), [h_or1])
    h_or4 = P.v(lambda v: v.tensor_tensor(ordp[:, :], ordp[:, :], tmp32[:, :],
                                          op=ALU.add), [h_or2, h_or3])
    h_p1 = P.v(lambda v: v.tensor_scalar(posi2[:, :], pos_s[:, :], 63, None,
                                         op0=ALU.bitwise_and), [ld["pos"]])
    h_p2 = P.v(lambda v: v.tensor_copy(tmp32[:, :], posi2[:, :]), [h_p1, h_or4])
    h_p3 = P.v(lambda v: v.tensor_scalar(posi2[:, :], pos_s[:, :], 6, None,
                                         op0=ALU.arith_shift_right), [h_p2])
    h_p4 = P.v(lambda v: v.tensor_copy(posf[:, :], posi2[:, :]), [h_p3])
    h_me = []
    h_me.append(P.v(lambda v: v.tensor_copy(meta[:, 0:192:6], thi[:, :]), [ld["thi"]]))
    h_me.append(P.v(lambda v: v.tensor_copy(meta[:, 1:192:6], tlo[:, :]), [ld["tlo"]]))
    h_me.append(P.v(lambda v: v.tensor_copy(meta[:, 2:192:6], posf[:, :]), [h_p4]))
    h_me.append(P.v(lambda v: v.tensor_copy(meta[:, 3:192:6], tmp32[:, :]), [h_p2, h_or4]))
    h_me.append(P.v(lambda v: v.tensor_copy(meta[:, 4:192:6], wacc[:, :]), [h_wb]))
    h_po_prev = None
    h_mm_prev = None
    for a in range(32):
        h_poa = P.v(lambda v, a=a: v.tensor_scalar(po[:, :], iota512[:, :],
                                                   ordp[:, a:a + 1], None,
                                                   op0=ALU.is_equal),
                    [h_or4, ld["iota512"]] + ([h_mm_prev] if h_mm_prev else [h_rs3, h_tri]))
        h_mm_prev = P.t(lambda t, a=a: t.matmul(p_qkv[1][0:5, 0:512],
                                                meta[:, 6 * a:6 * a + 5], po[:, :],
                                                start=(a == 0), stop=(a == 31),
                                                skip_group_check=True),
                        [h_poa, bank_last[1]] + h_me)
        h_po_prev = h_poa
    bank_last[1] = h_mm_prev
    h_s8 = P.v(lambda v: v.tensor_copy(sel8[0:5, :], p_qkv[1][0:5, 0:512]), [h_mm_prev])
    h_s8d = dma_g(sel8D[0:5, :], sel8[0:5, :], deps=[h_s8])
    h_r1d = dma_g(selrow[0:1, :], sel8D[0:1, :], deps=[h_s8d])
    h_r2d = dma_g(selrow2[0:1, :], sel8D[1:2, :], deps=[h_s8d])
    h_r3d = dma_g(posrow[0:1, :], sel8D[2:3, :], deps=[h_s8d])
    h_r4d = dma_g(selrow3[0:1, :], sel8D[3:4, :], deps=[h_s8d])
    h_r5d = dma_g(wrowt[0:1, :], sel8D[4:5, :], deps=[h_s8d])
    h_ri = P.v(lambda v: v.tensor_scalar(selrow[0:1, :], selrow[0:1, :], 64.0, None,
                                         op0=ALU.mult), [h_r1d])
    h_ri2 = P.v(lambda v: v.tensor_tensor(selrow[0:1, :], selrow[0:1, :],
                                          selrow2[0:1, :], op=ALU.add), [h_ri, h_r2d])
    h_sca = P.v(lambda v: v.tensor_scalar(scatrow[0:1, :], selrow[0:1, :],
                                          float(T), None, op0=ALU.min), [h_ri2])
    h_gat = P.v(lambda v: v.tensor_scalar(selrow[0:1, :], selrow[0:1, :],
                                          float(T - 1), None, op0=ALU.min), [h_sca])
    h_pi = P.v(lambda v: v.tensor_scalar(posrow[0:1, :], posrow[0:1, :], 64.0, None,
                                         op0=ALU.mult), [h_r3d])
    h_pi2 = P.v(lambda v: v.tensor_tensor(posrow[0:1, :], posrow[0:1, :],
                                          selrow3[0:1, :], op=ALU.add), [h_pi, h_r4d])
    h_rwd = dma_g(rowsD[0:1, :], selrow[0:1, :], deps=[h_gat])
    h_rwd2 = dma_g(rowsD[1:2, :], scatrow[0:1, :], deps=[h_sca])
    h_rwd3 = dma_g(rowsD[2:3, :], wrowt[0:1, :], deps=[h_r5d])
    h_cl1 = dma_g(selcol[:, 0:4], rowsD[0:1, :].rearrange("o (c p) -> (o p) c", p=128), allow_slow_non_contiguous=True,
                  deps=[h_rwd])
    h_cl2 = dma_g(wselc[:, 0:4], rowsD[2:3, :].rearrange("o (c p) -> (o p) c", p=128), allow_slow_non_contiguous=True,
                  deps=[h_rwd3])
    h_cl3 = dma_g(lodc[:, 0:2], rowsD[1:2, 0:256].rearrange("o (c p) -> (o p) c", p=128), allow_slow_non_contiguous=True,
                  deps=[h_rwd2, h_lc, h_tc, h_mm1])
    h_cl4 = dma_g(cnt2[:, 0:2], rowsD[1:2, 256:512].rearrange("o (c p) -> (o p) c", p=128), allow_slow_non_contiguous=True,
                  deps=[h_rwd2, h_rs2, h_cm_prev])
    h_ci = P.v(lambda v: v.tensor_copy(selcoli[:, :], selcol[:, :]), [h_cl1])
    h_si1 = P.v(lambda v: v.tensor_copy(scatcoli[:, 0:2], lodc[:, 0:2]), [h_cl3])
    h_si2 = P.v(lambda v: v.tensor_copy(scatcoli[:, 2:4], cnt2[:, 0:2]), [h_cl4])

    prev_u = [prev_ap, prev_sq, prev_m7]

    h_xg = []
    for c in range(4):
        h_xg.append(P.gd(lambda g, c=c: g.indirect_dma_start(
            xsel_t(c), None, x1D,
            IndirectOffsetOnAxis(selcoli[:, c:c + 1], 0)),
            [h_ci, aw_last[0], aw_last[1], statez_done[-1]] + h_x1_done + prev_div))
    h_m1 = []
    prev_mg = None
    for c in range(4):
        h_mg = P.gd(lambda g, c=c: g.indirect_dma_start(
            m1f[:, 0:4096], None, maskD,
            IndirectOffsetOnAxis(selcoli[:, c:c + 1], 0)),
            [h_ci, h_cm_prev] + ([prev_mg] if prev_mg else []))
        h_mc = P.v(lambda v, c=c: v.tensor_copy(cmpt[:, :], m1f[:, 0:4096]),
                   [h_mg] + ([h_cm_prev] if c == 0 else []) + h_m1[-1:])
        h_md = dma_g(m1D[c * 128:(c + 1) * 128, :], cmpt[:, :], deps=[h_mc])
        prev_mg = h_mc
        h_m1.append(h_md)

    wslots = {"fwq": BIGW[:, 0:8192], "fwk": BIGW[:, 8192:16384],
              "fwv": BIGW[:, 16384:24576], "fwo": BIGW[:, 24576:32768]}
    wsrc = {"fwq": fwqD, "fwk": fwkD, "fwv": fwvD, "fwo": fwoD}
    h_wffl = {}
    wdeps = list(prev_pass) + [h_mm_prev] + list(aw_last) + list(prev_div)
    for k in wslots:
        h_wffl[k] = [dma_g(wslots[k][:, kc * 1024:(kc + 1) * 1024],
                           wsrc[k][kc * 128:(kc + 1) * 128, :], deps=wdeps)
                     for kc in range(8)]

    h_msel_ev = []
    mselbanks = [p_qkv[0], p_qkv[1], p_awA, p_awB]
    mb_last = {0: bank_last[0], 1: bank_last[1], 2: aw_last[0], 3: aw_last[1]}
    h_selrep = P.t(lambda t: t.matmul(p_ctxA[:, 0:512], onesrow[0:1, :],
                                      selrow[0:1, :], start=True, stop=True),
                   [h_gat, prev_div[0] if prev_div else None, ld["onesrow"]])
    h_srv = P.v(lambda v: v.tensor_copy(selrep[:, :], p_ctxA[:, 0:512]),
                [h_selrep] + prev_div)
    h_m1T = [None, None]
    h_poc = None
    prev_ms = {0: None, 1: None, 2: None, 3: None}
    for a in range(32):
        buf = a % 2
        h_mt = dma_y(m1T[:, buf * 512:(buf + 1) * 512],
                     m1D[0:512, a * 128:(a + 1) * 128],
                     deps=h_m1 + ([h_m1T[buf]] if h_m1T[buf] else
                                  [h_mm_prev, statez_done[-1]]), tr=True)
        h_poc = P.v(lambda v, a=a: v.tensor_scalar(po[:, :], selrep[:, :],
                                                   tcol[:, a:a + 1], None,
                                                   op0=ALU.is_equal),
                    [h_srv, ld["tcol"], h_po_prev] +
                    ([h_poc] if h_poc else []) +
                    ([prev_ms[3]] if prev_ms[3] else []))
        for it_ in range(4):
            hmv = P.t(lambda t, a=a, it_=it_, buf=buf: t.matmul(
                mselbanks[it_][:, 0:512],
                m1T[:, buf * 512 + it_ * 128:buf * 512 + (it_ + 1) * 128],
                po[:, :], start=(a == 0), stop=(a == 31), skip_group_check=True),
                [h_mt, h_poc] + ([mb_last[it_]] if mb_last[it_] else []))
            mb_last[it_] = hmv
            h_m1T[buf] = hmv
        prev_ms[3] = mb_last[3]
    for it_ in range(4):
        hme = P.v(lambda v, it_=it_: v.tensor_copy(
            msel[:, it_ * 512:(it_ + 1) * 512], mselbanks[it_][:, 0:512]),
            [mb_last[it_], h_srv])
        hmz = P.v(lambda v, it_=it_: v.memset(msel[:, it_ * 512 + 511:it_ * 512 + 512],
                                              -1e30), [hme])
        h_msel_ev.append(hmz)
        mb_last[it_] = hmz

    h_hn = []
    prev_sq2 = None
    for c in range(4):
        h_q1 = P.s(lambda s, c=c: s.activation(trashb[:, :], xsel_t(c),
                                               AF.Square, accum_out=stat[:, 2:3]),
                   [h_xg[c]] + ([prev_sq2] if prev_sq2 else [prev_u[1]]) + h_uns[-1:])
        prev_sq2 = h_q1
        h_q2 = P.v(lambda v: v.tensor_scalar(stat[:, 3:4], stat[:, 2:3], 1.0 / 1024.0,
                                             None, op0=ALU.add if False else ALU.mult),
                   [h_q1] + h_uns[-1:])
        h_q3 = P.v(lambda v: v.tensor_scalar(stat[:, 3:4], stat[:, 3:4], 1e-6, None,
                                             op0=ALU.add), [h_q2])
        h_q4 = P.s(lambda s: s.activation(stat[:, 4:5], stat[:, 3:4], AF.Sqrt), [h_q3])
        h_q5 = P.v(lambda v: v.reciprocal(stat[:, 4:5], stat[:, 4:5]), [h_q4])
        h_q6 = P.v(lambda v, c=c: v.scalar_tensor_tensor(
            hnt[:, :], xsel_t(c), stat[:, 4:5],
            ln1_rep[:, :], op0=ALU.mult, op1=ALU.mult),
            [h_q5, ld["ln1"], h_xg[c]] + ([h_hn[-1]] if h_hn else prev_div))
        h_hnd = dma_g(hnD[c * 128:(c + 1) * 128, :], hnt[:, :], deps=[h_q6])
        h_hn.append(h_hnd)
    hnT = xTb[1]
    h_hnT = [dma_y(hnT[:, kc * 512:(kc + 1) * 512], hnD[0:512, kc * 128:(kc + 1) * 128],
                   deps=h_hn + wout_mms_hist[-1], tr=True) for kc in range(8)]

    angs = m1f[:, 0:1024]
    angc = m1f[:, 1024:2048]
    angt = m1f[:, 2048:3072]
    angt2v = m1f[:, 3072:4096]
    angt3v = rr[:, 0:1024]
    posi128 = r2f[:, 0:1024].bitcast(I32)
    qkbanks = [p_ctxA, p_ctxB]
    qk_last = {0: h_srv, 1: prev_div[1] if len(prev_div) > 1 else None}
    h_qtok = {}
    prev_rope = []
    wslot_mm = {"fwq": [], "fwk": [], "fwv": [], "fwo": []}
    for wq_name, dstD in (("fwq", qtokD), ("fwk", ktokD)):
        for c in range(4):
            h_qm = []
            for halfn in range(2):
                hmm = None
                for kc in range(8):
                    hmm = P.t(lambda t, c=c, halfn=halfn, kc=kc, wq_name=wq_name:
                              t.matmul(
                        qkbanks[halfn][:, 0:512],
                        hnT[:, kc * 512 + c * 128:kc * 512 + (c + 1) * 128],
                        wslots[wq_name][:, kc * 1024 + halfn * 512:kc * 1024 + (halfn + 1) * 512],
                        start=(kc == 0), stop=(kc == 7)),
                        (h_hnT + h_wffl[wq_name] +
                         ([qk_last[halfn]] if qk_last[halfn] else []))
                        if kc == 0 else [])
                h_qm.append(hmm)
                wslot_mm[wq_name].append(hmm)
            h_a1 = P.t(lambda t, c=c: t.matmul(p_kv[0][:, 0:512],
                                               posrow[0:1, c * 128:(c + 1) * 128],
                                               invf[0:1, 0:512], start=True, stop=True),
                       [h_pi2, ld["invf"], statez_done[-1] if wq_name == "fwq" and c == 0 else None] +
                       ([prev_rope[-1]] if prev_rope else []))
            h_a2 = P.t(lambda t, c=c: t.matmul(p_kv[1][:, 0:512],
                                               posrow[0:1, c * 128:(c + 1) * 128],
                                               invf[0:1, 512:1024], start=True,
                                               stop=True),
                       [h_pi2, statez_done[-1] if wq_name == "fwq" and c == 0 else None])
            h_b1 = P.v(lambda v: v.tensor_copy(angs[:, 0:512], p_kv[0][:, 0:512]),
                       [h_a1] + prev_rope[-1:])
            h_b2 = P.v(lambda v: v.tensor_copy(angs[:, 512:1024], p_kv[1][:, 0:512]),
                       [h_a2])
            h_b3 = P.v(lambda v: v.tensor_scalar(angc[:, :], angs[:, :],
                                                 float(0.5 * np.pi), None,
                                                 op0=ALU.add), [h_b1, h_b2])
            hsin = None
            for which, hb in (("sin", [h_b1, h_b2]), ("cos", [h_b3])):
                src_ang = angs if which == "sin" else angc
                h_t1_ = P.v(lambda v, src_ang=src_ang: v.tensor_scalar(
                    angt[:, :], src_ang[:, :], float(1.0 / TWO_PI), None,
                    op0=ALU.mult), hb + ([hsin] if hsin else []))
                h_t2_ = P.v(lambda v: v.tensor_copy(posi128, angt[:, :]),
                            [h_t1_, h_srv] + prev_u)
                h_t3_ = P.v(lambda v: v.tensor_copy(angt2v[:, :], posi128), [h_t2_])
                h_t4_ = P.v(lambda v: v.tensor_tensor(angt3v[:, :], angt2v[:, :],
                                                      angt[:, :], op=ALU.is_gt),
                            [h_t3_])
                h_t5_ = P.v(lambda v: v.tensor_tensor(angt2v[:, :], angt2v[:, :],
                                                      angt3v[:, :], op=ALU.subtract),
                            [h_t4_])
                h_t6_ = P.v(lambda v, src_ang=src_ang: v.scalar_tensor_tensor(
                    angt[:, :], angt2v[:, :], float(-TWO_PI), src_ang[:, :],
                    op0=ALU.mult, op1=ALU.add), [h_t5_])
                h_t7_ = P.v(lambda v: v.tensor_scalar(angt3v[:, :], angt[:, :],
                                                      float(np.pi), None,
                                                      op0=ALU.is_gt), [h_t6_])
                h_t8_ = P.v(lambda v: v.scalar_tensor_tensor(
                    angt[:, :], angt3v[:, :], float(-TWO_PI), angt[:, :],
                    op0=ALU.mult, op1=ALU.add), [h_t7_])
                hsin = P.s(lambda s, src_ang=src_ang: s.activation(
                    src_ang[:, :], angt[:, :], AF.Sin), [h_t8_, prev_sq2])
                prev_rope.append(hsin)
            h_e1_ = P.v(lambda v: v.tensor_copy(qtok[:, 0:512], qkbanks[0][:, 0:512]),
                        [h_qm[0], hsin] + prev_u +
                        (h_qtok.get(wq_name, [])[-1:] if h_qtok.get(wq_name) else
                         (h_qtok.get("fwq", [])[-1:] if wq_name == "fwk" else [])))
            h_e2_ = P.v(lambda v: v.tensor_copy(qtok[:, 512:1024],
                                                qkbanks[1][:, 0:512]), [h_qm[1]])
            qk_last[0], qk_last[1] = h_e1_, h_e2_
            h_rc1 = P.v(lambda v: v.tensor_copy(
                rcop[:, :].rearrange("p (h a b) -> p h a b", a=2, b=32)[:, :, 0, :],
                qtok[:, :].rearrange("p (h a b) -> p h a b", a=2, b=32)[:, :, 1, :]),
                [h_e1_, h_e2_] + prev_rope[-2:])
            h_rc2 = P.v(lambda v: v.tensor_copy(
                rcop[:, :].rearrange("p (h a b) -> p h a b", a=2, b=32)[:, :, 1, :],
                qtok[:, :].rearrange("p (h a b) -> p h a b", a=2, b=32)[:, :, 0, :]),
                [h_rc1])
            h_rm1 = P.v(lambda v: v.tensor_tensor(rcop[:, :], rcop[:, :], angs[:, :],
                                                  op=ALU.mult), [h_rc2])
            h_rm2 = P.v(lambda v: v.tensor_tensor(qtok[:, :], qtok[:, :], angc[:, :],
                                                  op=ALU.mult), [h_rm1])
            h_rm3 = P.v(lambda v: v.tensor_tensor(qtok[:, :], qtok[:, :], rcop[:, :],
                                                  op=ALU.add), [h_rm2])
            prev_rope.append(h_rm3)
            h_qd = dma_g(dstD[c * 128:(c + 1) * 128, :], qtok[:, :], deps=[h_rm3])
            h_qtok.setdefault(wq_name, []).append(h_qd)
    h_qT = [dma_y(qTb[:, kc * 512:(kc + 1) * 512],
                  qtokD[0:512, kc * 128:(kc + 1) * 128],
                  deps=h_qtok["fwq"] + h_qtok["fwk"] + [prev_rope[-1]], tr=True)
            for kc in range(8)]
    h_kT = [dma_y(kTb[:, kc * 512:(kc + 1) * 512],
                  ktokD[0:512, kc * 128:(kc + 1) * 128],
                  deps=h_qtok["fwk"] + h_m1 + [h_cm_prev, prev_mg], tr=True)
            for kc in range(8)]
    h_vm = []
    for c in range(4):
        for halfn in range(2):
            hmm = None
            for kc in range(8):
                hmm = P.t(lambda t, c=c, halfn=halfn, kc=kc: t.matmul(
                    qkbanks[halfn][:, 0:512],
                    hnT[:, kc * 512 + c * 128:kc * 512 + (c + 1) * 128],
                    wslots["fwv"][:, kc * 1024 + halfn * 512:kc * 1024 + (halfn + 1) * 512],
                    start=(kc == 0), stop=(kc == 7)),
                    (h_hnT + h_wffl["fwv"] + [qk_last[halfn]]) if kc == 0 else [])
            wslot_mm["fwv"].append(hmm)
            hev = P.v(lambda v, c=c, halfn=halfn: v.tensor_copy(
                vselb[:, c * 1024 + halfn * 512:c * 1024 + (halfn + 1) * 512],
                qkbanks[halfn][:, 0:512]), [hmm])
            qk_last[halfn] = hev
            h_vm.append(hev)

    slot_of = ["fwq", "fwq", "fwk", "fwk", "fwv", "fwv", "fwo", "fwo"]
    h_w1l = [dma_g(BIGW[:, kc * 4096:(kc + 1) * 4096],
                   w1D[kc * 128:(kc + 1) * 128, :],
                   deps=wslot_mm[slot_of[kc]]) for kc in range(6)]

    scbanks = [p_qkv[0], p_qkv[1], p_awA, p_awB]
    sc_last = {i: mb_last[i] for i in range(4)}
    obanks = [p_ctxA, p_ctxB, p_kv[0], p_kv[1]]
    ob_last = {0: qk_last[0], 1: qk_last[1], 2: None, 3: None}
    arow2 = trashv[:, 0:512]
    h_adma = {}
    h_oev = []
    aT_prev = [None, None]
    h_exp = [None, None]
    h_norm = [None, None]
    h_adl = [None, None]
    bank_i = 0
    h_scev_all = []

    def emit_o(hd):
        bk = hd % 4
        win = hd * 64 - 64 * (hd % 2)
        h_om = None
        for jc in range(4):
            buf = jc % 2
            dmaf = dma_y if jc % 2 == 0 else dma_s
            h_at = dmaf(aT[:, buf * 512:(buf + 1) * 512],
                        aDr[hd * 512:hd * 512 + 512, jc * 128:(jc + 1) * 128],
                        deps=h_adma[hd] + ([aT_prev[buf]] if aT_prev[buf] else []),
                        tr=True)
            h_om = P.t(lambda t, hd=hd, jc=jc, bk=bk, win=win, buf=buf: t.matmul(
                obanks[bk][:, 0:512],
                vselb[:, jc * 1024 + win:jc * 1024 + win + 128],
                aT[:, buf * 512:(buf + 1) * 512],
                start=(jc == 0), stop=(jc == 3), skip_group_check=True),
                [h_at] + h_vm + ([ob_last[bk]] if ob_last[bk] else []) +
                ([statez_done[-1]] if bk >= 2 else []))
            aT_prev[buf] = h_om
        ro = (hd % 2) * 64
        hoe = P.v(lambda v, hd=hd, bk=bk, ro=ro: v.tensor_copy(
            oT[ro:ro + 64, (hd // 2) * 512:(hd // 2 + 1) * 512],
            obanks[bk][ro:ro + 64, 0:512]), [h_om])
        ob_last[bk] = hoe
        h_oev.append(hoe)

    for hd in range(16):
        kc2, ro = hd // 2, (hd % 2) * 64
        for c in range(4):
            i = hd * 4 + c
            par = i % 2
            stt_b = stat if par == 0 else ustat
            rrs = rr[:, par * 512:(par + 1) * 512]
            arw = arow[:, :] if par == 0 else arow2
            bk = bank_i % 4
            bank_i += 1
            h_scm = P.t(lambda t, c=c, hd=hd, kc2=kc2, ro=ro, bk=bk: t.matmul(
                scbanks[bk][:, 0:512],
                qTb[ro:ro + 64, kc2 * 512 + c * 128:kc2 * 512 + (c + 1) * 128],
                kTb[ro:ro + 64, kc2 * 512:kc2 * 512 + 512],
                start=True, stop=True),
                h_qT + h_kT + [sc_last[bk]])
            h_s1_ = P.v(lambda v, c=c, bk=bk: v.scalar_tensor_tensor(
                scs[:, 0:512], scbanks[bk][:, 0:512], 0.125,
                msel[:, c * 512:(c + 1) * 512], op0=ALU.mult, op1=ALU.add),
                [h_scm, h_msel_ev[c]] +
                ([h_exp[1 - par]] if h_exp[1 - par] else []))
            h_s2_ = P.v(lambda v, stt_b=stt_b: v.tensor_reduce(
                stt_b[:, 6:7], scs[:, 0:512], axis=AX.X, op=ALU.max),
                [h_s1_] + ([h_exp[par]] if h_exp[par] else []))
            h_s3_ = P.v(lambda v, stt_b=stt_b: v.tensor_scalar(
                stt_b[:, 6:7], stt_b[:, 6:7], -1.0, None, op0=ALU.mult), [h_s2_])
            h_s4_ = P.s(lambda s, stt_b=stt_b, rrs=rrs: s.activation(
                rrs, scs[:, 0:512], AF.Exp, bias=stt_b[:, 6:7], scale=1.0,
                accum_out=stt_b[:, 7:8]),
                [h_s3_, prev_sq2] + ([h_norm[par]] if h_norm[par] else []) +
                (prev_u if i == 0 else []))
            h_exp[par] = h_s4_
            h_s5_ = P.v(lambda v, stt_b=stt_b: v.reciprocal(
                stt_b[:, 7:8], stt_b[:, 7:8]), [h_s4_])
            h_s6_ = P.v(lambda v, stt_b=stt_b, rrs=rrs, arw=arw: v.tensor_scalar(
                arw, rrs, stt_b[:, 7:8], None, op0=ALU.mult),
                [h_s5_] + ([h_adl[par]] if h_adl[par] else []))
            h_norm[par] = h_s6_
            h_ad = P.gd(lambda g, c=c, hd=hd, arw=arw: g.dma_start(
                out=aDr[hd * 512 + c * 128:hd * 512 + (c + 1) * 128, :],
                in_=arw), [h_s6_])
            h_adl[par] = h_ad
            h_adma.setdefault(hd, []).append(h_ad)
            h_scev_all.append(h_s6_)
            sc_last[bk] = h_s1_
        if hd >= 1:
            emit_o(hd - 1)
    emit_o(15)
    h_x2a = []
    wo_last = {0: ob_last[0], 1: ob_last[1]}
    for c in range(4):
        for halfn in range(2):
            hmm = None
            for kc in range(8):
                hmm = P.t(lambda t, c=c, halfn=halfn, kc=kc: t.matmul(
                    qkbanks[halfn][:, 0:512],
                    oT[:, kc * 512 + c * 128:kc * 512 + (c + 1) * 128],
                    wslots["fwo"][:, kc * 1024 + halfn * 512:kc * 1024 + (halfn + 1) * 512],
                    start=(kc == 0), stop=(kc == 7)),
                    (h_oev + h_wffl["fwo"] + [wo_last[halfn]]) if kc == 0 else [])
            wslot_mm["fwo"].append(hmm)
            hx2 = P.v(lambda v, c=c, halfn=halfn: v.scalar_tensor_tensor(
                x2a[:, c * 1024 + halfn * 512:c * 1024 + (halfn + 1) * 512],
                qkbanks[halfn][:, 0:512], 1.0,
                xsel_t(c)[:, halfn * 512:(halfn + 1) * 512],
                op0=ALU.mult, op1=ALU.add),
                [hmm, h_xg[c]] + ([prev_rope[-1]] if c == 0 and halfn == 0 else []))
            wo_last[halfn] = hx2
            h_x2a.append(hx2)

    h_hn2 = []
    prev_sq3 = prev_sq2
    for c in range(4):
        h_q1 = P.s(lambda s, c=c: s.activation(trashb[:, :],
                                               x2a[:, c * 1024:(c + 1) * 1024],
                                               AF.Square, accum_out=stat[:, 2:3]),
                   [h_x2a[2 * c], h_x2a[2 * c + 1], prev_sq3])
        prev_sq3 = h_q1
        h_q2 = P.v(lambda v: v.tensor_scalar(stat[:, 3:4], stat[:, 2:3], 1.0 / 1024.0,
                                             None, op0=ALU.mult), [h_q1])
        h_q3 = P.v(lambda v: v.tensor_scalar(stat[:, 3:4], stat[:, 3:4], 1e-6, None,
                                             op0=ALU.add), [h_q2])
        h_q4 = P.s(lambda s: s.activation(stat[:, 4:5], stat[:, 3:4], AF.Sqrt),
                   [h_q3, prev_sq3])
        prev_sq3 = h_q4
        h_q5 = P.v(lambda v: v.reciprocal(stat[:, 4:5], stat[:, 4:5]), [h_q4])
        h_q6 = P.v(lambda v, c=c: v.scalar_tensor_tensor(
            hnt[:, :], x2a[:, c * 1024:(c + 1) * 1024], stat[:, 4:5],
            ln2_rep[:, :], op0=ALU.mult, op1=ALU.mult),
            [h_q5, ld["ln2"]] + ([h_hn2[-1]] if h_hn2 else [h_hn[-1]]))
        h_h2d = dma_g(hn2D[c * 128:(c + 1) * 128, :], hnt[:, :], deps=[h_q6])
        h_hn2.append(h_h2d)
    h_hn2T = [dma_y(hn2T[:, kc * 512:(kc + 1) * 512],
                    hn2D[0:512, kc * 128:(kc + 1) * 128],
                    deps=h_hn2 + h_scev_all[-1:] + [h_mm_prev, prev_sq2],
                    tr=True) for kc in range(8)]
    h_w1l += [dma_g(BIGW[:, kc * 4096:(kc + 1) * 4096],
                    w1D[kc * 128:(kc + 1) * 128, :],
                    deps=wslot_mm["fwo"]) for kc in (6, 7)]
    h_gel = []
    h_gcp = []
    g_last = {0: wo_last[0], 1: wo_last[1], 2: aw_last[0], 3: aw_last[1]}
    g_last = {0: wo_last[0], 1: wo_last[1], 2: sc_last[2], 3: sc_last[3]}
    h1_written = {}
    h_w1mm = []
    for ft in range(32):
        bk = ft % 4
        hmm = None
        for kc in range(8):
            hmm = P.t(lambda t, ft=ft, kc=kc, bk=bk: t.matmul(
                scbanks[bk][:, 0:512],
                BIGW[:, kc * 4096 + ft * 128:kc * 4096 + (ft + 1) * 128],
                hn2T[:, kc * 512:(kc + 1) * 512],
                start=(kc == 0), stop=(kc == 7)),
                (h_hn2T + [g_last[bk], h_w1l[kc]] +
                 (h_kT if ft == 2 else []) + (h_vm if ft == 3 else []))
                if kc == 0 else [h_w1l[kc]])
        gs = (ft % 2) * 512
        hge = P.s(lambda s, ft=ft, bk=bk, gs=gs: s.activation(
            rr[:, gs:gs + 512], scbanks[bk][:, 0:512],
            AF.Gelu_apprx_tanh), [hmm, prev_sq3, prev_rope[-1], h_oev[-1],
                                  h_adma[15][-1], h_adma[15][-2], h_srv,
                                  h_scev_all[-1]] +
            ([h_gel[-1]] if h_gel else []) +
            ([h_gcp[ft - 2]] if ft >= 2 else []))
        hcp = P.v(lambda v, ft=ft, gs=gs: v.tensor_copy(
            h1slot(ft), rr[:, gs:gs + 512]), [hge] + h_uns[-1:])
        g_last[bk] = hge
        h_gcp.append(hcp)
        h_gel.append(hge)
        h_gel.append(hcp)
        h1_written[ft] = hcp
        h_w1mm.append(hmm)
    h_w2l = [dma_g(BIGW[:, fc * 1024:(fc + 1) * 1024], w2D[fc * 128:(fc + 1) * 128, :],
                   deps=[h_w1mm[ft] for ft in range((fc % 4) * 8, (fc % 4) * 8 + 8)])
             for fc in range(32)]
    h_sct = []
    w2_last = {0: g_last[0], 1: g_last[1]}
    prev_tail = list(prev_u) + [prev_rope[-1]]
    for c in range(4):
        h_ps = []
        for halfn in range(2):
            hmm = None
            for fc in range(32):
                hmm = P.t(lambda t, c=c, halfn=halfn, fc=fc: t.matmul(
                    qkbanks[halfn][:, 0:512],
                    h1slot(fc)[:, c * 128:(c + 1) * 128],
                    BIGW[:, fc * 1024 + halfn * 512:fc * 1024 + (halfn + 1) * 512],
                    start=(fc == 0), stop=(fc == 31)),
                    [h_w2l[fc], h1_written[fc]] +
                    ([w2_last[halfn]] if fc == 0 else []))
            h_ps.append(hmm)
        h_v1 = []
        for halfn in range(2):
            h_v1.append(P.v(lambda v, c=c, halfn=halfn: v.scalar_tensor_tensor(
                rr[:, halfn * 512:(halfn + 1) * 512], qkbanks[halfn][:, 0:512], 1.0,
                x2a[:, c * 1024 + halfn * 512:c * 1024 + (halfn + 1) * 512],
                op0=ALU.mult, op1=ALU.add),
                [h_ps[halfn]] + prev_tail + ([h_x2a[-1]] if c == 0 and halfn == 0 else [])))
            w2_last[halfn] = h_v1[halfn]
        h_v2 = []
        for halfn in range(2):
            h_v2.append(P.v(lambda v, c=c, halfn=halfn: v.scalar_tensor_tensor(
                r2f[:, halfn * 512:(halfn + 1) * 512], rr[:, halfn * 512:(halfn + 1) * 512],
                wselc[:, c:c + 1],
                xsel_t(c)[:, halfn * 512:(halfn + 1) * 512],
                op0=ALU.mult, op1=ALU.add, accum_out=stat[:, halfn:halfn + 1]),
                [h_v1[halfn], h_cl2, h_xg[c]] + prev_tail))
        h_w1_ = P.s(lambda s: s.activation(trashb[:, :], r2f[:, :], AF.Square,
                                           accum_out=stat[:, 2:3]),
                    h_v2 + [prev_sq3] + ([h_gel[-1]] if h_gel else []))
        prev_sq3 = h_w1_
        h_w2_ = P.v(lambda v: v.tensor_tensor(stat[:, 0:1], stat[:, 0:1], stat[:, 1:2],
                                              op=ALU.add), h_v2)
        h_w3_ = P.v(lambda v: v.tensor_scalar(stat[:, 0:1], stat[:, 0:1], 1.0 / 1024.0,
                                              None, op0=ALU.mult), [h_w2_])
        h_w4_ = P.v(lambda v: v.tensor_tensor(stat[:, 3:4], stat[:, 0:1], stat[:, 0:1],
                                              op=ALU.mult), [h_w3_])
        h_w5_ = P.v(lambda v: v.tensor_scalar(stat[:, 2:3], stat[:, 2:3], 1.0 / 1024.0,
                                              None, op0=ALU.mult), [h_w1_])
        h_w6_ = P.v(lambda v: v.tensor_tensor(stat[:, 3:4], stat[:, 2:3], stat[:, 3:4],
                                              op=ALU.subtract), [h_w4_, h_w5_])
        h_w7_ = P.v(lambda v: v.tensor_scalar(stat[:, 3:4], stat[:, 3:4], 1e-5, None,
                                              op0=ALU.add), [h_w6_])
        h_w8_ = P.s(lambda s: s.activation(stat[:, 4:5], stat[:, 3:4], AF.Sqrt),
                    [h_w7_, prev_sq3])
        h_w9_ = P.v(lambda v: v.reciprocal(stat[:, 4:5], stat[:, 4:5]), [h_w8_])
        h_wA_ = P.v(lambda v: v.tensor_scalar(rr2[:, :], r2f[:, :], stat[:, 0:1],
                                              stat[:, 4:5], op0=ALU.subtract,
                                              op1=ALU.mult),
                    [h_w9_] + prev_tail + h_uns[-1:] + h_sct[-1:])
        h_sc_ = P.gd(lambda g, c=c: g.indirect_dma_start(
            outD, IndirectOffsetOnAxis(scatcoli[:, c:c + 1], 0), rr2[:, :], None),
            [h_wA_, h_si1, h_si2] + h_uns)
        h_sct.append(h_sc_)
        prev_tail = [h_wA_, h_w8_]
    P.g(lambda g: g.engine_nop(), h_sct)

    with nc.Block() as block, ExitStack() as sstack:
        P.emit(nc, block, sstack)
    ctx.close()
    nc._n_items = len(P.items)
    return nc


def _consts():
    t_grid = (np.arange(32)[None, :] * 128 + np.arange(128)[:, None]).astype(np.float32)
    inv = (1.0 / 10000.0 ** (np.arange(0, 64, 2, dtype=np.float64) / 64.0))
    invs = np.concatenate([-inv, inv])
    invf = np.tile(invs, 16).astype(np.float32)[None, :]
    return {
        "icol": np.arange(1, 129, dtype=np.float32).reshape(128, 1),
        "iota512": np.broadcast_to(np.arange(512, dtype=np.float32), (128, 512)).copy(),
        "thi": np.floor(t_grid / 64).astype(np.float32),
        "tlo": (t_grid % 64).astype(np.float32),
        "tcol": t_grid.copy(),
        "tri": np.tril(np.ones((128, 128)), -1).T.astype(ml_dtypes.bfloat16),
        "idf128": np.eye(128, dtype=np.float32),
        "onesrow": np.ones((1, 128), np.float32),
        "awmask": np.tile(np.triu(np.ones((128, 128), np.float32)), (1, 4)).astype(ml_dtypes.bfloat16),
        "invf": invf,
        "onesm": np.ones((128, 32), np.float32),
    }


def _stage_core(inputs, b):
    rep = lambda v: np.ascontiguousarray(np.broadcast_to(v.reshape(1, -1), (128, v.size)).astype(np.float32))
    pos = np.asarray(inputs["L0_ffn_position_ids"][b]).astype(np.int32)
    posb = np.ascontiguousarray(pos.reshape(32, 128).T)
    m = {
        "x": np.ascontiguousarray(inputs["L0_attn_x"][b], np.float32),
        "wqkv": np.ascontiguousarray(inputs["L0_attn_Wqkv"], np.float32),
        "bqkv": np.ascontiguousarray(inputs["L0_attn_bqkv"], np.float32).reshape(-1, 1),
        "proj": np.ascontiguousarray(inputs["L0_attn_proj"], np.float32),
        "wout": np.ascontiguousarray(inputs["L0_attn_Wout"], np.float32),
        "maskb": np.ascontiguousarray(inputs["L0_ffn_attention_mask"][b, 0], np.float32),
        "posb": posb,
        "rb": np.asarray(inputs["L0_ffn_router_b"], np.float32).reshape(1, 1),
        "fwq": np.ascontiguousarray(inputs["L0_ffn_wq"], np.float32),
        "fwk": np.ascontiguousarray(inputs["L0_ffn_wk"], np.float32),
        "fwv": np.ascontiguousarray(inputs["L0_ffn_wv"], np.float32),
        "fwo": np.ascontiguousarray(inputs["L0_ffn_wo"], np.float32),
        "w1": np.ascontiguousarray(inputs["L0_ffn_w1"], np.float32),
        "w2": np.ascontiguousarray(inputs["L0_ffn_w2"], np.float32),
        "bout_rep": rep(np.asarray(inputs["L0_attn_bout"])),
        "bv_rep": rep(np.asarray(inputs["L0_attn_bqkv"])[2048:]),
        "ln1_rep": rep(np.asarray(inputs["L0_ffn_ln1"])),
        "ln2_rep": rep(np.asarray(inputs["L0_ffn_ln2"])),
        "rw_rep": rep(np.asarray(inputs["L0_ffn_router_w"])[:, 0]),
    }
    m.update(_consts())
    return m


def kernel(**inputs):
    nc = build_nc(debug_stage=0)
    maps = [_stage_core(inputs, c >> 1) for c in range(8)]
    res = run_bass_kernel_spmd(nc, maps, core_ids=list(range(8)))
    out = np.stack([np.asarray(res.results[2 * b]["out"])[:T] for b in range(B)])
    return out
